# version 09
# baseline (speedup 1.0000x reference)
import contextlib
from collections import defaultdict

import numpy as np
import ml_dtypes

import concourse.bass as bass
import concourse.mybir as mybir
from concourse.bass_utils import run_bass_kernel_spmd

F32 = mybir.dt.float32
BF16 = mybir.dt.bfloat16
F16 = mybir.dt.float16
I8 = mybir.dt.int8
I32 = mybir.dt.int32
AF = mybir.ActivationFunctionType
ALU = mybir.AluOpType
AXX = mybir.AxisListType.X

HID = 4096
PROJ = 18560
INTER = 8192
H = 128
PD = 64
G = 8
CS = 256
ROWS = 1024
LIN = 1027
KT = 32
NCH = 4
EPS = 1e-5
FM_SPLITS = ((0, 342), (342, 342), (684, 343))
ENGS = ("sync", "scalar", "vector", "tensor", "gpsimd")


def build(debug=(), upto="c", no_states=False, ag_late=True):
    nc = bass.Bass("TRN2", target_bir_lowering=False)

    xs = nc.dram_tensor("xs", [LIN, HID], F32, kind="ExternalInput").ap()
    winfm = nc.dram_tensor("winfm", [81, 128, KT * 128], F32, kind="ExternalInput").ap()
    wingt = nc.dram_tensor("wingt", [64, 128, 8 * 512], F32, kind="ExternalInput").ap()
    woutt = nc.dram_tensor("woutt", [64, 128, 8 * 512], F32, kind="ExternalInput").ap()
    pkT = nc.dram_tensor("pkT", [HID // 2, HID], I8, kind="ExternalInput").ap()
    convw_in = nc.dram_tensor("convw", [128, 320], F32, kind="ExternalInput").ap()
    convb_in = nc.dram_tensor("convb", [128, 80], F32, kind="ExternalInput").ap()
    dtb_in = nc.dram_tensor("dtb", [H], F32, kind="ExternalInput").ap()
    alog_in = nc.dram_tensor("alog", [H], F32, kind="ExternalInput").ap()
    dvec_in = nc.dram_tensor("dvec", [1, H], F32, kind="ExternalInput").ap()
    nw_in = nc.dram_tensor("nw", [64, 128], F32, kind="ExternalInput").ap()
    ffnb_in = nc.dram_tensor("ffnb", [1, HID], F16, kind="ExternalInput").ap()
    fscale_in = nc.dram_tensor("fscale", [1, 1], F16, kind="ExternalInput").ap()
    identbf_in = nc.dram_tensor("identbf", [128, 128], BF16, kind="ExternalInput").ap()
    identf_in = nc.dram_tensor("identf", [128, 128], F32, kind="ExternalInput").ap()
    maskd_in = nc.dram_tensor("maskd", [128, 128], BF16, kind="ExternalInput").ap()
    fmask_in = nc.dram_tensor("fmask", [128, 6], F32, kind="ExternalInput").ap()

    out = nc.dram_tensor("out", [ROWS, HID], F32, kind="ExternalOutput").ap()

    def scratch(name, shape, dtype):
        kind = "ExternalOutput" if name in debug else "Internal"
        return nc.dram_tensor(name, shape, dtype, kind=kind).ap()

    hs_sp = scratch("hs_sp", [8, 128, 64, 128], BF16)
    gate_sp = scratch("gate_sp", [8, 128, INTER], BF16)
    st_sp = scratch("st_sp", [NCH * 128, INTER], BF16)
    bc_sp = scratch("bc_sp", [NCH, 128, 4096], BF16)
    g_sp = scratch("g_sp", [8, 128, 64, 128], BF16)
    x2_sp = scratch("x2_sp", [8, 128, HID], F32)
    cum_dram = scratch("cum_dram", [NCH, 128, 256], F32)
    ag_in = scratch("ag_in", [128, INTER + 128], BF16)
    ag_out = scratch("ag_out", [4 * 128, INTER + 128], BF16)
    dbgs = {}
    for name in debug:
        if name.startswith("dbg_"):
            dbgs[name] = scratch(name, [128, INTER], F32)

    ops = {e: [] for e in ENGS}
    cnt = defaultdict(int)

    def E(eng, fn):
        ops[eng].append(fn)

    def inc(sem_h, n=1):
        cnt[sem_h] += n
        return cnt[sem_h]

    def DR(eng):
        E(eng, lambda e: e.drain())

    def WO(eng, g):
        if g is not None:
            W(eng, *g)

    def CP(e, o, i):
        return e.copy(o, i) if hasattr(e, "copy") else e.tensor_copy(o, i)

    def W(eng, sem_h, val):
        assert val <= cnt[sem_h], f"wait on future value {val} > {cnt[sem_h]}"
        if val > 0:
            E(eng, lambda e, s=sem_h, v=val: e.wait_ge(s, v))

    es = contextlib.ExitStack()
    sem = {k: es.enter_context(nc.semaphore(f"s_{k}"))
           for k in ("ld", "w", "sp", "te", "ve", "se", "gp", "cc", "out", "dg")}
    MARK = {"tensor": "te", "vector": "ve", "scalar": "se", "gpsimd": "gp",
            "sync": "ld"}

    def dma(eng, out_ap, in_ap, s="ld", waits=()):
        v = inc(sem[s], 16)
        def f(e, o=out_ap, i=in_ap, s2=sem[s], ws=tuple(waits)):
            for (sh, vv) in ws:
                e.wait_ge(sh, vv)
            e.dma_start(out=o, in_=i).then_inc(s2, 16)
        E(eng, f)
        return v

    sb = {}

    def alloc(name, shape, dtype):
        t = es.enter_context(nc.sbuf_tensor("sb_" + name, shape, dtype))
        sb[name] = t
        return t

    ident_bf = alloc("ident_bf", [128, 128], BF16)
    ident_f = alloc("ident_f", [128, 128], F32)
    maskd = alloc("maskd", [128, 128], BF16)
    fmask = alloc("fmask", [128, 6], F32)
    convw_sb = alloc("convw_sb", [128, 320], F32)
    convb_sb = alloc("convb_sb", [128, 80], F32)
    cols = alloc("cols", [128, 8], F32)
    dbc = alloc("dbc", [128, 128], F32)
    nwcols = alloc("nwcols", [128, 64], F32)
    onesrow = alloc("onesrow", [1, 128], F32)
    ones256 = alloc("ones256", [128, 256], F32)
    drow = alloc("drow", [1, 128], F32)
    clrow = alloc("clrow", [1, 128], F32)
    nw64 = alloc("nw64", [64, 128], F32)
    dtdrm = alloc("dtdrm", [128, 1024], F32)
    ecirm = alloc("ecirm", [128, 1024], F32)
    dtrm = alloc("dtrm", [128, 1024], F32)
    cumrm = alloc("cumrm", [128, 1024], F32)
    cdec = alloc("cdec", [128, 512], F32)
    acc = alloc("acc", [128, INTER], BF16)
    pbc = alloc("pbc", [128, 128], F32)
    stats = alloc("stats", [128, 96], F32)

    xfm = alloc("xfm", [128, KT * LIN], BF16)
    agbuf = alloc("agbuf", [128, INTER], BF16)
    sqscrap = agbuf
    xrow0 = alloc("xrow0", [128, HID], F32)
    xrow = [xrow0, xrow0]
    xln = [alloc(f"xln{i}", [128, HID], BF16) for i in range(2)]
    wbuf = [alloc(f"wbuf{i}", [128, KT * 128], BF16) for i in range(2)]
    cbuf = alloc("cbuf", [128, LIN], BF16)
    cvacc = alloc("cvacc", [128, 1024], BF16)
    silout = alloc("silout", [128, 1024], BF16)
    brm = alloc("brm", [128, 8192], BF16)
    dtraw = alloc("dtraw", [128, 1024], F32)
    dtw = [alloc(f"dtw{i}", [128, 256], F32) for i in range(8)]
    hstr = alloc("hstr", [128, 1024], BF16)
    ststg = alloc("ststg", [128, 512], BF16)
    pstg = alloc("pstg", [128, 128], BF16)
    xwb = alloc("xwb", [128, 256], BF16)
    wgq = wbuf
    cbufs = [cbuf, xln[0][:, 0:1027]]
    cvaccs = [cvacc, xln[0][:, 1027:2051]]
    silouts = [silout, xln[0][:, 2051:3075]]
    gstage = [cbuf[:, 0:512], cbuf[:, 512:1024]]
    agp = alloc("agp", [128, 128], BF16)
    onesbf = alloc("onesbf", [1, 128], BF16)

    pband = [es.enter_context(nc.psum_tensor(f"pb{i}", [128, 512], F32))
             for i in range(8)]

    ring_guard = [None, None]
    ring_state = {"i": 0}

    def small_pe(pe_fn, ev_fn, evac_eng="vector", pre_waits=()):
        slot = ring_state["i"] % 2
        ring_state["i"] += 1
        for (sh, v) in pre_waits:
            W("tensor", sh, v)
        g = ring_guard[slot]
        if g is not None:
            W("tensor", g[0], g[1])
        def f(e, fn=pe_fn, s=slot):
            fn(e, s).then_inc(sem["te"], 1)
        E("tensor", f)
        tv = inc(sem["te"])
        W(evac_eng, sem["te"], tv)
        m = sem[MARK[evac_eng]]
        def f2(e, fn=ev_fn, s=slot, m=m):
            fn(e, s).then_inc(m, 1)
        E(evac_eng, f2)
        ev = inc(m)
        ring_guard[slot] = (m, ev)
        return (m, ev)

    dma("sync", ident_bf[:], identbf_in)
    dma("sync", ident_f[:], identf_in)
    dma("sync", maskd[:], maskd_in)
    dma("sync", fmask[:], fmask_in)
    dma("sync", convw_sb[:], convw_in)
    dma("sync", convb_sb[:], convb_in)
    dma("sync", cols[:, 0:1], dtb_in.rearrange("(p o) -> p o", o=1))
    dma("sync", cols[:, 2:3], alog_in.rearrange("(p o) -> p o", o=1))
    dma("sync", drow[:], dvec_in)
    v_consts = dma("sync", nw64[:], nw_in)

    W("vector", sem["ld"], v_consts)
    def f(e):
        e.memset(onesrow[:], 1.0)
        e.memset(onesbf[:], 1.0)
        e.memset(ones256[:], 1.0)
        e.memset(acc[:], 0.0)
        e.memset(pbc[:], 1.0).then_inc(sem["ve"], 1)
    E("vector", f)
    v_memset = inc(sem["ve"])

    W("scalar", sem["ld"], v_consts)
    E("scalar", lambda e: e.activation(out=cols[:, 1:2], in_=cols[:, 2:3],
                                       func=AF.Exp).then_inc(sem["se"], 1))
    s_A = inc(sem["se"])
    W("vector", sem["se"], s_A)
    E("vector", lambda e: e.tensor_scalar_mul(cols[:, 1:2], cols[:, 1:2], -1.0)
      .then_inc(sem["ve"], 1))
    v_A = inc(sem["ve"])

    W("tensor", sem["ld"], v_consts)
    W("tensor", sem["ve"], v_memset)
    def f(e):
        e.matmul(pband[6][:, 0:128], onesrow[:], drow[:], start=True, stop=True)
        e.transpose(pband[7][:, 0:64], in_=nw64[:], identity=ident_f[0:64, 0:64])\
            .then_inc(sem["te"], 1)
    E("tensor", f)
    t_const = inc(sem["te"])
    W("vector", sem["te"], t_const)
    def f(e):
        e.tensor_copy(dbc[:], pband[6][:, 0:128])
        e.tensor_copy(nwcols[:], pband[7][:, 0:64]).then_inc(sem["ve"], 1)
    E("vector", f)
    v_const2 = inc(sem["ve"])
    ring_guard[0] = (sem["ve"], v_const2)
    ring_guard[1] = (sem["ve"], v_const2)

    ln_done = {}
    tr_done = {}
    for rt in range(9):
        par = rt % 2
        sc = 8 * rt
        r0 = rt * 128
        nr = 128 if rt < 8 else 3
        dwaits = []
        if rt >= 1:
            dwaits.append((sem["ve"], ln_done[rt - 1]))
        if rt == 8:
            E("vector", lambda e, b=xrow[par]: e.memset(b[:], 1.0)
              .then_inc(sem["ve"], 1))
            dwaits = [(sem["ve"], inc(sem["ve"]))]
        vx = dma("sync", xrow[par][0:nr, :], xs[r0:r0 + nr, :], waits=dwaits)

        W("vector", sem["ld"], vx)
        E("vector", lambda e, b=xrow[par], sc=sc: e.reduce_sum(
            stats[:, sc + 0:sc + 1], b[:], axis=AXX).then_inc(sem["ve"], 1))
        v1 = inc(sem["ve"])
        DR("vector")
        W("scalar", sem["ld"], vx)
        E("scalar", lambda e, b=xrow[par], sc=sc: e.activation(
            out=sqscrap[:, 0:HID], in_=b[:], func=AF.Square,
            accum_out=stats[:, sc + 1:sc + 2]).then_inc(sem["se"], 1))
        s1 = inc(sem["se"])
        W("vector", sem["se"], s1)
        def f(e, sc=sc):
            e.tensor_scalar_mul(stats[:, sc + 2:sc + 3], stats[:, sc + 0:sc + 1],
                                1.0 / HID)
            e.tensor_scalar_mul(stats[:, sc + 3:sc + 4], stats[:, sc + 1:sc + 2],
                                1.0 / HID)
            e.drain()
            e.tensor_mul(stats[:, sc + 4:sc + 5], stats[:, sc + 2:sc + 3],
                         stats[:, sc + 2:sc + 3])
            e.drain()
            e.tensor_sub(stats[:, sc + 5:sc + 6], stats[:, sc + 3:sc + 4],
                         stats[:, sc + 4:sc + 5])
            e.drain()
            e.tensor_scalar_add(stats[:, sc + 5:sc + 6], stats[:, sc + 5:sc + 6],
                                EPS).then_inc(sem["ve"], 1)
        E("vector", f)
        v2 = inc(sem["ve"])
        W("scalar", sem["ve"], v2)
        def f(e, sc=sc):
            e.activation(out=stats[:, sc + 6:sc + 7], in_=stats[:, sc + 5:sc + 6],
                         func=AF.Ln)
            e.drain()
            e.activation(out=stats[:, sc + 7:sc + 8], in_=stats[:, sc + 6:sc + 7],
                         func=AF.Exp, scale=-0.5).then_inc(sem["se"], 1)
        E("scalar", f)
        s2 = inc(sem["se"])
        W("vector", sem["se"], s2)
        if rt >= 2:
            W("vector", sem["te"], tr_done[rt - 2])
        def f(e, sc=sc, par=par):
            e.tensor_mul(stats[:, sc + 6:sc + 7], stats[:, sc + 2:sc + 3],
                         stats[:, sc + 7:sc + 8])
            e.drain()
            e.tensor_scalar_mul(stats[:, sc + 6:sc + 7], stats[:, sc + 6:sc + 7],
                                -1.0)
            e.drain()
            e.tensor_scalar(xln[par][:], xrow[par][:], stats[:, sc + 7:sc + 8],
                            stats[:, sc + 6:sc + 7], ALU.mult, ALU.add)\
                .then_inc(sem["ve"], 1)
        E("vector", f)
        ln_done[rt] = inc(sem["ve"])

        for kt in range(KT):
            def mk_pe(kt=kt, par=par):
                def pe(e, slot):
                    return e.transpose(
                        pband[6 + slot][:, :].bitcast(BF16)[:, 0:128],
                        in_=xln[par][:, kt * 128:(kt + 1) * 128],
                        identity=ident_bf[:])
                return pe
            def mk_ev(kt=kt, rt=rt, nr=nr):
                def ev(e, slot):
                    return e.tensor_copy(
                        xfm[:, kt * LIN + rt * 128: kt * LIN + rt * 128 + nr],
                        pband[6 + slot][:, :].bitcast(BF16)[:, 0:nr])
                return ev
            small_pe(mk_pe(), mk_ev(),
                     pre_waits=[(sem["ve"], ln_done[rt])] if kt == 0 else ())
        tr_done[rt] = cnt[sem["te"]]
    v_xfm = cnt[sem["ve"]]

    STAGES = ["a1", "a2dt", "a2bc", "a2", "a3", "a4", "a5", "b", "c"]
    def stage_ge(s):
        return STAGES.index(upto) >= STAGES.index(s)
    nt_order = []
    if stage_ge("a2dt"):
        nt_order = [80]
    if stage_ge("a2bc"):
        nt_order += list(range(64, 80))
    if stage_ge("a2"):
        nt_order += list(range(64))
    te_after_nt = {}
    fm_evac_done = {}
    guard_cbuf = {}
    guard_sil = {0: [], 1: []}
    guard_hstr = []
    guard_xwb = []
    guard_ststg = []
    fold_v = 0

    def emit_post(nt, par, sd):
        nonlocal guard_hstr, guard_xwb, guard_ststg, fold_v
        silo = silouts[par]
        if 64 <= nt < 80:
            isB = nt < 72
            g2 = nt - 64 if isB else nt - 72
            for c in range(NCH):
                col = g2 * 256 if isB else 2048 + g2 * 256
                v = dma("sync", bc_sp[c, :, col:col + 256],
                        silo[:, c * 256:(c + 1) * 256], s="sp",
                        waits=[(sem["se"], sd)])
            guard_sil[par].append((sem["sp"], v))
            if isB:
                for c in range(NCH):
                    for jh in range(2):
                        def mk_pe(c=c, jh=jh, silo=silo):
                            def pe(e, slot):
                                return e.transpose(
                                    pband[6 + slot][:, :].bitcast(BF16)[:, 0:128],
                                    in_=silo[:, c * 256 + jh * 128:
                                             c * 256 + jh * 128 + 128],
                                    identity=ident_bf[:])
                            return pe
                        def mk_ev(c=c, jh=jh, g2=g2):
                            def ev(e, slot):
                                col = ((g2 * 4 + c) * 2 + jh) * 128
                                return CP(e,
                                    brm[:, col:col + 128],
                                    pband[6 + slot][:, :].bitcast(BF16)[:, 0:128])
                            return ev
                        small_pe(mk_pe(), mk_ev(), evac_eng="scalar",
                                 pre_waits=[(sem["se"], sd)])
                guard_sil[par].append((sem["te"], cnt[sem["te"]]))
            return
        h0 = 2 * nt
        g2 = h0 >> 4
        ev_blocks = {}
        first = True
        for c in range(NCH):
            for mh in range(2):
                def mk_pe(c=c, mh=mh, silo=silo):
                    def pe(e, slot):
                        return e.transpose(
                            pband[6 + slot][:, :].bitcast(BF16)[:, 0:128],
                            in_=silo[:, c * 256 + mh * 128:
                                     c * 256 + mh * 128 + 128],
                            identity=ident_bf[:])
                    return pe
                def mk_ev(c=c, mh=mh):
                    def ev(e, slot):
                        col = (2 * c + mh) * 128
                        return CP(e,
                            hstr[:, col:col + 128],
                            pband[6 + slot][:, :].bitcast(BF16)[:, 0:128])
                    return ev
                pw = [(sem["se"], sd)]
                if first:
                    for gg in guard_hstr:
                        W("vector", *gg)
                    guard_hstr = []
                    first = False
                ev_blocks[(c, mh)] = small_pe(mk_pe(), mk_ev(),
                                              evac_eng="scalar", pre_waits=pw)
        guard_sil[par].append((sem["te"], cnt[sem["te"]]))
        for c in range(NCH):
            for mh in range(2):
                rb = 2 * c + mh
                v = dma("sync", hs_sp[rb, :, nt, :],
                        hstr[:, rb * 128:rb * 128 + 128], s="sp",
                        waits=[ev_blocks[(c, mh)]])
        guard_hstr.append((sem["sp"], v))
        for c in range(NCH if not no_states else 0):
            WO("vector", ev_blocks[(c, 1)])
            for gg in guard_xwb:
                W("vector", *gg)
            guard_xwb = []
            def f(e, c=c, h0=h0):
                last = None
                for jh in range(2):
                    for hh in range(2):
                        h = h0 + hh
                        last = e.tensor_scalar(
                            xwb[:, (jh * 2 + hh) * 64:(jh * 2 + hh) * 64 + 64],
                            hstr[:, (2 * c + jh) * 128 + hh * 64:
                                 (2 * c + jh) * 128 + hh * 64 + 64],
                            dtdrm[:, (2 * c + jh) * 128 + h:
                                  (2 * c + jh) * 128 + h + 1],
                            None, ALU.mult)
                last.then_inc(sem["ve"], 1)
            E("vector", f)
            vxw = inc(sem["ve"])
            for hh in range(2):
                h = h0 + hh
                def mk_pe(c=c, hh=hh, g2=g2):
                    def pe(e, slot):
                        last = None
                        for jh in range(2):
                            col = ((g2 * 4 + c) * 2 + jh) * 128
                            last = e.matmul(
                                pband[6 + slot][:, 384:448],
                                brm[:, col:col + 128],
                                xwb[:, (jh * 2 + hh) * 64:
                                    (jh * 2 + hh) * 64 + 64],
                                start=(jh == 0), stop=(jh == 1),
                                skip_group_check=True)
                        return last
                    return pe
                def mk_ev(c=c, hh=hh):
                    def ev(e, slot):
                        return CP(e,
                            ststg[:, c * 128 + hh * 64:c * 128 + hh * 64 + 64],
                            pband[6 + slot][:, 384:448])
                    return ev
                pw = [(sem["ve"], vxw)]
                if guard_ststg:
                    for gg in guard_ststg:
                        W("vector", *gg)
                    guard_ststg = []
                evs = small_pe(mk_pe(), mk_ev(), evac_eng="scalar",
                               pre_waits=pw)
                WO("vector", evs)
                def f(e, c=c, hh=hh, h0=h0):
                    h = h0 + hh
                    e.scalar_tensor_tensor(
                        out=acc[:, h * 64:h * 64 + 64],
                        in0=acc[:, h * 64:h * 64 + 64],
                        scalar=cdec[:, c * 128 + h:c * 128 + h + 1],
                        in1=ststg[:, c * 128 + hh * 64:
                                  c * 128 + hh * 64 + 64],
                        op0=ALU.mult, op1=ALU.add).then_inc(sem["ve"], 1)
                E("vector", f)
                fold_v = inc(sem["ve"])
            guard_xwb.append((sem["te"], cnt[sem["te"]]))
        for c in range(NCH if not no_states else 0):
            v = dma("sync",
                    st_sp[c * 128:(c + 1) * 128, 128 * nt:128 * nt + 128],
                    ststg[:, c * 128:(c + 1) * 128], s="sp",
                    waits=[(sem["ve"], fold_v)])
            guard_ststg.append((sem["sp"], v))

    pending_post = None
    for it, nt in enumerate(nt_order):
        par = it % 2
        col0 = INTER + 128 * nt
        wws = []
        if it >= 2:
            wws.append((sem["te"], te_after_nt[nt_order[it - 2]]))
        wd = dma("gpsimd", wbuf[par][:], winfm[nt], s="w", waits=wws)
        W("tensor", sem["w"], wd)
        if it == 0:
            W("tensor", sem["ve"], v_xfm)
        if par in fm_evac_done:
            W("tensor", *fm_evac_done[par])
        pbase = 0 if par == 0 else 3
        def f(e, par=par, pbase=pbase):
            for kt in range(KT):
                lhsT = wbuf[par][:, kt * 128:(kt + 1) * 128]
                last = None
                for si, (off, n) in enumerate(FM_SPLITS):
                    last = e.matmul(pband[pbase + si][:, 0:n], lhsT,
                                    xfm[:, kt * LIN + off: kt * LIN + off + n],
                                    start=(kt == 0), stop=(kt == KT - 1),
                                    skip_group_check=True)
            last.then_inc(sem["te"], 1)
        E("tensor", f)
        te_after_nt[nt] = inc(sem["te"])

        W("scalar", sem["te"], te_after_nt[nt])
        if nt == 80:
            def f(e, pbase=pbase):
                e.copy(dtraw[:, 0:339], pband[pbase + 0][:, 3:342])
                e.copy(dtraw[:, 339:681], pband[pbase + 1][:, 0:342])
                e.copy(dtraw[:, 681:1024], pband[pbase + 2][:, 0:343])\
                    .then_inc(sem["se"], 1)
        else:
            if par in guard_cbuf:
                W("scalar", *guard_cbuf[par])
            def f(e, pbase=pbase, par=par):
                e.copy(cbufs[par][:, 0:342], pband[pbase + 0][:, 0:342])
                e.copy(cbufs[par][:, 342:684], pband[pbase + 1][:, 0:342])
                e.copy(cbufs[par][:, 684:1027], pband[pbase + 2][:, 0:343])\
                    .then_inc(sem["se"], 1)
        E("scalar", f)
        ed = inc(sem["se"])
        fm_evac_done[par] = (sem["se"], ed)

        if nt == 80:
            for c in range(NCH):
                t0, t1, t2, t3 = dtw[4 * (c % 2):4 * (c % 2) + 4]
                sc = 16 * (c % 2)
                W("scalar", sem["se"], ed)
                W("scalar", sem["ve"], v_A)
                if c >= 2:
                    W("scalar", sem["te"], dt_tr[c - 2])
                def f(e, c=c, t0=t0, t1=t1, t2=t2, t3=t3):
                    dcol = dtraw[:, c * 256:(c + 1) * 256]
                    e.activation(out=t0[:], in_=dcol, func=AF.Abs,
                                 bias=cols[:, 0:1])
                    e.drain()
                    e.activation(out=t1[:], in_=t0[:], func=AF.Exp, scale=-1.0)
                    e.drain()
                    e.activation(out=t2[:], in_=t1[:], func=AF.Ln, bias=1.0)
                    e.activation(out=t3[:], in_=dcol, func=AF.Relu,
                                 bias=cols[:, 0:1]).then_inc(sem["se"], 1)
                E("scalar", f)
                sd = inc(sem["se"])
                W("vector", sem["se"], sd)
                def f(e, t1=t1, t2=t2, t3=t3, sc=sc):
                    e.tensor_add(t3[:], t3[:], t2[:])
                    e.drain()
                    e.tensor_scalar(t1[:], t3[:], cols[:, 1:2], None,
                                    ALU.mult)
                    e.drain()
                    e.tensor_tensor_scan(out=t2[:], data0=ones256[:],
                                         data1=t1[:], initial=0.0,
                                         op0=ALU.mult, op1=ALU.add)
                    e.drain()
                    e.tensor_scalar_mul(stats[:, sc + 12:sc + 13],
                                        t2[:, 255:256], -1.0)\
                        .then_inc(sem["ve"], 1)
                E("vector", f)
                vd = inc(sem["ve"])
                dma("sync", cum_dram[c], t2[:], s="sp", waits=[(sem["ve"], vd)])
                W("scalar", sem["ve"], vd)
                def f(e, t0=t0, t1=t1, t2=t2):
                    e.activation(out=t1[:], in_=t2[:], func=AF.Exp, scale=-1.0,
                                 bias=t2[:, 255:256])
                    e.activation(out=t0[:], in_=t2[:], func=AF.Exp, scale=1.0)\
                        .then_inc(sem["se"], 1)
                E("scalar", f)
                sd2 = inc(sem["se"])
                W("vector", sem["se"], sd2)
                def f(e, t1=t1, t3=t3, c=c):
                    e.tensor_mul(t1[:], t3[:], t1[:]).then_inc(sem["ve"], 1)
                E("vector", f)
                DR("vector")
                vd2 = inc(sem["ve"])
                for (tsrc, dstap) in ((t1, dtdrm), (t0, ecirm), (t3, dtrm),
                                      (t2, cumrm)):
                    for jh in range(2):
                        def mk_pe(tsrc=tsrc, jh=jh):
                            def pe(e, slot):
                                return e.transpose(
                                    pband[6 + slot][:, 128:256],
                                    in_=tsrc[:, jh * 128:(jh + 1) * 128],
                                    identity=ident_f[:])
                            return pe
                        def mk_ev(dstap=dstap, c=c, jh=jh):
                            def ev(e, slot):
                                col = (2 * c + jh) * 128
                                return e.tensor_copy(
                                    dstap[:, col:col + 128],
                                    pband[6 + slot][:, 128:256])
                            return ev
                        small_pe(mk_pe(), mk_ev(),
                                 pre_waits=[(sem["ve"], vd2)])
                def pe2(e, slot, t2=t2):
                    return e.transpose(pband[6 + slot][0:1, 128:256],
                                       in_=t2[:, 255:256], identity=ident_f[:])
                def ev2(e, slot):
                    return e.tensor_copy(clrow[:], pband[6 + slot][0:1, 128:256])
                evn = small_pe(pe2, ev2, pre_waits=[(sem["ve"], vd)])
                def pe3(e, slot):
                    return e.matmul(pband[6 + slot][:, 256:384], onesrow[:],
                                    clrow[:], start=True, stop=True)
                def ev3(e, slot, c=c):
                    return e.activation(out=cdec[:, c * 128:(c + 1) * 128],
                                        in_=pband[6 + slot][:, 256:384],
                                        func=AF.Exp)
                small_pe(pe3, ev3, evac_eng="scalar", pre_waits=[evn])
                dt_tr = dt_tr if c > 0 else {}
                dt_tr[c] = cnt[sem["te"]]
        else:
            W("vector", sem["se"], ed)
            def f(e, t=nt, par=par):
                e.tensor_scalar(cvaccs[par][:], cbufs[par][:, 0:1024],
                                convw_sb[:, 4 * t:4 * t + 1],
                                convb_sb[:, t:t + 1], ALU.mult, ALU.add)
                for k in range(1, 4):
                    e.scalar_tensor_tensor(
                        out=cvaccs[par][:], in0=cbufs[par][:, k:k + 1024],
                        scalar=convw_sb[:, 4 * t + k:4 * t + k + 1],
                        in1=cvaccs[par][:], op0=ALU.mult, op1=ALU.add)
                e.tensor_copy(cvaccs[par][:, 0:1], cvaccs[par][:, 0:1])\
                    .then_inc(sem["ve"], 1)
            E("vector", f)
            cv = inc(sem["ve"])
            guard_cbuf[par] = (sem["ve"], cv)
            W("scalar", sem["ve"], cv)
            for gg in guard_sil[par]:
                W("scalar", *gg)
            guard_sil[par] = []
            E("scalar", lambda e, par=par: e.activation(
                out=silouts[par][:], in_=cvaccs[par][:], func=AF.Silu)
              .then_inc(sem["se"], 1))
            sd = inc(sem["se"])
            if pending_post is not None:
                emit_post(*pending_post)
            pending_post = (nt, par, sd)
    if pending_post is not None:
        emit_post(*pending_post)
        pending_post = None

    cc_done = 0
    d2 = 0
    if stage_ge("a3"):
        W("vector", sem["ve"], cnt[sem["ve"]])
        def f(e):
            last = None
            for c in range(NCH):
                e.drain()
                last = e.tensor_mul(pbc[:], pbc[:],
                                    cdec[:, c * 128:(c + 1) * 128])
            last.then_inc(sem["ve"], 1)
        E("vector", f)
        v_p = inc(sem["ve"])
        d1 = dma("sync", ag_in[:, 0:INTER], acc[:], s="sp",
                 waits=[(sem["ve"], cnt[sem["ve"]])])
        W("vector", sem["ve"], v_p)
        E("vector", lambda e: e.tensor_copy(pstg[:], pbc[:])
          .then_inc(sem["ve"], 1))
        v_pbf = inc(sem["ve"])
        d2 = dma("sync", ag_in[:, INTER:INTER + 128], pstg[:], s="sp",
                 waits=[(sem["ve"], v_pbf)])
        if "dbg_acc" in dbgs:
            dma("gpsimd", dbgs["dbg_acc"][:, 0:INTER], acc[:], s="dg",
                waits=[(sem["ve"], cnt[sem["ve"]])])
        def emit_collective():
            W("gpsimd", sem["sp"], cnt[sem["sp"]])
            W("gpsimd", sem["w"], cnt[sem["w"]])
            W("gpsimd", sem["dg"], cnt[sem["dg"]])
            W("gpsimd", sem["ld"], cnt[sem["ld"]])
            def f(e):
                e.collective_compute(
                    "AllGather", ALU.bypass,
                    replica_groups=[[0, 1, 2, 3], [4, 5, 6, 7]],
                    ins=[ag_in], outs=[ag_out]).then_inc(sem["cc"], 1)
            E("gpsimd", f)
            return inc(sem["cc"])
        if not ag_late:
            cc_done = emit_collective()

    W("tensor", sem["se"], cnt[sem["se"]])
    W("tensor", sem["ve"], cnt[sem["ve"]])
    a4_ve = cnt[sem["ve"]]
    a4_te = cnt[sem["te"]]
    a4_first_scalar = [True]
    gate_te = {}
    for gnb in range(16 if stage_ge("a4") else 0):
        for kq in range(4):
            gi = gnb * 4 + kq
            par = gi % 2
            wws = []
            if gi >= 2:
                wws.append((sem["te"], gate_te[gi - 2]))
            else:
                wws.append((sem["te"], a4_te))
            wd = dma("gpsimd", wgq[par][:], wingt[gnb * 4 + kq], s="w",
                     waits=wws)
            W("tensor", sem["w"], wd)
            if kq == 0 and gnb > 0:
                W("tensor", sem["sp"], gate_spill_done)
            def f(e, par=par, kq=kq):
                last = None
                for kt8 in range(8):
                    kt = kq * 8 + kt8
                    for mt in range(8):
                        last = e.matmul(
                            pband[mt][:, 0:512],
                            xfm[:, kt * LIN + 3 + 128 * mt:
                                kt * LIN + 3 + 128 * mt + 128],
                            wgq[par][:, kt8 * 512:kt8 * 512 + 512],
                            start=(kt == 0), stop=(kt == KT - 1),
                            skip_group_check=True)
                last.then_inc(sem["te"], 1)
            E("tensor", f)
            gate_te[gi] = inc(sem["te"])
        ge = gate_te[gnb * 4 + 3]
        for mt in range(8):
            gpar = mt % 2
            W("scalar", sem["te"], ge)
            if a4_first_scalar[0]:
                W("scalar", sem["ve"], a4_ve)
                a4_first_scalar[0] = False
            if gnb > 0 or mt >= 2:
                W("scalar", sem["sp"], gstage_guard[gpar])
            E("scalar", lambda e, mt=mt, gpar=gpar: e.activation(
                out=gstage[gpar], in_=pband[mt][:, 0:512], func=AF.Silu)
              .then_inc(sem["se"], 1))
            sv = inc(sem["se"])
            v = dma("sync", gate_sp[mt, :, 512 * gnb:512 * gnb + 512],
                    gstage[gpar], s="sp", waits=[(sem["se"], sv)])
            if gnb == 0 and mt == 0:
                gstage_guard = {}
            gstage_guard[gpar] = v
        gate_spill_done = cnt[sem["sp"]]

    if stage_ge("a3") and ag_late:
        cc_done = emit_collective()
    srun = acc
    if stage_ge("a5"):
        W("vector", sem["sp"], d2)
        W("vector", sem["dg"], cnt[sem["dg"]])
        E("vector", lambda e: e.memset(srun[:], 0.0).then_inc(sem["ve"], 1))
    v_sz = inc(sem["ve"]) if stage_ge("a5") else 0
    for d in range(3 if stage_ge("a5") else 0):
        da = dma("sync", agbuf[:], ag_out[d * 128:(d + 1) * 128, 0:INTER],
                 s="ld", waits=[(sem["cc"], cc_done)] if d == 0 else
                 [(sem["ve"], v_sz)])
        dp = dma("sync", agp[:], ag_out[d * 128:(d + 1) * 128, INTER:INTER + 128],
                 s="ld")
        W("vector", sem["ld"], dp)
        W("vector", sem["ve"], v_sz)
        def f(e, d=d):
            e.tensor_scalar(pbc[:], agp[:], fmask[:, d:d + 1],
                            fmask[:, 3 + d:3 + d + 1], ALU.mult, ALU.add)
            e.tensor_scalar(agbuf[:], agbuf[:], fmask[:, d:d + 1], None,
                            ALU.mult)
            e.drain()
            last = None
            for h in range(H):
                last = e.scalar_tensor_tensor(
                    out=srun[:, h * 64:h * 64 + 64],
                    in0=srun[:, h * 64:h * 64 + 64],
                    scalar=pbc[:, h:h + 1],
                    in1=agbuf[:, h * 64:h * 64 + 64],
                    op0=ALU.mult, op1=ALU.add)
            last.then_inc(sem["ve"], 1)
        E("vector", f)
        v_sz = inc(sem["ve"])


    if stage_ge("b"):
        hsc = [xfm[:, 0:8192], xfm[:, 8192:16384]]
        yfull = [xfm[:, 16384:24576], xfm[:, 24576:32768]]
        stc = brm[:, 0:8192]
        gate_mh1 = brm[:, 0:8192]
        gate_mh0 = agbuf[:, 0:8192]
        bcc = wbuf[0][:, 0:4096]
        smb = wbuf[1][:, 0:3072]
        cumb = [dtraw[:, i * 256:(i + 1) * 256] for i in range(4)]
        argb = [cbuf[:, 0:384], cbuf[:, 384:768],
                cvacc[:, 0:384], cvacc[:, 384:768]]
        Lb = [silout[:, 0:384], silout[:, 384:768],
              hstr[:, 0:384], hstr[:, 384:768]]
        swb = [dtw[1][:].bitcast(BF16)[:, 0:384],
               dtw[3][:].bitcast(BF16)[:, 0:384]]
        diagb = [ststg[:, i * 128:(i + 1) * 128] for i in range(4)]
        ibb = dtdrm[:, :].bitcast(BF16)
        ib = [[ibb[:, 0:512], ibb[:, 512:1024]],
              [ibb[:, 1024:1536], ibb[:, 1536:2048]]]
        gstag = [dtw[0][:].bitcast(BF16), dtw[2][:].bitcast(BF16)]
        ecix = dtw[5][:].bitcast(BF16)

        def tcol(ap, col):
            return ap[:, col:col + 1]

        def tsl(h):
            return xln[h // 64][:, (h % 64) * 64:(h % 64) * 64 + 64]

        vnow = lambda: (sem["ve"], cnt[sem["ve"]])
        gB = {
            "hs_free": None, "yfull_free": None, "stc_free": None,
            "bcc_free": None, "smb_free": None,
            "tst_free": (sem["te"], cnt[sem["te"]]),
            "gate0_free": None, "gate1_free": None,
            "banks01_free": None, "banks67_free": vnow(),
            "gstag_free": [None, None],
        }
        first_b_dma = [True]

        def bdma(out_ap, in_ap, waits=()):
            w = [x for x in waits if x is not None]
            if first_b_dma[0]:
                w.append((sem["cc"], cc_done))
                first_b_dma[0] = False
            return dma("sync", out_ap, in_ap, s="ld", waits=w)

        sw_te = {}
        cumb_done = {}
        arg_done = {}
        L_done = {}
        yevac = {}
        yass = {}
        for c in range(NCH):
            d_hs0 = bdma(hsc[0], hs_sp[2 * c], waits=[gB["hs_free"]])
            d_hs1 = bdma(hsc[1], hs_sp[2 * c + 1])
            d_st = bdma(stc, st_sp[c * 128:(c + 1) * 128, :],
                        waits=[gB["stc_free"]])
            d_bc = bdma(bcc, bc_sp[c], waits=[gB["bcc_free"]])
            d_g0 = bdma(gate_mh0, gate_sp[2 * c], waits=[gB["gate0_free"]])

            sevac = {}
            for g in range(G):
                if g == 0:
                    WO("tensor", gB["banks01_free"])
                    W("tensor", sem["ld"], d_bc)
                else:
                    W("tensor", sem["se"], sevac[g - 1])
                def f(e, g=g):
                    bcol = g * 256
                    e.matmul(pband[0][:, 0:256], bcc[:, bcol:bcol + 128],
                             bcc[:, 2048 + bcol:2048 + bcol + 256],
                             start=True, stop=True, skip_group_check=True)
                    e.matmul(pband[1][:, 0:128], bcc[:, bcol + 128:bcol + 256],
                             bcc[:, 2048 + bcol + 128:2048 + bcol + 256],
                             start=True, stop=True, skip_group_check=True)\
                        .then_inc(sem["te"], 1)
                E("tensor", f)
                t_g = inc(sem["te"])
                W("scalar", sem["te"], t_g)
                if g == 0:
                    WO("scalar", gB["smb_free"])
                def f(e, g=g):
                    e.copy(smb[:, g * 384:g * 384 + 256], pband[0][:, 0:256])
                    e.copy(smb[:, g * 384 + 256:g * 384 + 384],
                           pband[1][:, 0:128]).then_inc(sem["se"], 1)
                E("scalar", f)
                sevac[g] = inc(sem["se"])
                W("vector", sem["se"], sevac[g])
                def f(e, g=g):
                    e.tensor_tensor(smb[:, g * 384:g * 384 + 128],
                                    smb[:, g * 384:g * 384 + 128],
                                    maskd[:], ALU.mult)
                    e.tensor_tensor(smb[:, g * 384 + 256:g * 384 + 384],
                                    smb[:, g * 384 + 256:g * 384 + 384],
                                    maskd[:], ALU.mult).then_inc(sem["ve"], 1)
                E("vector", f)
                inc(sem["ve"])
            gp_masks = cnt[sem["ve"]]
            ldtc = dtw[4]
            W("scalar", sem["ve"], gp_masks)
            E("scalar", lambda e, c=c: e.activation(
                out=ldtc[:], in_=dtrm[:, 2 * c * 128:(2 * c + 2) * 128],
                func=AF.Ln).then_inc(sem["se"], 1))
            s_ldt = inc(sem["se"])

            WO("vector", gB["tst_free"])
            def f(e, c=c):
                last = None
                for h in range(H):
                    last = e.tensor_scalar(tsl(h), srun[:, h * 64:h * 64 + 64],
                                           cdec[:, c * 128 + h:c * 128 + h + 1],
                                           None, ALU.mult)
                last.then_inc(sem["ve"], 1)
            E("vector", f)
            v_t = inc(sem["ve"])

            for h in range(H):
                idx = c * H + h
                par = h % 4
                spar2 = h % 2
                g = h >> 4
                hg = h >> 3
                hgidx = c * 16 + hg
                ypar = hg % 2
                if h % 2 == 0:
                    ws = []
                    if idx >= 3:
                        ws.append((sem["ve"], arg_done[idx - 3]))
                    cum_src = cum_dram[c, h:h + 2, :].unsqueeze(0)\
                        .broadcast_to((128, 2, 256))
                    cd = dma("sync",
                             dtraw[:, par * 256:par * 256 + 512].rearrange(
                                 "p (two i) -> p two i", two=2),
                             cum_src, s="ld", waits=ws)
                    cumb_done[h] = cd
                    cumb_done[h + 1] = cd
                W("vector", sem["ld"], cumb_done[h])
                if idx == 0:
                    W("vector", sem["ve"], gp_masks)
                if idx >= 4:
                    W("vector", sem["se"], L_done[idx - 4])
                if idx >= 4:
                    W("vector", sem["te"], sw_te[idx - 4])
                def f(e, c=c, h=h, par=par):
                    e.tensor_scalar(argb[par][:, 0:256], cumb[par][:, :],
                                    tcol(cumrm, (2 * c) * 128 + h), 0.0,
                                    ALU.subtract, ALU.min)
                    e.tensor_scalar(argb[par][:, 256:384],
                                    cumb[par][:, 128:256],
                                    tcol(cumrm, (2 * c + 1) * 128 + h), 0.0,
                                    ALU.subtract, ALU.min)
                    e.tensor_scalar(diagb[par], ident_bf[:], tcol(dbc, h),
                                    None, ALU.mult).then_inc(sem["ve"], 1)
                E("vector", f)
                arg_done[idx] = inc(sem["ve"])
                W("scalar", sem["ve"], arg_done[idx])
                if idx >= 4:
                    W("scalar", sem["te"], sw_te[idx - 4])
                def f(e, c=c, h=h, par=par):
                    e.activation(out=Lb[par][:, 0:256],
                                 in_=argb[par][:, 0:256], func=AF.Exp,
                                 bias=tcol(ldtc, h))
                    e.activation(out=Lb[par][:, 256:384],
                                 in_=argb[par][:, 256:384], func=AF.Exp,
                                 bias=tcol(ldtc, 128 + h)).then_inc(sem["se"], 1)
                E("scalar", f)
                L_done[idx] = inc(sem["se"])
                W("vector", sem["se"], L_done[idx])
                if idx >= 2:
                    W("vector", sem["te"], sw_te[idx - 2])
                def f(e, par=par, g=g, spar2=spar2):
                    e.tensor_tensor(swb[spar2][:, 0:384], Lb[par][:, 0:384],
                                    smb[:, g * 384:g * 384 + 384], ALU.mult)\
                        .then_inc(sem["ve"], 1)
                E("vector", f)
                sw_v = inc(sem["ve"])
                ybank = [pband[2 + 2 * ypar], pband[3 + 2 * ypar]]
                ibank = [pband[6], pband[7]] if ypar == 0 else \
                        [pband[0], pband[1]]
                sl = slice(64 * (h % 8), 64 * (h % 8) + 64)
                W("tensor", sem["ve"], sw_v)
                if h == 0:
                    W("tensor", sem["ve"], v_t)
                    W("tensor", sem["ld"], d_hs1)
                if h % 8 == 0:
                    if hgidx >= 2:
                        W("tensor", sem["se"], yevac[hgidx - 2])
                    if ypar == 0:
                        WO("tensor", gB["banks67_free"])
                def f(e, c=c, h=h, par=par, g=g, sl=sl, ybank=ybank,
                      ibank=ibank, spar2=spar2):
                    e.matmul(ybank[0][:, sl], swb[spar2][:, 0:128],
                             hsc[0][:, h * 64:h * 64 + 64],
                             start=True, stop=False, skip_group_check=True)
                    e.matmul(ybank[0][:, sl], diagb[par],
                             hsc[0][:, h * 64:h * 64 + 64],
                             start=False, stop=True, skip_group_check=True)
                    e.matmul(ybank[1][:, sl], swb[spar2][:, 128:256],
                             hsc[0][:, h * 64:h * 64 + 64],
                             start=True, stop=False, skip_group_check=True)
                    e.matmul(ybank[1][:, sl], swb[spar2][:, 256:384],
                             hsc[1][:, h * 64:h * 64 + 64],
                             start=False, stop=False, skip_group_check=True)
                    e.matmul(ybank[1][:, sl], diagb[par],
                             hsc[1][:, h * 64:h * 64 + 64],
                             start=False, stop=True, skip_group_check=True)
                    e.matmul(ibank[0][:, sl],
                             bcc[:, 2048 + g * 256:2048 + g * 256 + 128],
                             tsl(h), start=True, stop=True,
                             skip_group_check=True)
                    e.matmul(ibank[1][:, sl],
                             bcc[:, 2048 + g * 256 + 128:2048 + g * 256 + 256],
                             tsl(h), start=True, stop=True,
                             skip_group_check=True).then_inc(sem["te"], 1)
                E("tensor", f)
                sw_te[idx] = inc(sem["te"])
                if h % 8 == 7:
                    W("scalar", sem["te"], sw_te[idx])
                    if hg == 0:
                        WO("scalar", gB["yfull_free"])
                    if hgidx >= 2:
                        W("scalar", sem["ve"], yass[hgidx - 2])
                    def f(e, hg=hg, ybank=ybank, ibank=ibank, ypar=ypar):
                        e.copy(yfull[0][:, hg * 512:hg * 512 + 512],
                               ybank[0][:, :])
                        e.copy(yfull[1][:, hg * 512:hg * 512 + 512],
                               ybank[1][:, :])
                        e.copy(ib[0][ypar], ibank[0][:, :])
                        e.copy(ib[1][ypar], ibank[1][:, :])\
                            .then_inc(sem["se"], 1)
                    E("scalar", f)
                    yevac[hgidx] = inc(sem["se"])
                    W("vector", sem["se"], yevac[hgidx])
                    def f(e, c=c, hg=hg, ypar=ypar):
                        sl2 = slice(hg * 512, hg * 512 + 512)
                        for ih in range(2):
                            eci_src = ecirm[:, (2 * c + ih) * 128 + hg * 8:
                                            (2 * c + ih) * 128 + hg * 8 + 8]
                            e.tensor_copy(
                                ecix[:].rearrange("p (h q) -> p h q", h=8),
                                eci_src.unsqueeze(2).broadcast_to((128, 8, 64)))
                            e.drain()
                            e.tensor_mul(ib[ih][ypar], ib[ih][ypar], ecix[:])
                            e.drain()
                            e.tensor_add(yfull[ih][:, sl2], yfull[ih][:, sl2],
                                         ib[ih][ypar])
                        e.drain()
                        e.tensor_copy(ecix[:, 0:1], ecix[:, 0:1])\
                            .then_inc(sem["ve"], 1)
                    E("vector", f)
                    yass[hgidx] = inc(sem["ve"])

            W("vector", sem["te"], cnt[sem["te"]])
            def f(e):
                e.tensor_add(srun[:, 0:4096], xln[0][:], stc[:, 0:4096])
                e.tensor_add(srun[:, 4096:8192], xln[1][:],
                             stc[:, 4096:8192]).then_inc(sem["ve"], 1)
            E("vector", f)
            v_srun = inc(sem["ve"])
            d_g1 = dma("sync", gate_mh1, gate_sp[2 * c + 1], s="ld",
                       waits=[x for x in [(sem["ve"], v_srun),
                                          gB["gate1_free"]] if x])

            sgs = stats[:, 80:88]
            rst8 = stats[:, 88:96]
            gate_dma = {0: d_g0, 1: d_g1}
            gatebuf = {0: gate_mh0, 1: gate_mh1}
            gsp_guard = list(gB["gstag_free"])
            for mh in range(2):
                W("vector", sem["ve"], yass[c * 16 + 15])
                W("vector", sem["ld"], gate_dma[mh])
                def f(e, mh=mh):
                    e.tensor_mul(yfull[mh][:, :], yfull[mh][:, :],
                                 gatebuf[mh][:, :]).then_inc(sem["ve"], 1)
                E("vector", f)
                v_g1 = inc(sem["ve"])
                W("scalar", sem["ve"], v_g1)
                def f(e, mh=mh):
                    last = None
                    for g in range(G):
                        last = e.activation(
                            out=sqscrap[:, 0:1024],
                            in_=yfull[mh][:, g * 1024:(g + 1) * 1024],
                            func=AF.Square, accum_out=sgs[:, g:g + 1])
                        e.drain()
                    last.then_inc(sem["se"], 1)
                E("scalar", f)
                s_sq = inc(sem["se"])
                W("vector", sem["se"], s_sq)
                E("vector", lambda e: e.tensor_scalar(
                    sgs[:], sgs[:], 1.0 / 1024.0, EPS, ALU.mult, ALU.add)
                  .then_inc(sem["ve"], 1))
                v_v8 = inc(sem["ve"])
                W("scalar", sem["ve"], v_v8)
                def f(e):
                    e.activation(out=rst8[:], in_=sgs[:], func=AF.Ln)
                    e.drain()
                    e.activation(out=rst8[:], in_=rst8[:], func=AF.Exp,
                                 scale=-0.5).then_inc(sem["se"], 1)
                E("scalar", f)
                s_r8 = inc(sem["se"])
                W("vector", sem["se"], s_r8)
                def f(e, mh=mh):
                    last = None
                    for g in range(G):
                        last = e.tensor_scalar(
                            yfull[mh][:, g * 1024:(g + 1) * 1024],
                            yfull[mh][:, g * 1024:(g + 1) * 1024],
                            rst8[:, g:g + 1], None, ALU.mult)
                    last.then_inc(sem["ve"], 1)
                E("vector", f)
                v_gn = inc(sem["ve"])
                mb = 2 * c + mh
                for kb in range(16):
                    spar = kb % 2
                    if gsp_guard[spar] is not None:
                        WO("vector", gsp_guard[spar])
                    for k4 in range(4):
                        kt = kb * 4 + k4
                        def mk_pe(kt=kt, mh=mh):
                            def pe(e, slot):
                                return e.transpose(
                                    pband[6 + slot][:, :]
                                    .bitcast(BF16)[:, 0:128],
                                    in_=yfull[mh][:, kt * 128:kt * 128 + 128],
                                    identity=ident_bf[:])
                            return pe
                        def mk_ev(kt=kt, k4=k4, spar=spar):
                            def ev(e, slot):
                                return e.tensor_scalar(
                                    gstag[spar][:, k4 * 128:k4 * 128 + 128],
                                    pband[6 + slot][:, :]
                                    .bitcast(BF16)[:, 0:128],
                                    nwcols[:, kt:kt + 1], None, ALU.mult)
                            return ev
                        pw = [(sem["ve"], v_gn)] if (kt == 0 and mh == 0) \
                            else ()
                        if kt == 0:
                            pw = [(sem["ve"], v_gn)]
                        small_pe(mk_pe(), mk_ev(), pre_waits=pw)
                    v_last = cnt[sem["ve"]]
                    sp_v = dma("sync", g_sp[mb, :, kb * 4:kb * 4 + 4, :],
                               gstag[spar][:, :].rearrange(
                                   "p (k m) -> p k m", k=4),
                               s="sp", waits=[(sem["ve"], v_last)])
                    gsp_guard[spar] = (sem["sp"], sp_v)

            gB["hs_free"] = (sem["te"], cnt[sem["te"]])
            gB["yfull_free"] = (sem["te"], cnt[sem["te"]])
            gB["stc_free"] = (sem["ve"], cnt[sem["ve"]])
            gB["bcc_free"] = (sem["te"], cnt[sem["te"]])
            gB["smb_free"] = (sem["ve"], cnt[sem["ve"]])
            gB["tst_free"] = (sem["ve"], v_srun)
            gB["gate0_free"] = (sem["ve"], cnt[sem["ve"]])
            gB["gate1_free"] = (sem["ve"], cnt[sem["ve"]])
            gB["banks01_free"] = (sem["se"], cnt[sem["se"]])
            gB["banks67_free"] = (sem["ve"], cnt[sem["ve"]])
            gB["gstag_free"] = gsp_guard
        b_end_guards = dict(gB)
        ring_guard[0] = (sem["ve"], cnt[sem["ve"]])
        ring_guard[1] = (sem["ve"], cnt[sem["ve"]])

    if stage_ge("c"):
        gq = [xfm[:, 0:8192], xfm[:, 8192:16384]]
        wq = [xfm[:, 16384:20480], xfm[:, 20480:24576]]
        ffn_bf = xfm[:, 0:32768]
        aslot = [agbuf[:, 0:4096], agbuf[:, 4096:8192],
                 brm[:, 0:4096], brm[:, 4096:8192],
                 wbuf[0][:, 0:4096], wbuf[1][:, 0:4096],
                 xln[0][:, 0:4096], xln[1][:, 0:4096]]
        biasrow = acc[0:1, 0:4096]
        cscrap = acc[:, 4096:8192]
        pk2 = dtraw[:, :].bitcast(I8)
        pkb = [pk2[:, 0:512], pk2[:, 512:1024]]
        lo4b = pk2[:, 1024:1536]
        p_bf = cbuf[:, 0:512]
        lo4_bf = cbuf[:, 512:1024]
        wfs = [[cvacc[:, 0:512], cvacc[:, 512:1024]],
               [silout[:, 0:512], silout[:, 512:1024]]]
        x2ln_tmp = hstr[:, 0:512]

        bstart = b_end_guards if stage_ge("b") else {}

        d_bias = dma("gpsimd", biasrow[:], ffnb_in, s="dg")
        d_fs = dma("gpsimd", stats[0:1, 61:62], fscale_in, s="dg")
        W("tensor", sem["dg"], d_fs)
        WO("tensor", ring_guard[0]) if stage_ge("b") else None
        def f(e):
            e.matmul(pband[6][:, 0:1], onesrow[:], stats[0:1, 61:62],
                     start=True, stop=True, skip_group_check=True)\
                .then_inc(sem["te"], 1)
        E("tensor", f)
        t_fs = inc(sem["te"])
        W("vector", sem["te"], t_fs)
        E("vector", lambda e: e.tensor_copy(stats[:, 62:63], pband[6][:, 0:1])
          .then_inc(sem["ve"], 1))
        v_fs = inc(sem["ve"])
        scol = stats[:, 62:63]

        c1evac = {}
        gq_te = {}
        wq_te = {}
        for nblk in range(8):
            for kq in range(8):
                par = kq % 2
                key = nblk * 8 + kq
                ws = []
                if key >= 2:
                    ws.append((sem["te"], wq_te[key - 2]))
                elif stage_ge("b"):
                    ws.append(bstart["banks67_free"])
                d_w = dma("gpsimd", wq[par][:], woutt[nblk * 8 + kq], s="w",
                          waits=ws)
                gws = []
                if key >= 2:
                    gws.append((sem["te"], gq_te[key - 2]))
                for m in range(8):
                    d_g = dma("sync", gq[par][:, m * 1024:(m + 1) * 1024],
                              g_sp[m, :, kq * 8:kq * 8 + 8, :],
                              s="ld", waits=gws if m == 0 else ())
                W("tensor", sem["w"], d_w)
                W("tensor", sem["ld"], d_g)
                for m in range(8):
                    if kq == 0:
                        if nblk > 0:
                            W("tensor", sem["se"], c1evac[(nblk - 1, m)])
                        elif m == 0 and stage_ge("b"):
                            pass
                    def f(e, m=m, par=par, kq=kq):
                        last = None
                        for kt8 in range(8):
                            last = e.matmul(
                                pband[m][:, 0:512],
                                gq[par][:, m * 1024 + kt8 * 128:
                                        m * 1024 + kt8 * 128 + 128],
                                wq[par][:, kt8 * 512:kt8 * 512 + 512],
                                start=(kq == 0 and kt8 == 0),
                                stop=(kq == 7 and kt8 == 7),
                                skip_group_check=True)
                        last.then_inc(sem["te"], 1)
                    E("tensor", f)
                    t_m = inc(sem["te"])
                    if kq == 7:
                        W("scalar", sem["te"], t_m)
                        if nblk == 0 and stage_ge("b"):
                            WO("scalar", bstart["yfull_free"]) if m == 0 \
                                else None
                        def f(e, m=m, nblk=nblk):
                            e.copy(aslot[m][:, nblk * 512:nblk * 512 + 512],
                                   pband[m][:, 0:512]).then_inc(sem["se"], 1)
                        E("scalar", f)
                        c1evac[(nblk, m)] = inc(sem["se"])
                wq_te[key] = cnt[sem["te"]]
                gq_te[key] = cnt[sem["te"]]

        def emit_ln(src_ap, sc, prod_sem, prod_val):
            W("vector", prod_sem, prod_val)
            E("vector", lambda e, s=src_ap, sc=sc: e.reduce_sum(
                stats[:, sc:sc + 1], s, axis=AXX).then_inc(sem["ve"], 1))
            v1 = inc(sem["ve"])
            DR("vector")
            W("scalar", prod_sem, prod_val)
            E("scalar", lambda e, s=src_ap, sc=sc: e.activation(
                out=cscrap[:], in_=s, func=AF.Square,
                accum_out=stats[:, sc + 1:sc + 2]).then_inc(sem["se"], 1))
            s1 = inc(sem["se"])
            W("vector", sem["se"], s1)
            def f(e, sc=sc):
                e.tensor_scalar_mul(stats[:, sc + 2:sc + 3],
                                    stats[:, sc:sc + 1], 1.0 / HID)
                e.tensor_scalar_mul(stats[:, sc + 3:sc + 4],
                                    stats[:, sc + 1:sc + 2], 1.0 / HID)
                e.drain()
                e.tensor_mul(stats[:, sc + 4:sc + 5], stats[:, sc + 2:sc + 3],
                             stats[:, sc + 2:sc + 3])
                e.drain()
                e.tensor_sub(stats[:, sc + 5:sc + 6], stats[:, sc + 3:sc + 4],
                             stats[:, sc + 4:sc + 5])
                e.drain()
                e.tensor_scalar_add(stats[:, sc + 5:sc + 6],
                                    stats[:, sc + 5:sc + 6], EPS)\
                    .then_inc(sem["ve"], 1)
            E("vector", f)
            v2 = inc(sem["ve"])
            W("scalar", sem["ve"], v2)
            def f(e, sc=sc):
                e.activation(out=stats[:, sc + 6:sc + 7],
                             in_=stats[:, sc + 5:sc + 6], func=AF.Ln)
                e.drain()
                e.activation(out=stats[:, sc + 7:sc + 8],
                             in_=stats[:, sc + 6:sc + 7], func=AF.Exp,
                             scale=-0.5).then_inc(sem["se"], 1)
            E("scalar", f)
            s2 = inc(sem["se"])
            W("vector", sem["se"], s2)
            def f(e, sc=sc):
                e.tensor_mul(stats[:, sc + 6:sc + 7], stats[:, sc + 2:sc + 3],
                             stats[:, sc + 7:sc + 8])
                e.drain()
                e.tensor_scalar_mul(stats[:, sc + 6:sc + 7],
                                    stats[:, sc + 6:sc + 7], -1.0)\
                    .then_inc(sem["ve"], 1)
            E("vector", f)
            return inc(sem["ve"])

        c2_done = {}
        xrow_free = None
        for m in range(8):
            ln1 = emit_ln(aslot[m][:, :], 0, sem["se"], c1evac[(7, m)])
            dx = dma("sync", xrow[0][:], xs[3 + m * 128:3 + m * 128 + 128, :],
                     s="ld", waits=[x for x in [xrow_free] if x])
            W("vector", sem["ve"], ln1)
            W("vector", sem["ld"], dx)
            def f(e, m=m):
                e.scalar_tensor_tensor(
                    out=xrow[0][:], in0=aslot[m][:, :],
                    scalar=stats[:, 7:8], in1=xrow[0][:],
                    op0=ALU.mult, op1=ALU.add)
                e.drain()
                e.tensor_scalar(xrow[0][:], xrow[0][:], stats[:, 6:7], None,
                                ALU.add).then_inc(sem["ve"], 1)
            E("vector", f)
            v_x2 = inc(sem["ve"])
            d_sp = dma("sync", x2_sp[m], xrow[0][:], s="sp",
                       waits=[(sem["ve"], v_x2)])
            ln2 = emit_ln(xrow[0][:], 8, sem["ve"], v_x2)
            W("vector", sem["ve"], ln2)
            def f(e, m=m):
                e.tensor_scalar(aslot[m][:, :], xrow[0][:], stats[:, 15:16],
                                stats[:, 14:15], ALU.mult, ALU.add)\
                    .then_inc(sem["ve"], 1)
            E("vector", f)
            v_x2ln = inc(sem["ve"])
            xrow_free = (sem["ve"], v_x2ln)
            prev_ev = None
            for ks in range(16):
                W("tensor", sem["ve"], v_x2ln if ks == 0 else 0)
                if prev_ev is not None:
                    W("tensor", *prev_ev)
                def f(e, m=m, ks=ks):
                    e.transpose(pband[6][:, :].bitcast(BF16)[:, 0:128],
                                in_=aslot[m][:, ks * 256:ks * 256 + 256:2],
                                identity=ident_bf[:])
                    e.transpose(pband[7][:, :].bitcast(BF16)[:, 0:128],
                                in_=aslot[m][:, ks * 256 + 1:ks * 256 + 256:2],
                                identity=ident_bf[:]).then_inc(sem["te"], 1)
                E("tensor", f)
                t_tr = inc(sem["te"])
                W("vector", sem["te"], t_tr)
                W("vector", sem["ve"], v_fs)
                def f(e, m=m, ks=ks):
                    e.tensor_scalar(
                        aslot[m][:, ks * 256:ks * 256 + 128],
                        pband[6][:, :].bitcast(BF16)[:, 0:128],
                        scol, None, ALU.mult)
                    e.tensor_scalar(
                        aslot[m][:, ks * 256 + 128:ks * 256 + 256],
                        pband[7][:, :].bitcast(BF16)[:, 0:128],
                        scol, None, ALU.mult).then_inc(sem["ve"], 1)
                E("vector", f)
                prev_ev = (sem["ve"], inc(sem["ve"]))
            c2_done[m] = prev_ev[1]

        c3evac = {}
        unp_v = {}
        for nblk in range(8):
            for ks in range(16):
                par = ks % 2
                key = nblk * 16 + ks
                ws = []
                if key >= 2:
                    ws.append((sem["ve"], unp_v[key - 2]))
                d_pk = dma("sync", pkb[par],
                           pkT[ks * 128:ks * 128 + 128,
                               nblk * 512:nblk * 512 + 512],
                           s="ld", waits=ws)
                W("vector", sem["ld"], d_pk)
                if key >= 2:
                    W("vector", sem["te"], unp_te[key - 2])
                if key == 0:
                    for m in range(8):
                        W("vector", sem["ve"], c2_done[m])
                def f(e, par=par):
                    e.tensor_copy(p_bf[:], pkb[par])
                    e.tensor_single_scalar(lo4b[:].bitcast(I32),
                                           pkb[par].bitcast(I32),
                                           0x0F0F0F0F, ALU.bitwise_and)
                    e.drain()
                    e.tensor_copy(lo4_bf[:], lo4b[:])
                    e.drain()
                    e.tensor_sub(wfs[par][0][:], p_bf[:], lo4_bf[:])
                    e.tensor_single_scalar(p_bf[:], lo4_bf[:], 7.5, ALU.is_gt)
                    e.drain()
                    e.tensor_scalar_mul(wfs[par][0][:], wfs[par][0][:],
                                        1.0 / 16.0)
                    e.scalar_tensor_tensor(
                        out=wfs[par][1][:], in0=p_bf[:], scalar=-16.0,
                        in1=lo4_bf[:], op0=ALU.mult, op1=ALU.add)\
                        .then_inc(sem["ve"], 1)
                E("vector", f)
                unp_v[key] = inc(sem["ve"])
                W("tensor", sem["ve"], unp_v[key])
                for m in range(8):
                    if ks == 0:
                        if nblk > 0:
                            W("tensor", sem["se"], c3evac[(nblk - 1, m)])
                    def f(e, m=m, par=par, ks=ks, nblk=nblk):
                        e.matmul(pband[m][:, 0:512],
                                 aslot[m][:, (2 * ks) * 128:(2 * ks) * 128 + 128],
                                 wfs[par][0][:],
                                 start=(ks == 0), stop=False,
                                 skip_group_check=True)
                        i = e.matmul(pband[m][:, 0:512],
                                     aslot[m][:, (2 * ks + 1) * 128:
                                              (2 * ks + 1) * 128 + 128],
                                     wfs[par][1][:],
                                     start=False, stop=(ks == 15),
                                     skip_group_check=True)
                        if ks == 0:
                            i = e.matmul(pband[m][:, 0:512], onesbf[:],
                                         biasrow[0:1, nblk * 512:
                                                 nblk * 512 + 512],
                                         start=False, stop=False,
                                         skip_group_check=True)
                        i.then_inc(sem["te"], 1)
                    E("tensor", f)
                    t_m = inc(sem["te"])
                    if ks == 15:
                        W("scalar", sem["te"], t_m)
                        if nblk == 0 and m == 0:
                            pass
                        def f(e, m=m, nblk=nblk):
                            e.copy(ffn_bf[:, m * 4096 + nblk * 512:
                                          m * 4096 + nblk * 512 + 512],
                                   pband[m][:, 0:512]).then_inc(sem["se"], 1)
                        E("scalar", f)
                        c3evac[(nblk, m)] = inc(sem["se"])
                unp_te = unp_te if key > 0 else {}
                unp_te[key] = cnt[sem["te"]]

        x2back = None
        for m in range(8):
            fslice = ffn_bf[:, m * 4096:(m + 1) * 4096]
            ln3 = emit_ln(fslice, 16, sem["se"], c3evac[(7, m)])
            dx2 = dma("sync", xrow[0][:], x2_sp[m], s="ld",
                      waits=[x for x in [x2back] if x])
            W("vector", sem["ve"], ln3)
            W("vector", sem["ld"], dx2)
            def f(e, m=m):
                e.scalar_tensor_tensor(
                    out=xrow[0][:], in0=ffn_bf[:, m * 4096:(m + 1) * 4096],
                    scalar=stats[:, 23:24], in1=xrow[0][:],
                    op0=ALU.mult, op1=ALU.add)
                e.drain()
                e.tensor_scalar(xrow[0][:], xrow[0][:], stats[:, 22:23], None,
                                ALU.add).then_inc(sem["ve"], 1)
            E("vector", f)
            v_out = inc(sem["ve"])
            d_out = dma("sync", out[m * 128:(m + 1) * 128, :], xrow[0][:],
                        s="out", waits=[(sem["ve"], v_out)])
            x2back = (sem["out"], d_out)

    dump_waits = [(sem["ve"], cnt[sem["ve"]]), (sem["se"], cnt[sem["se"]])]
    for name, ap in dbgs.items():
        key = name[4:]
        src = {"srun": srun, "dtdrm": dtdrm, "ecirm": ecirm,
               "dtrm": dtrm, "cumrm": cumrm,
               "cdec": cdec, "brm": brm, "dtraw": dtraw, "stats": stats,
               "cols": cols, "xfm": xfm, "wbuf0": wbuf[0], "xln0": xln[0]}.get(key)
        if src is None:
            continue
        ncols = min(src.shape[1], INTER)
        dma("gpsimd", ap[:, 0:ncols], src[:, 0:ncols], s="dg",
            waits=dump_waits)
        dump_waits = []

    if not stage_ge("c"):
        W("vector", sem["ve"], cnt[sem["ve"]])
        W("vector", sem["dg"], cnt[sem["dg"]])
        E("vector", lambda e: e.memset(dtraw[:], 0.0).then_inc(sem["ve"], 1))
        vz = inc(sem["ve"])
        for mt in range(8):
            for q in range(4):
                dma("sync", out[mt * 128:(mt + 1) * 128,
                                q * 1024:(q + 1) * 1024],
                    dtraw[:], s="out", waits=[(sem["ve"], vz)] if q == 0 else ())
    W("sync", sem["out"], cnt[sem["out"]])
    W("sync", sem["sp"], cnt[sem["sp"]])
    W("sync", sem["dg"], cnt[sem["dg"]])
    W("sync", sem["cc"], cc_done)

    class _MockInstr:
        def __init__(self, rec):
            self.rec = rec
        def then_inc(self, s, n=1):
            self.rec[s] = self.rec.get(s, 0) + n
            return self

    class _MockEng:
        def __init__(self, rec):
            self._rec = rec
        def __getattr__(self, name):
            rec = self._rec
            def fn(*a, **k):
                return _MockInstr(rec)
            return fn

    mock_cnt = {}
    for engname in ENGS:
        me = _MockEng(mock_cnt)
        for fop in ops[engname]:
            fop(me)
    for s, v in cnt.items():
        mv = mock_cnt.get(s, 0)
        assert mv == v, f"sem accounting mismatch for {s.name}: planned {v} actual {mv}"
    for s, v in mock_cnt.items():
        assert cnt.get(s, 0) == v, f"unplanned incs on {s.name}: {v}"

    with nc.Block() as block:
        @block.sync
        def _(e):
            for f in ops["sync"]:
                f(e)

        @block.scalar
        def _(e):
            for f in ops["scalar"]:
                f(e)

        @block.vector
        def _(e):
            for f in ops["vector"]:
                f(e)

        @block.tensor
        def _(e):
            for f in ops["tensor"]:
                f(e)

        @block.gpsimd
        def _(e):
            for f in ops["gpsimd"]:
                f(e)

    es.close()
    return nc


def _prep(inputs):
    x = np.asarray(inputs["L0_attn_input_states"], dtype=np.float32)
    win = np.asarray(inputs["L0_attn_in_proj_w"], np.float32)
    winfm = np.ascontiguousarray(
        win[INTER:, :].reshape(81, 128, 32, 128).transpose(0, 3, 2, 1)
        .reshape(81, 128, 32 * 128))
    wingt = np.ascontiguousarray(
        win[:INTER, :].reshape(16, 512, 32, 128).transpose(0, 2, 3, 1)
        .reshape(16, 4, 8, 128, 512).transpose(0, 1, 3, 2, 4)
        .reshape(64, 128, 8 * 512))
    wout = np.asarray(inputs["L0_attn_out_w"], np.float32)
    woutt = np.ascontiguousarray(
        wout.reshape(8, 512, 64, 128).transpose(0, 2, 3, 1)
        .reshape(8, 8, 8, 128, 512).transpose(0, 1, 3, 2, 4)
        .reshape(64, 128, 8 * 512))
    pkT = np.ascontiguousarray(
        np.asarray(inputs["L0_ffn_weight_int4"], np.int8).T)
    convw = np.ascontiguousarray(
        np.asarray(inputs["L0_attn_conv_w"], np.float32).reshape(80, 128, 4)
        .transpose(1, 0, 2).reshape(128, 320))
    convb = np.ascontiguousarray(
        np.asarray(inputs["L0_attn_conv_b"], np.float32).reshape(80, 128).T)
    nw = np.ascontiguousarray(
        np.asarray(inputs["L0_attn_norm_w"], np.float32).reshape(64, 128))
    ffnb = np.asarray(inputs["L0_ffn_bias"], np.float16).reshape(1, HID)
    fscale = np.asarray(inputs["L0_ffn_weight_scale"], np.float16).reshape(1, 1)
    identbf = np.eye(128, dtype=ml_dtypes.bfloat16)
    identf = np.eye(128, dtype=np.float32)
    jj, ii = np.meshgrid(np.arange(128), np.arange(128), indexing="ij")
    maskd = (ii >= jj).astype(ml_dtypes.bfloat16)

    per_core = []
    for core in range(8):
        b, cl = core // 4, core % 4
        xb = x[b]
        r0 = cl * 1024
        halo = np.zeros((3, HID), np.float32) if cl == 0 else xb[r0 - 3:r0]
        xsh = np.ascontiguousarray(np.concatenate([halo, xb[r0:r0 + 1024]], 0))
        fm = np.zeros((128, 6), np.float32)
        for d in range(3):
            m = 1.0 if d < cl else 0.0
            fm[:, d] = m
            fm[:, 3 + d] = 1.0 - m
        per_core.append(dict(
            xs=xsh, winfm=winfm, wingt=wingt, woutt=woutt, pkT=pkT,
            convw=convw, convb=convb,
            dtb=np.asarray(inputs["L0_attn_dt_bias"], np.float32),
            alog=np.asarray(inputs["L0_attn_A_log"], np.float32),
            dvec=np.asarray(inputs["L0_attn_D"], np.float32).reshape(1, H),
            nw=nw, ffnb=ffnb, fscale=fscale, identbf=identbf, identf=identf,
            maskd=maskd, fmask=fm))
    return per_core


def kernel(**inputs):
    per_core = _prep(inputs)
    nc = build()
    res = run_bass_kernel_spmd(nc, per_core, core_ids=list(range(8)))
    outs = [res.results[i]["out"] for i in range(8)]
    b0 = np.concatenate(outs[0:4], axis=0)
    b1 = np.concatenate(outs[4:8], axis=0)
    return np.stack([b0, b1], axis=0)


# version 10 (final)
# speedup vs baseline: 1.0164x; 1.0164x over previous
import contextlib
from collections import defaultdict

import numpy as np
import ml_dtypes

import concourse.bass as bass
import concourse.mybir as mybir
from concourse.bass_utils import run_bass_kernel_spmd

F32 = mybir.dt.float32
BF16 = mybir.dt.bfloat16
F16 = mybir.dt.float16
I8 = mybir.dt.int8
I32 = mybir.dt.int32
AF = mybir.ActivationFunctionType
ALU = mybir.AluOpType
AXX = mybir.AxisListType.X

HID = 4096
PROJ = 18560
INTER = 8192
H = 128
PD = 64
G = 8
CS = 256
ROWS = 1024
LIN = 1027
KT = 32
NCH = 4
EPS = 1e-5
FM_SPLITS = ((0, 342), (342, 342), (684, 343))
ENGS = ("sync", "scalar", "vector", "tensor", "gpsimd")


def build(debug=(), upto="c", no_states=False, ag_late=True):
    nc = bass.Bass("TRN2", target_bir_lowering=False)

    xs = nc.dram_tensor("xs", [LIN, HID], F32, kind="ExternalInput").ap()
    winfm = nc.dram_tensor("winfm", [81, 128, KT * 128], F32, kind="ExternalInput").ap()
    wingt = nc.dram_tensor("wingt", [64, 128, 8 * 512], F32, kind="ExternalInput").ap()
    woutt = nc.dram_tensor("woutt", [64, 128, 8 * 512], F32, kind="ExternalInput").ap()
    pkT = nc.dram_tensor("pkT", [HID // 2, HID], I8, kind="ExternalInput").ap()
    convw_in = nc.dram_tensor("convw", [128, 320], F32, kind="ExternalInput").ap()
    convb_in = nc.dram_tensor("convb", [128, 80], F32, kind="ExternalInput").ap()
    dtb_in = nc.dram_tensor("dtb", [H], F32, kind="ExternalInput").ap()
    alog_in = nc.dram_tensor("alog", [H], F32, kind="ExternalInput").ap()
    dvec_in = nc.dram_tensor("dvec", [1, H], F32, kind="ExternalInput").ap()
    nw_in = nc.dram_tensor("nw", [64, 128], F32, kind="ExternalInput").ap()
    ffnb_in = nc.dram_tensor("ffnb", [1, HID], F16, kind="ExternalInput").ap()
    fscale_in = nc.dram_tensor("fscale", [1, 1], F16, kind="ExternalInput").ap()
    identbf_in = nc.dram_tensor("identbf", [128, 128], BF16, kind="ExternalInput").ap()
    identf_in = nc.dram_tensor("identf", [128, 128], F32, kind="ExternalInput").ap()
    maskd_in = nc.dram_tensor("maskd", [128, 128], BF16, kind="ExternalInput").ap()
    fmask_in = nc.dram_tensor("fmask", [128, 6], F32, kind="ExternalInput").ap()

    out = nc.dram_tensor("out", [ROWS, HID], F32, kind="ExternalOutput").ap()

    def scratch(name, shape, dtype):
        kind = "ExternalOutput" if name in debug else "Internal"
        return nc.dram_tensor(name, shape, dtype, kind=kind).ap()

    hs_sp = scratch("hs_sp", [8, 128, 64, 128], BF16)
    gate_sp = scratch("gate_sp", [8, 128, INTER], BF16)
    st_sp = scratch("st_sp", [NCH * 128, INTER], BF16)
    bc_sp = scratch("bc_sp", [NCH, 128, 4096], BF16)
    g_sp = scratch("g_sp", [8, 128, 64, 128], BF16)
    x2_sp = scratch("x2_sp", [8, 128, HID], F32)
    cum_dram = scratch("cum_dram", [NCH, 128, 256], F32)
    ag_in = scratch("ag_in", [128, INTER + 128], BF16)
    ag_out = scratch("ag_out", [4 * 128, INTER + 128], BF16)
    dbgs = {}
    for name in debug:
        if name.startswith("dbg_"):
            dbgs[name] = scratch(name, [128, INTER], F32)

    ops = {e: [] for e in ENGS}
    cnt = defaultdict(int)

    def E(eng, fn):
        ops[eng].append(fn)

    def inc(sem_h, n=1):
        cnt[sem_h] += n
        return cnt[sem_h]

    def DR(eng):
        E(eng, lambda e: e.drain())

    def W(eng, sem_h, val):
        assert val <= cnt[sem_h], f"wait on future value {val} > {cnt[sem_h]}"
        if val > 0:
            E(eng, lambda e, s=sem_h, v=val: e.wait_ge(s, v))

    es = contextlib.ExitStack()
    sem = {k: es.enter_context(nc.semaphore(f"s_{k}"))
           for k in ("ld", "w", "sp", "te", "ve", "se", "gp", "cc", "out", "dg")}
    MARK = {"tensor": "te", "vector": "ve", "scalar": "se", "gpsimd": "gp",
            "sync": "ld"}

    def dma(eng, out_ap, in_ap, s="ld", waits=()):
        v = inc(sem[s], 16)
        def f(e, o=out_ap, i=in_ap, s2=sem[s], ws=tuple(waits)):
            for (sh, vv) in ws:
                e.wait_ge(sh, vv)
            e.dma_start(out=o, in_=i).then_inc(s2, 16)
        E(eng, f)
        return v

    sb = {}

    def alloc(name, shape, dtype):
        t = es.enter_context(nc.sbuf_tensor("sb_" + name, shape, dtype))
        sb[name] = t
        return t

    ident_bf = alloc("ident_bf", [128, 128], BF16)
    ident_f = alloc("ident_f", [128, 128], F32)
    maskd = alloc("maskd", [128, 128], BF16)
    fmask = alloc("fmask", [128, 6], F32)
    convw_sb = alloc("convw_sb", [128, 320], F32)
    convb_sb = alloc("convb_sb", [128, 80], F32)
    cols = alloc("cols", [128, 8], F32)
    dbc = alloc("dbc", [128, 128], F32)
    nwcols = alloc("nwcols", [128, 64], F32)
    onesrow = alloc("onesrow", [1, 128], F32)
    ones256 = alloc("ones256", [128, 256], F32)
    drow = alloc("drow", [1, 128], F32)
    clrow = alloc("clrow", [1, 128], F32)
    nw64 = alloc("nw64", [64, 128], F32)
    dtdrm = alloc("dtdrm", [128, 1024], F32)
    ecirm = alloc("ecirm", [128, 1024], F32)
    dtrm = alloc("dtrm", [128, 1024], F32)
    cumrm = alloc("cumrm", [128, 1024], F32)
    cdec = alloc("cdec", [128, 512], F32)
    acc = alloc("acc", [128, INTER], BF16)
    pbc = alloc("pbc", [128, 128], F32)
    stats = alloc("stats", [128, 96], F32)

    xfm = alloc("xfm", [128, KT * LIN], BF16)
    agbuf = alloc("agbuf", [128, INTER], BF16)
    sqscrap = agbuf
    xrow0 = alloc("xrow0", [128, HID], F32)
    xrow = [xrow0, xrow0]
    xln = [alloc(f"xln{i}", [128, HID], BF16) for i in range(2)]
    wbuf = [alloc(f"wbuf{i}", [128, KT * 128], BF16) for i in range(2)]
    cbuf = alloc("cbuf", [128, LIN], BF16)
    cvacc = alloc("cvacc", [128, 1024], BF16)
    silout = alloc("silout", [128, 1024], BF16)
    brm = alloc("brm", [128, 8192], BF16)
    dtraw = alloc("dtraw", [128, 1024], F32)
    dtw = [alloc(f"dtw{i}", [128, 256], F32) for i in range(8)]
    hstr = alloc("hstr", [128, 1024], BF16)
    ststg = alloc("ststg", [128, 512], BF16)
    pstg = alloc("pstg", [128, 128], BF16)
    xwb = alloc("xwb", [128, 256], BF16)
    wgq = wbuf
    cbufs = [cbuf, xln[0][:, 0:1027]]
    cvaccs = [cvacc, xln[0][:, 1027:2051]]
    silouts = [silout, xln[0][:, 2051:3075]]
    gstage = [cbuf[:, 0:512], cbuf[:, 512:1024]]
    agp = alloc("agp", [128, 128], BF16)
    onesbf = alloc("onesbf", [1, 128], BF16)

    pband = [es.enter_context(nc.psum_tensor(f"pb{i}", [128, 512], F32))
             for i in range(8)]

    ring_guard = [None, None]
    ring_state = {"i": 0}

    def small_pe(pe_fn, ev_fn, evac_eng="vector", pre_waits=()):
        slot = ring_state["i"] % 2
        ring_state["i"] += 1
        for (sh, v) in pre_waits:
            W("tensor", sh, v)
        g = ring_guard[slot]
        if g is not None:
            W("tensor", g[0], g[1])
        def f(e, fn=pe_fn, s=slot):
            fn(e, s).then_inc(sem["te"], 1)
        E("tensor", f)
        tv = inc(sem["te"])
        W(evac_eng, sem["te"], tv)
        m = sem[MARK[evac_eng]]
        def f2(e, fn=ev_fn, s=slot, m=m):
            fn(e, s).then_inc(m, 1)
        E(evac_eng, f2)
        ev = inc(m)
        ring_guard[slot] = (m, ev)
        return ev

    dma("sync", ident_bf[:], identbf_in)
    dma("sync", ident_f[:], identf_in)
    dma("sync", maskd[:], maskd_in)
    dma("sync", fmask[:], fmask_in)
    dma("sync", convw_sb[:], convw_in)
    dma("sync", convb_sb[:], convb_in)
    dma("sync", cols[:, 0:1], dtb_in.rearrange("(p o) -> p o", o=1))
    dma("sync", cols[:, 2:3], alog_in.rearrange("(p o) -> p o", o=1))
    dma("sync", drow[:], dvec_in)
    v_consts = dma("sync", nw64[:], nw_in)

    W("vector", sem["ld"], v_consts)
    def f(e):
        e.memset(onesrow[:], 1.0)
        e.memset(onesbf[:], 1.0)
        e.memset(ones256[:], 1.0)
        e.memset(acc[:], 0.0)
        e.memset(pbc[:], 1.0).then_inc(sem["ve"], 1)
    E("vector", f)
    v_memset = inc(sem["ve"])

    W("scalar", sem["ld"], v_consts)
    E("scalar", lambda e: e.activation(out=cols[:, 1:2], in_=cols[:, 2:3],
                                       func=AF.Exp).then_inc(sem["se"], 1))
    s_A = inc(sem["se"])
    W("vector", sem["se"], s_A)
    E("vector", lambda e: e.tensor_scalar_mul(cols[:, 1:2], cols[:, 1:2], -1.0)
      .then_inc(sem["ve"], 1))
    v_A = inc(sem["ve"])

    W("tensor", sem["ld"], v_consts)
    W("tensor", sem["ve"], v_memset)
    def f(e):
        e.matmul(pband[6][:, 0:128], onesrow[:], drow[:], start=True, stop=True)
        e.transpose(pband[7][:, 0:64], in_=nw64[:], identity=ident_f[0:64, 0:64])\
            .then_inc(sem["te"], 1)
    E("tensor", f)
    t_const = inc(sem["te"])
    W("vector", sem["te"], t_const)
    def f(e):
        e.tensor_copy(dbc[:], pband[6][:, 0:128])
        e.tensor_copy(nwcols[:], pband[7][:, 0:64]).then_inc(sem["ve"], 1)
    E("vector", f)
    v_const2 = inc(sem["ve"])
    ring_guard[0] = (sem["ve"], v_const2)
    ring_guard[1] = (sem["ve"], v_const2)

    ln_done = {}
    tr_done = {}
    for rt in range(9):
        par = rt % 2
        sc = 8 * rt
        r0 = rt * 128
        nr = 128 if rt < 8 else 3
        dwaits = []
        if rt >= 1:
            dwaits.append((sem["ve"], ln_done[rt - 1]))
        if rt == 8:
            E("vector", lambda e, b=xrow[par]: e.memset(b[:], 1.0)
              .then_inc(sem["ve"], 1))
            dwaits = [(sem["ve"], inc(sem["ve"]))]
        vx = dma("sync", xrow[par][0:nr, :], xs[r0:r0 + nr, :], waits=dwaits)

        W("vector", sem["ld"], vx)
        E("vector", lambda e, b=xrow[par], sc=sc: e.reduce_sum(
            stats[:, sc + 0:sc + 1], b[:], axis=AXX).then_inc(sem["ve"], 1))
        v1 = inc(sem["ve"])
        DR("vector")
        W("scalar", sem["ld"], vx)
        E("scalar", lambda e, b=xrow[par], sc=sc: e.activation(
            out=sqscrap[:, 0:HID], in_=b[:], func=AF.Square,
            accum_out=stats[:, sc + 1:sc + 2]).then_inc(sem["se"], 1))
        s1 = inc(sem["se"])
        W("vector", sem["se"], s1)
        def f(e, sc=sc):
            e.tensor_scalar_mul(stats[:, sc + 2:sc + 3], stats[:, sc + 0:sc + 1],
                                1.0 / HID)
            e.tensor_scalar_mul(stats[:, sc + 3:sc + 4], stats[:, sc + 1:sc + 2],
                                1.0 / HID)
            e.drain()
            e.tensor_mul(stats[:, sc + 4:sc + 5], stats[:, sc + 2:sc + 3],
                         stats[:, sc + 2:sc + 3])
            e.drain()
            e.tensor_sub(stats[:, sc + 5:sc + 6], stats[:, sc + 3:sc + 4],
                         stats[:, sc + 4:sc + 5])
            e.drain()
            e.tensor_scalar_add(stats[:, sc + 5:sc + 6], stats[:, sc + 5:sc + 6],
                                EPS).then_inc(sem["ve"], 1)
        E("vector", f)
        v2 = inc(sem["ve"])
        W("scalar", sem["ve"], v2)
        def f(e, sc=sc):
            e.activation(out=stats[:, sc + 6:sc + 7], in_=stats[:, sc + 5:sc + 6],
                         func=AF.Ln)
            e.drain()
            e.activation(out=stats[:, sc + 7:sc + 8], in_=stats[:, sc + 6:sc + 7],
                         func=AF.Exp, scale=-0.5).then_inc(sem["se"], 1)
        E("scalar", f)
        s2 = inc(sem["se"])
        W("vector", sem["se"], s2)
        if rt >= 2:
            W("vector", sem["te"], tr_done[rt - 2])
        def f(e, sc=sc, par=par):
            e.tensor_mul(stats[:, sc + 6:sc + 7], stats[:, sc + 2:sc + 3],
                         stats[:, sc + 7:sc + 8])
            e.drain()
            e.tensor_scalar_mul(stats[:, sc + 6:sc + 7], stats[:, sc + 6:sc + 7],
                                -1.0)
            e.drain()
            e.tensor_scalar(xln[par][:], xrow[par][:], stats[:, sc + 7:sc + 8],
                            stats[:, sc + 6:sc + 7], ALU.mult, ALU.add)\
                .then_inc(sem["ve"], 1)
        E("vector", f)
        ln_done[rt] = inc(sem["ve"])

        for kt in range(KT):
            def mk_pe(kt=kt, par=par):
                def pe(e, slot):
                    return e.transpose(
                        pband[6 + slot][:, :].bitcast(BF16)[:, 0:128],
                        in_=xln[par][:, kt * 128:(kt + 1) * 128],
                        identity=ident_bf[:])
                return pe
            def mk_ev(kt=kt, rt=rt, nr=nr):
                def ev(e, slot):
                    return e.tensor_copy(
                        xfm[:, kt * LIN + rt * 128: kt * LIN + rt * 128 + nr],
                        pband[6 + slot][:, :].bitcast(BF16)[:, 0:nr])
                return ev
            small_pe(mk_pe(), mk_ev(),
                     pre_waits=[(sem["ve"], ln_done[rt])] if kt == 0 else ())
        tr_done[rt] = cnt[sem["te"]]
    v_xfm = cnt[sem["ve"]]

    STAGES = ["a1", "a2dt", "a2bc", "a2", "a3", "a4", "a5", "b", "c"]
    def stage_ge(s):
        return STAGES.index(upto) >= STAGES.index(s)
    nt_order = []
    if stage_ge("a2dt"):
        nt_order = [80]
    if stage_ge("a2bc"):
        nt_order += list(range(64, 80))
    if stage_ge("a2"):
        nt_order += list(range(64))
    te_after_nt = {}
    fm_evac_done = {}
    guard_cbuf = {}
    guard_sil = {0: [], 1: []}
    guard_hstr = []
    guard_xwb = []
    guard_ststg = []
    fold_v = 0

    def emit_post(nt, par, sd):
        nonlocal guard_hstr, guard_xwb, guard_ststg, fold_v
        silo = silouts[par]
        if 64 <= nt < 80:
            isB = nt < 72
            g2 = nt - 64 if isB else nt - 72
            for c in range(NCH):
                col = g2 * 256 if isB else 2048 + g2 * 256
                v = dma("sync", bc_sp[c, :, col:col + 256],
                        silo[:, c * 256:(c + 1) * 256], s="sp",
                        waits=[(sem["se"], sd)])
            guard_sil[par].append((sem["sp"], v))
            if isB:
                for c in range(NCH):
                    for jh in range(2):
                        def mk_pe(c=c, jh=jh, silo=silo):
                            def pe(e, slot):
                                return e.transpose(
                                    pband[6 + slot][:, :].bitcast(BF16)[:, 0:128],
                                    in_=silo[:, c * 256 + jh * 128:
                                             c * 256 + jh * 128 + 128],
                                    identity=ident_bf[:])
                            return pe
                        def mk_ev(c=c, jh=jh, g2=g2):
                            def ev(e, slot):
                                col = ((g2 * 4 + c) * 2 + jh) * 128
                                return e.tensor_copy(
                                    brm[:, col:col + 128],
                                    pband[6 + slot][:, :].bitcast(BF16)[:, 0:128])
                            return ev
                        small_pe(mk_pe(), mk_ev(), pre_waits=[(sem["se"], sd)])
                guard_sil[par].append((sem["te"], cnt[sem["te"]]))
            return
        h0 = 2 * nt
        g2 = h0 >> 4
        ev_blocks = {}
        first = True
        for c in range(NCH):
            for mh in range(2):
                def mk_pe(c=c, mh=mh, silo=silo):
                    def pe(e, slot):
                        return e.transpose(
                            pband[6 + slot][:, :].bitcast(BF16)[:, 0:128],
                            in_=silo[:, c * 256 + mh * 128:
                                     c * 256 + mh * 128 + 128],
                            identity=ident_bf[:])
                    return pe
                def mk_ev(c=c, mh=mh):
                    def ev(e, slot):
                        col = (2 * c + mh) * 128
                        return e.tensor_copy(
                            hstr[:, col:col + 128],
                            pband[6 + slot][:, :].bitcast(BF16)[:, 0:128])
                    return ev
                pw = [(sem["se"], sd)]
                if first:
                    for gg in guard_hstr:
                        W("vector", *gg)
                    guard_hstr = []
                    first = False
                ev_blocks[(c, mh)] = small_pe(mk_pe(), mk_ev(), pre_waits=pw)
        guard_sil[par].append((sem["te"], cnt[sem["te"]]))
        for c in range(NCH):
            for mh in range(2):
                rb = 2 * c + mh
                v = dma("sync", hs_sp[rb, :, nt, :],
                        hstr[:, rb * 128:rb * 128 + 128], s="sp",
                        waits=[(sem["ve"], ev_blocks[(c, mh)])])
        guard_hstr.append((sem["sp"], v))
        for c in range(NCH if not no_states else 0):
            W("vector", sem["ve"], ev_blocks[(c, 1)])
            for gg in guard_xwb:
                W("vector", *gg)
            guard_xwb = []
            def f(e, c=c, h0=h0):
                last = None
                for jh in range(2):
                    for hh in range(2):
                        h = h0 + hh
                        last = e.tensor_scalar(
                            xwb[:, (jh * 2 + hh) * 64:(jh * 2 + hh) * 64 + 64],
                            hstr[:, (2 * c + jh) * 128 + hh * 64:
                                 (2 * c + jh) * 128 + hh * 64 + 64],
                            dtdrm[:, (2 * c + jh) * 128 + h:
                                  (2 * c + jh) * 128 + h + 1],
                            None, ALU.mult)
                last.then_inc(sem["ve"], 1)
            E("vector", f)
            vxw = inc(sem["ve"])
            for hh in range(2):
                h = h0 + hh
                def mk_pe(c=c, hh=hh, g2=g2):
                    def pe(e, slot):
                        last = None
                        for jh in range(2):
                            col = ((g2 * 4 + c) * 2 + jh) * 128
                            last = e.matmul(
                                pband[6 + slot][:, 384:448],
                                brm[:, col:col + 128],
                                xwb[:, (jh * 2 + hh) * 64:
                                    (jh * 2 + hh) * 64 + 64],
                                start=(jh == 0), stop=(jh == 1),
                                skip_group_check=True)
                        return last
                    return pe
                def mk_ev(c=c, hh=hh):
                    def ev(e, slot):
                        return e.tensor_copy(
                            ststg[:, c * 128 + hh * 64:c * 128 + hh * 64 + 64],
                            pband[6 + slot][:, 384:448])
                    return ev
                pw = [(sem["ve"], vxw)]
                if guard_ststg:
                    for gg in guard_ststg:
                        W("vector", *gg)
                    guard_ststg = []
                evs = small_pe(mk_pe(), mk_ev(), pre_waits=pw)
                W("vector", sem["ve"], evs)
                DR("vector")
                def f(e, c=c, hh=hh, h0=h0):
                    h = h0 + hh
                    e.scalar_tensor_tensor(
                        out=acc[:, h * 64:h * 64 + 64],
                        in0=acc[:, h * 64:h * 64 + 64],
                        scalar=cdec[:, c * 128 + h:c * 128 + h + 1],
                        in1=ststg[:, c * 128 + hh * 64:
                                  c * 128 + hh * 64 + 64],
                        op0=ALU.mult, op1=ALU.add).then_inc(sem["ve"], 1)
                E("vector", f)
                fold_v = inc(sem["ve"])
            guard_xwb.append((sem["te"], cnt[sem["te"]]))
        for c in range(NCH if not no_states else 0):
            v = dma("sync",
                    st_sp[c * 128:(c + 1) * 128, 128 * nt:128 * nt + 128],
                    ststg[:, c * 128:(c + 1) * 128], s="sp",
                    waits=[(sem["ve"], fold_v)])
            guard_ststg.append((sem["sp"], v))

    pending_post = None
    for it, nt in enumerate(nt_order):
        par = it % 2
        col0 = INTER + 128 * nt
        wws = []
        if it >= 2:
            wws.append((sem["te"], te_after_nt[nt_order[it - 2]]))
        wd = dma("gpsimd", wbuf[par][:], winfm[nt], s="w", waits=wws)
        W("tensor", sem["w"], wd)
        if it == 0:
            W("tensor", sem["ve"], v_xfm)
        if par in fm_evac_done:
            W("tensor", *fm_evac_done[par])
        pbase = 0 if par == 0 else 3
        def f(e, par=par, pbase=pbase):
            for kt in range(KT):
                lhsT = wbuf[par][:, kt * 128:(kt + 1) * 128]
                last = None
                for si, (off, n) in enumerate(FM_SPLITS):
                    last = e.matmul(pband[pbase + si][:, 0:n], lhsT,
                                    xfm[:, kt * LIN + off: kt * LIN + off + n],
                                    start=(kt == 0), stop=(kt == KT - 1),
                                    skip_group_check=True)
            last.then_inc(sem["te"], 1)
        E("tensor", f)
        te_after_nt[nt] = inc(sem["te"])

        W("scalar", sem["te"], te_after_nt[nt])
        if nt == 80:
            def f(e, pbase=pbase):
                e.copy(dtraw[:, 0:339], pband[pbase + 0][:, 3:342])
                e.copy(dtraw[:, 339:681], pband[pbase + 1][:, 0:342])
                e.copy(dtraw[:, 681:1024], pband[pbase + 2][:, 0:343])\
                    .then_inc(sem["se"], 1)
        else:
            if par in guard_cbuf:
                W("scalar", *guard_cbuf[par])
            def f(e, pbase=pbase, par=par):
                e.copy(cbufs[par][:, 0:342], pband[pbase + 0][:, 0:342])
                e.copy(cbufs[par][:, 342:684], pband[pbase + 1][:, 0:342])
                e.copy(cbufs[par][:, 684:1027], pband[pbase + 2][:, 0:343])\
                    .then_inc(sem["se"], 1)
        E("scalar", f)
        ed = inc(sem["se"])
        fm_evac_done[par] = (sem["se"], ed)

        if nt == 80:
            for c in range(NCH):
                t0, t1, t2, t3 = dtw[4 * (c % 2):4 * (c % 2) + 4]
                sc = 16 * (c % 2)
                W("scalar", sem["se"], ed)
                W("scalar", sem["ve"], v_A)
                if c >= 2:
                    W("scalar", sem["te"], dt_tr[c - 2])
                def f(e, c=c, t0=t0, t1=t1, t2=t2, t3=t3):
                    dcol = dtraw[:, c * 256:(c + 1) * 256]
                    e.activation(out=t0[:], in_=dcol, func=AF.Abs,
                                 bias=cols[:, 0:1])
                    e.drain()
                    e.activation(out=t1[:], in_=t0[:], func=AF.Exp, scale=-1.0)
                    e.drain()
                    e.activation(out=t2[:], in_=t1[:], func=AF.Ln, bias=1.0)
                    e.activation(out=t3[:], in_=dcol, func=AF.Relu,
                                 bias=cols[:, 0:1]).then_inc(sem["se"], 1)
                E("scalar", f)
                sd = inc(sem["se"])
                W("vector", sem["se"], sd)
                def f(e, t1=t1, t2=t2, t3=t3, sc=sc):
                    e.tensor_add(t3[:], t3[:], t2[:])
                    e.drain()
                    e.tensor_scalar(t1[:], t3[:], cols[:, 1:2], None,
                                    ALU.mult)
                    e.drain()
                    e.tensor_tensor_scan(out=t2[:], data0=ones256[:],
                                         data1=t1[:], initial=0.0,
                                         op0=ALU.mult, op1=ALU.add)
                    e.drain()
                    e.tensor_scalar_mul(stats[:, sc + 12:sc + 13],
                                        t2[:, 255:256], -1.0)\
                        .then_inc(sem["ve"], 1)
                E("vector", f)
                vd = inc(sem["ve"])
                dma("sync", cum_dram[c], t2[:], s="sp", waits=[(sem["ve"], vd)])
                W("scalar", sem["ve"], vd)
                def f(e, t0=t0, t1=t1, t2=t2):
                    e.activation(out=t1[:], in_=t2[:], func=AF.Exp, scale=-1.0,
                                 bias=t2[:, 255:256])
                    e.activation(out=t0[:], in_=t2[:], func=AF.Exp, scale=1.0)\
                        .then_inc(sem["se"], 1)
                E("scalar", f)
                sd2 = inc(sem["se"])
                W("vector", sem["se"], sd2)
                def f(e, t1=t1, t3=t3, c=c):
                    e.tensor_mul(t1[:], t3[:], t1[:]).then_inc(sem["ve"], 1)
                E("vector", f)
                DR("vector")
                vd2 = inc(sem["ve"])
                for (tsrc, dstap) in ((t1, dtdrm), (t0, ecirm), (t3, dtrm),
                                      (t2, cumrm)):
                    for jh in range(2):
                        def mk_pe(tsrc=tsrc, jh=jh):
                            def pe(e, slot):
                                return e.transpose(
                                    pband[6 + slot][:, 128:256],
                                    in_=tsrc[:, jh * 128:(jh + 1) * 128],
                                    identity=ident_f[:])
                            return pe
                        def mk_ev(dstap=dstap, c=c, jh=jh):
                            def ev(e, slot):
                                col = (2 * c + jh) * 128
                                return e.tensor_copy(
                                    dstap[:, col:col + 128],
                                    pband[6 + slot][:, 128:256])
                            return ev
                        small_pe(mk_pe(), mk_ev(),
                                 pre_waits=[(sem["ve"], vd2)])
                def pe2(e, slot, t2=t2):
                    return e.transpose(pband[6 + slot][0:1, 128:256],
                                       in_=t2[:, 255:256], identity=ident_f[:])
                def ev2(e, slot):
                    return e.tensor_copy(clrow[:], pband[6 + slot][0:1, 128:256])
                evn = small_pe(pe2, ev2, pre_waits=[(sem["ve"], vd)])
                def pe3(e, slot):
                    return e.matmul(pband[6 + slot][:, 256:384], onesrow[:],
                                    clrow[:], start=True, stop=True)
                def ev3(e, slot, c=c):
                    return e.activation(out=cdec[:, c * 128:(c + 1) * 128],
                                        in_=pband[6 + slot][:, 256:384],
                                        func=AF.Exp)
                small_pe(pe3, ev3, evac_eng="scalar",
                         pre_waits=[(sem["ve"], evn)])
                dt_tr = dt_tr if c > 0 else {}
                dt_tr[c] = cnt[sem["te"]]
        else:
            W("vector", sem["se"], ed)
            def f(e, t=nt, par=par):
                e.tensor_scalar(cvaccs[par][:], cbufs[par][:, 0:1024],
                                convw_sb[:, 4 * t:4 * t + 1],
                                convb_sb[:, t:t + 1], ALU.mult, ALU.add)
                for k in range(1, 4):
                    e.scalar_tensor_tensor(
                        out=cvaccs[par][:], in0=cbufs[par][:, k:k + 1024],
                        scalar=convw_sb[:, 4 * t + k:4 * t + k + 1],
                        in1=cvaccs[par][:], op0=ALU.mult, op1=ALU.add)
                e.tensor_copy(cvaccs[par][:, 0:1], cvaccs[par][:, 0:1])\
                    .then_inc(sem["ve"], 1)
            E("vector", f)
            cv = inc(sem["ve"])
            guard_cbuf[par] = (sem["ve"], cv)
            W("scalar", sem["ve"], cv)
            for gg in guard_sil[par]:
                W("scalar", *gg)
            guard_sil[par] = []
            E("scalar", lambda e, par=par: e.activation(
                out=silouts[par][:], in_=cvaccs[par][:], func=AF.Silu)
              .then_inc(sem["se"], 1))
            sd = inc(sem["se"])
            if pending_post is not None:
                emit_post(*pending_post)
            pending_post = (nt, par, sd)
    if pending_post is not None:
        emit_post(*pending_post)
        pending_post = None

    cc_done = 0
    d2 = 0
    if stage_ge("a3"):
        W("vector", sem["ve"], cnt[sem["ve"]])
        def f(e):
            last = None
            for c in range(NCH):
                e.drain()
                last = e.tensor_mul(pbc[:], pbc[:],
                                    cdec[:, c * 128:(c + 1) * 128])
            last.then_inc(sem["ve"], 1)
        E("vector", f)
        v_p = inc(sem["ve"])
        d1 = dma("sync", ag_in[:, 0:INTER], acc[:], s="sp",
                 waits=[(sem["ve"], cnt[sem["ve"]])])
        W("vector", sem["ve"], v_p)
        E("vector", lambda e: e.tensor_copy(pstg[:], pbc[:])
          .then_inc(sem["ve"], 1))
        v_pbf = inc(sem["ve"])
        d2 = dma("sync", ag_in[:, INTER:INTER + 128], pstg[:], s="sp",
                 waits=[(sem["ve"], v_pbf)])
        if "dbg_acc" in dbgs:
            dma("gpsimd", dbgs["dbg_acc"][:, 0:INTER], acc[:], s="dg",
                waits=[(sem["ve"], cnt[sem["ve"]])])
        def emit_collective():
            W("gpsimd", sem["sp"], cnt[sem["sp"]])
            W("gpsimd", sem["w"], cnt[sem["w"]])
            W("gpsimd", sem["dg"], cnt[sem["dg"]])
            W("gpsimd", sem["ld"], cnt[sem["ld"]])
            def f(e):
                e.collective_compute(
                    "AllGather", ALU.bypass,
                    replica_groups=[[0, 1, 2, 3], [4, 5, 6, 7]],
                    ins=[ag_in], outs=[ag_out]).then_inc(sem["cc"], 1)
            E("gpsimd", f)
            return inc(sem["cc"])
        if not ag_late:
            cc_done = emit_collective()

    W("tensor", sem["se"], cnt[sem["se"]])
    W("tensor", sem["ve"], cnt[sem["ve"]])
    a4_ve = cnt[sem["ve"]]
    a4_te = cnt[sem["te"]]
    a4_first_scalar = [True]
    gate_te = {}
    for gnb in range(16 if stage_ge("a4") else 0):
        for kq in range(4):
            gi = gnb * 4 + kq
            par = gi % 2
            wws = []
            if gi >= 2:
                wws.append((sem["te"], gate_te[gi - 2]))
            else:
                wws.append((sem["te"], a4_te))
            wd = dma("gpsimd", wgq[par][:], wingt[gnb * 4 + kq], s="w",
                     waits=wws)
            W("tensor", sem["w"], wd)
            if kq == 0 and gnb > 0:
                W("tensor", sem["sp"], gate_spill_done)
            def f(e, par=par, kq=kq):
                last = None
                for kt8 in range(8):
                    kt = kq * 8 + kt8
                    for mt in range(8):
                        last = e.matmul(
                            pband[mt][:, 0:512],
                            xfm[:, kt * LIN + 3 + 128 * mt:
                                kt * LIN + 3 + 128 * mt + 128],
                            wgq[par][:, kt8 * 512:kt8 * 512 + 512],
                            start=(kt == 0), stop=(kt == KT - 1),
                            skip_group_check=True)
                last.then_inc(sem["te"], 1)
            E("tensor", f)
            gate_te[gi] = inc(sem["te"])
        ge = gate_te[gnb * 4 + 3]
        for mt in range(8):
            gpar = mt % 2
            W("scalar", sem["te"], ge)
            if a4_first_scalar[0]:
                W("scalar", sem["ve"], a4_ve)
                a4_first_scalar[0] = False
            if gnb > 0 or mt >= 2:
                W("scalar", sem["sp"], gstage_guard[gpar])
            E("scalar", lambda e, mt=mt, gpar=gpar: e.activation(
                out=gstage[gpar], in_=pband[mt][:, 0:512], func=AF.Silu)
              .then_inc(sem["se"], 1))
            sv = inc(sem["se"])
            v = dma("sync", gate_sp[mt, :, 512 * gnb:512 * gnb + 512],
                    gstage[gpar], s="sp", waits=[(sem["se"], sv)])
            if gnb == 0 and mt == 0:
                gstage_guard = {}
            gstage_guard[gpar] = v
        gate_spill_done = cnt[sem["sp"]]

    if stage_ge("a3") and ag_late:
        cc_done = emit_collective()
    srun = acc
    if stage_ge("a5"):
        W("vector", sem["sp"], d2)
        W("vector", sem["dg"], cnt[sem["dg"]])
        E("vector", lambda e: e.memset(srun[:], 0.0).then_inc(sem["ve"], 1))
    v_sz = inc(sem["ve"]) if stage_ge("a5") else 0
    for d in range(3 if stage_ge("a5") else 0):
        da = dma("sync", agbuf[:], ag_out[d * 128:(d + 1) * 128, 0:INTER],
                 s="ld", waits=[(sem["cc"], cc_done)] if d == 0 else
                 [(sem["ve"], v_sz)])
        dp = dma("sync", agp[:], ag_out[d * 128:(d + 1) * 128, INTER:INTER + 128],
                 s="ld")
        W("vector", sem["ld"], dp)
        W("vector", sem["ve"], v_sz)
        def f(e, d=d):
            e.tensor_scalar(pbc[:], agp[:], fmask[:, d:d + 1],
                            fmask[:, 3 + d:3 + d + 1], ALU.mult, ALU.add)
            e.tensor_scalar(agbuf[:], agbuf[:], fmask[:, d:d + 1], None,
                            ALU.mult)
            e.drain()
            last = None
            for h in range(H):
                last = e.scalar_tensor_tensor(
                    out=srun[:, h * 64:h * 64 + 64],
                    in0=srun[:, h * 64:h * 64 + 64],
                    scalar=pbc[:, h:h + 1],
                    in1=agbuf[:, h * 64:h * 64 + 64],
                    op0=ALU.mult, op1=ALU.add)
            last.then_inc(sem["ve"], 1)
        E("vector", f)
        v_sz = inc(sem["ve"])


    if stage_ge("b"):
        hsc = [xfm[:, 0:8192], xfm[:, 8192:16384]]
        yfull = [xfm[:, 16384:24576], xfm[:, 24576:32768]]
        stc = brm[:, 0:8192]
        gate_mh1 = brm[:, 0:8192]
        gate_mh0 = agbuf[:, 0:8192]
        bcc = wbuf[0][:, 0:4096]
        smb = wbuf[1][:, 0:3072]
        cumb = [dtraw[:, i * 256:(i + 1) * 256] for i in range(4)]
        argb = [cbuf[:, 0:384], cbuf[:, 384:768],
                cvacc[:, 0:384], cvacc[:, 384:768]]
        Lb = [silout[:, 0:384], silout[:, 384:768],
              hstr[:, 0:384], hstr[:, 384:768]]
        swb = [dtw[1][:].bitcast(BF16)[:, 0:384],
               dtw[3][:].bitcast(BF16)[:, 0:384]]
        diagb = [ststg[:, i * 128:(i + 1) * 128] for i in range(4)]
        ibb = dtdrm[:, :].bitcast(BF16)
        ib = [[ibb[:, 0:512], ibb[:, 512:1024]],
              [ibb[:, 1024:1536], ibb[:, 1536:2048]]]
        gstag = [dtw[0][:].bitcast(BF16), dtw[2][:].bitcast(BF16)]
        ecix = dtw[5][:].bitcast(BF16)

        def tcol(ap, col):
            return ap[:, col:col + 1]

        def tsl(h):
            return xln[h // 64][:, (h % 64) * 64:(h % 64) * 64 + 64]

        vnow = lambda: (sem["ve"], cnt[sem["ve"]])
        gB = {
            "hs_free": None, "yfull_free": None, "stc_free": None,
            "bcc_free": None, "smb_free": None,
            "tst_free": (sem["te"], cnt[sem["te"]]),
            "gate0_free": None, "gate1_free": None,
            "banks01_free": None, "banks67_free": vnow(),
            "gstag_free": [None, None],
        }
        first_b_dma = [True]

        def bdma(out_ap, in_ap, waits=()):
            w = [x for x in waits if x is not None]
            if first_b_dma[0]:
                w.append((sem["cc"], cc_done))
                first_b_dma[0] = False
            return dma("sync", out_ap, in_ap, s="ld", waits=w)

        def WO(eng, g):
            if g is not None:
                W(eng, *g)

        sw_te = {}
        cumb_done = {}
        arg_done = {}
        L_done = {}
        yevac = {}
        yass = {}
        for c in range(NCH):
            d_hs0 = bdma(hsc[0], hs_sp[2 * c], waits=[gB["hs_free"]])
            d_hs1 = bdma(hsc[1], hs_sp[2 * c + 1])
            d_st = bdma(stc, st_sp[c * 128:(c + 1) * 128, :],
                        waits=[gB["stc_free"]])
            d_bc = bdma(bcc, bc_sp[c], waits=[gB["bcc_free"]])
            d_g0 = bdma(gate_mh0, gate_sp[2 * c], waits=[gB["gate0_free"]])

            sevac = {}
            for g in range(G):
                if g == 0:
                    WO("tensor", gB["banks01_free"])
                    W("tensor", sem["ld"], d_bc)
                else:
                    W("tensor", sem["se"], sevac[g - 1])
                def f(e, g=g):
                    bcol = g * 256
                    e.matmul(pband[0][:, 0:256], bcc[:, bcol:bcol + 128],
                             bcc[:, 2048 + bcol:2048 + bcol + 256],
                             start=True, stop=True, skip_group_check=True)
                    e.matmul(pband[1][:, 0:128], bcc[:, bcol + 128:bcol + 256],
                             bcc[:, 2048 + bcol + 128:2048 + bcol + 256],
                             start=True, stop=True, skip_group_check=True)\
                        .then_inc(sem["te"], 1)
                E("tensor", f)
                t_g = inc(sem["te"])
                W("scalar", sem["te"], t_g)
                if g == 0:
                    WO("scalar", gB["smb_free"])
                def f(e, g=g):
                    e.copy(smb[:, g * 384:g * 384 + 256], pband[0][:, 0:256])
                    e.copy(smb[:, g * 384 + 256:g * 384 + 384],
                           pband[1][:, 0:128]).then_inc(sem["se"], 1)
                E("scalar", f)
                sevac[g] = inc(sem["se"])
                W("vector", sem["se"], sevac[g])
                def f(e, g=g):
                    e.tensor_tensor(smb[:, g * 384:g * 384 + 128],
                                    smb[:, g * 384:g * 384 + 128],
                                    maskd[:], ALU.mult)
                    e.tensor_tensor(smb[:, g * 384 + 256:g * 384 + 384],
                                    smb[:, g * 384 + 256:g * 384 + 384],
                                    maskd[:], ALU.mult).then_inc(sem["ve"], 1)
                E("vector", f)
                inc(sem["ve"])
            gp_masks = cnt[sem["ve"]]
            ldtc = dtw[4]
            W("scalar", sem["ve"], gp_masks)
            E("scalar", lambda e, c=c: e.activation(
                out=ldtc[:], in_=dtrm[:, 2 * c * 128:(2 * c + 2) * 128],
                func=AF.Ln).then_inc(sem["se"], 1))
            s_ldt = inc(sem["se"])

            WO("vector", gB["tst_free"])
            def f(e, c=c):
                last = None
                for h in range(H):
                    last = e.tensor_scalar(tsl(h), srun[:, h * 64:h * 64 + 64],
                                           cdec[:, c * 128 + h:c * 128 + h + 1],
                                           None, ALU.mult)
                last.then_inc(sem["ve"], 1)
            E("vector", f)
            v_t = inc(sem["ve"])

            for h in range(0, H, 2):
                idx = c * H + h
                par = h % 4
                g = h >> 4
                hg = h >> 3
                hgidx = c * 16 + hg
                ypar = hg % 2
                ws = []
                if idx >= 3:
                    ws.append((sem["ve"], arg_done[idx - 3]))
                cum_src = cum_dram[c, h:h + 2, :].unsqueeze(0)\
                    .broadcast_to((128, 2, 256))
                cd = dma("sync",
                         dtraw[:, par * 256:par * 256 + 512].rearrange(
                             "p (two i) -> p two i", two=2),
                         cum_src, s="ld", waits=ws)
                cumb_done[h] = cd
                W("vector", sem["ld"], cumb_done[h])
                if idx == 0:
                    W("vector", sem["ve"], gp_masks)
                if idx >= 4:
                    W("vector", sem["se"], L_done[idx - 4])
                if idx >= 4:
                    W("vector", sem["te"], sw_te[idx - 4])
                def f(e, c=c, h=h, par=par):
                    for dh in range(2):
                        hh, pp = h + dh, par + dh
                        e.tensor_scalar(argb[pp][:, 0:256], cumb[pp][:, :],
                                        tcol(cumrm, (2 * c) * 128 + hh), 0.0,
                                        ALU.subtract, ALU.min)
                        e.tensor_scalar(argb[pp][:, 256:384],
                                        cumb[pp][:, 128:256],
                                        tcol(cumrm, (2 * c + 1) * 128 + hh),
                                        0.0, ALU.subtract, ALU.min)
                        last = e.tensor_scalar(diagb[pp], ident_bf[:],
                                               tcol(dbc, hh), None, ALU.mult)
                    last.then_inc(sem["ve"], 1)
                E("vector", f)
                arg_done[idx] = arg_done[idx + 1] = inc(sem["ve"])
                W("scalar", sem["ve"], arg_done[idx])
                if idx >= 4:
                    W("scalar", sem["te"], sw_te[idx - 4])
                def f(e, c=c, h=h, par=par):
                    for dh in range(2):
                        hh, pp = h + dh, par + dh
                        e.activation(out=Lb[pp][:, 0:256],
                                     in_=argb[pp][:, 0:256], func=AF.Exp,
                                     bias=tcol(ldtc, hh))
                        last = e.activation(out=Lb[pp][:, 256:384],
                                            in_=argb[pp][:, 256:384],
                                            func=AF.Exp,
                                            bias=tcol(ldtc, 128 + hh))
                    last.then_inc(sem["se"], 1)
                E("scalar", f)
                L_done[idx] = L_done[idx + 1] = inc(sem["se"])
                W("vector", sem["se"], L_done[idx])
                if idx >= 2:
                    W("vector", sem["te"], sw_te[idx - 2])
                def f(e, par=par, g=g):
                    e.tensor_tensor(swb[0][:, 0:384], Lb[par][:, 0:384],
                                    smb[:, g * 384:g * 384 + 384], ALU.mult)
                    e.tensor_tensor(swb[1][:, 0:384], Lb[par + 1][:, 0:384],
                                    smb[:, g * 384:g * 384 + 384], ALU.mult)\
                        .then_inc(sem["ve"], 1)
                E("vector", f)
                sw_v = inc(sem["ve"])
                ybank = [pband[2 + 2 * ypar], pband[3 + 2 * ypar]]
                ibank = [pband[6], pband[7]] if ypar == 0 else \
                        [pband[0], pband[1]]
                W("tensor", sem["ve"], sw_v)
                if h == 0:
                    W("tensor", sem["ve"], v_t)
                    W("tensor", sem["ld"], d_hs1)
                if h % 8 == 0:
                    if hgidx >= 2:
                        W("tensor", sem["se"], yevac[hgidx - 2])
                    if ypar == 0:
                        WO("tensor", gB["banks67_free"])
                def f(e, c=c, h=h, par=par, g=g, ybank=ybank, ibank=ibank):
                    for dh in range(2):
                        hh, pp = h + dh, par + dh
                        sl = slice(64 * (hh % 8), 64 * (hh % 8) + 64)
                        e.matmul(ybank[0][:, sl], swb[dh][:, 0:128],
                                 hsc[0][:, hh * 64:hh * 64 + 64],
                                 start=True, stop=False, skip_group_check=True)
                        e.matmul(ybank[0][:, sl], diagb[pp],
                                 hsc[0][:, hh * 64:hh * 64 + 64],
                                 start=False, stop=True, skip_group_check=True)
                        e.matmul(ybank[1][:, sl], swb[dh][:, 128:256],
                                 hsc[0][:, hh * 64:hh * 64 + 64],
                                 start=True, stop=False, skip_group_check=True)
                        e.matmul(ybank[1][:, sl], swb[dh][:, 256:384],
                                 hsc[1][:, hh * 64:hh * 64 + 64],
                                 start=False, stop=False,
                                 skip_group_check=True)
                        e.matmul(ybank[1][:, sl], diagb[pp],
                                 hsc[1][:, hh * 64:hh * 64 + 64],
                                 start=False, stop=True, skip_group_check=True)
                        e.matmul(ibank[0][:, sl],
                                 bcc[:, 2048 + g * 256:2048 + g * 256 + 128],
                                 tsl(hh), start=True, stop=True,
                                 skip_group_check=True)
                        last = e.matmul(
                            ibank[1][:, sl],
                            bcc[:, 2048 + g * 256 + 128:2048 + g * 256 + 256],
                            tsl(hh), start=True, stop=True,
                            skip_group_check=True)
                    last.then_inc(sem["te"], 1)
                E("tensor", f)
                sw_te[idx] = sw_te[idx + 1] = inc(sem["te"])
                if h % 8 == 6:
                    W("scalar", sem["te"], sw_te[idx])
                    if hg == 0:
                        WO("scalar", gB["yfull_free"])
                    if hgidx >= 2:
                        W("scalar", sem["ve"], yass[hgidx - 2])
                    def f(e, hg=hg, ybank=ybank, ibank=ibank, ypar=ypar):
                        e.copy(yfull[0][:, hg * 512:hg * 512 + 512],
                               ybank[0][:, :])
                        e.copy(yfull[1][:, hg * 512:hg * 512 + 512],
                               ybank[1][:, :])
                        e.copy(ib[0][ypar], ibank[0][:, :])
                        e.copy(ib[1][ypar], ibank[1][:, :])\
                            .then_inc(sem["se"], 1)
                    E("scalar", f)
                    yevac[hgidx] = inc(sem["se"])
                    W("vector", sem["se"], yevac[hgidx])
                    def f(e, c=c, hg=hg, ypar=ypar):
                        sl2 = slice(hg * 512, hg * 512 + 512)
                        for ih in range(2):
                            eci_src = ecirm[:, (2 * c + ih) * 128 + hg * 8:
                                            (2 * c + ih) * 128 + hg * 8 + 8]
                            e.tensor_copy(
                                ecix[:].rearrange("p (h q) -> p h q", h=8),
                                eci_src.unsqueeze(2).broadcast_to((128, 8, 64)))
                            e.drain()
                            e.tensor_mul(ib[ih][ypar], ib[ih][ypar], ecix[:])
                            e.drain()
                            e.tensor_add(yfull[ih][:, sl2], yfull[ih][:, sl2],
                                         ib[ih][ypar])
                        e.drain()
                        e.tensor_copy(ecix[:, 0:1], ecix[:, 0:1])\
                            .then_inc(sem["ve"], 1)
                    E("vector", f)
                    yass[hgidx] = inc(sem["ve"])

            W("vector", sem["te"], cnt[sem["te"]])
            def f(e):
                e.tensor_add(srun[:, 0:4096], xln[0][:], stc[:, 0:4096])
                e.tensor_add(srun[:, 4096:8192], xln[1][:],
                             stc[:, 4096:8192]).then_inc(sem["ve"], 1)
            E("vector", f)
            v_srun = inc(sem["ve"])
            d_g1 = dma("sync", gate_mh1, gate_sp[2 * c + 1], s="ld",
                       waits=[x for x in [(sem["ve"], v_srun),
                                          gB["gate1_free"]] if x])

            sgs = stats[:, 80:88]
            rst8 = stats[:, 88:96]
            gate_dma = {0: d_g0, 1: d_g1}
            gatebuf = {0: gate_mh0, 1: gate_mh1}
            gsp_guard = list(gB["gstag_free"])
            for mh in range(2):
                W("vector", sem["ve"], yass[c * 16 + 15])
                W("vector", sem["ld"], gate_dma[mh])
                def f(e, mh=mh):
                    e.tensor_mul(yfull[mh][:, :], yfull[mh][:, :],
                                 gatebuf[mh][:, :]).then_inc(sem["ve"], 1)
                E("vector", f)
                v_g1 = inc(sem["ve"])
                W("scalar", sem["ve"], v_g1)
                def f(e, mh=mh):
                    last = None
                    for g in range(G):
                        last = e.activation(
                            out=sqscrap[:, 0:1024],
                            in_=yfull[mh][:, g * 1024:(g + 1) * 1024],
                            func=AF.Square, accum_out=sgs[:, g:g + 1])
                        e.drain()
                    last.then_inc(sem["se"], 1)
                E("scalar", f)
                s_sq = inc(sem["se"])
                W("vector", sem["se"], s_sq)
                E("vector", lambda e: e.tensor_scalar(
                    sgs[:], sgs[:], 1.0 / 1024.0, EPS, ALU.mult, ALU.add)
                  .then_inc(sem["ve"], 1))
                v_v8 = inc(sem["ve"])
                W("scalar", sem["ve"], v_v8)
                def f(e):
                    e.activation(out=rst8[:], in_=sgs[:], func=AF.Ln)
                    e.drain()
                    e.activation(out=rst8[:], in_=rst8[:], func=AF.Exp,
                                 scale=-0.5).then_inc(sem["se"], 1)
                E("scalar", f)
                s_r8 = inc(sem["se"])
                W("vector", sem["se"], s_r8)
                def f(e, mh=mh):
                    last = None
                    for g in range(G):
                        last = e.tensor_scalar(
                            yfull[mh][:, g * 1024:(g + 1) * 1024],
                            yfull[mh][:, g * 1024:(g + 1) * 1024],
                            rst8[:, g:g + 1], None, ALU.mult)
                    last.then_inc(sem["ve"], 1)
                E("vector", f)
                v_gn = inc(sem["ve"])
                mb = 2 * c + mh
                for kb in range(16):
                    spar = kb % 2
                    if gsp_guard[spar] is not None:
                        WO("vector", gsp_guard[spar])
                    for k4 in range(4):
                        kt = kb * 4 + k4
                        def mk_pe(kt=kt, mh=mh):
                            def pe(e, slot):
                                return e.transpose(
                                    pband[6 + slot][:, :]
                                    .bitcast(BF16)[:, 0:128],
                                    in_=yfull[mh][:, kt * 128:kt * 128 + 128],
                                    identity=ident_bf[:])
                            return pe
                        def mk_ev(kt=kt, k4=k4, spar=spar):
                            def ev(e, slot):
                                return e.tensor_scalar(
                                    gstag[spar][:, k4 * 128:k4 * 128 + 128],
                                    pband[6 + slot][:, :]
                                    .bitcast(BF16)[:, 0:128],
                                    nwcols[:, kt:kt + 1], None, ALU.mult)
                            return ev
                        pw = [(sem["ve"], v_gn)] if (kt == 0 and mh == 0) \
                            else ()
                        if kt == 0:
                            pw = [(sem["ve"], v_gn)]
                        small_pe(mk_pe(), mk_ev(), pre_waits=pw)
                    v_last = cnt[sem["ve"]]
                    sp_v = dma("sync", g_sp[mb, :, kb * 4:kb * 4 + 4, :],
                               gstag[spar][:, :].rearrange(
                                   "p (k m) -> p k m", k=4),
                               s="sp", waits=[(sem["ve"], v_last)])
                    gsp_guard[spar] = (sem["sp"], sp_v)

            gB["hs_free"] = (sem["te"], cnt[sem["te"]])
            gB["yfull_free"] = (sem["te"], cnt[sem["te"]])
            gB["stc_free"] = (sem["ve"], cnt[sem["ve"]])
            gB["bcc_free"] = (sem["te"], cnt[sem["te"]])
            gB["smb_free"] = (sem["ve"], cnt[sem["ve"]])
            gB["tst_free"] = (sem["ve"], v_srun)
            gB["gate0_free"] = (sem["ve"], cnt[sem["ve"]])
            gB["gate1_free"] = (sem["ve"], cnt[sem["ve"]])
            gB["banks01_free"] = (sem["se"], cnt[sem["se"]])
            gB["banks67_free"] = (sem["ve"], cnt[sem["ve"]])
            gB["gstag_free"] = gsp_guard
        b_end_guards = dict(gB)
        ring_guard[0] = (sem["ve"], cnt[sem["ve"]])
        ring_guard[1] = (sem["ve"], cnt[sem["ve"]])

    if stage_ge("c"):
        gq = [xfm[:, 0:8192], xfm[:, 8192:16384]]
        wq = [xfm[:, 16384:20480], xfm[:, 20480:24576]]
        ffn_bf = xfm[:, 0:32768]
        aslot = [agbuf[:, 0:4096], agbuf[:, 4096:8192],
                 brm[:, 0:4096], brm[:, 4096:8192],
                 wbuf[0][:, 0:4096], wbuf[1][:, 0:4096],
                 xln[0][:, 0:4096], xln[1][:, 0:4096]]
        biasrow = acc[0:1, 0:4096]
        cscrap = acc[:, 4096:8192]
        pk2 = dtraw[:, :].bitcast(I8)
        pkb = [pk2[:, 0:512], pk2[:, 512:1024]]
        lo4b = pk2[:, 1024:1536]
        p_bf = cbuf[:, 0:512]
        lo4_bf = cbuf[:, 512:1024]
        wfs = [[cvacc[:, 0:512], cvacc[:, 512:1024]],
               [silout[:, 0:512], silout[:, 512:1024]]]
        x2ln_tmp = hstr[:, 0:512]

        bstart = b_end_guards if stage_ge("b") else {}

        d_bias = dma("gpsimd", biasrow[:], ffnb_in, s="dg")
        d_fs = dma("gpsimd", stats[0:1, 61:62], fscale_in, s="dg")
        W("tensor", sem["dg"], d_fs)
        WO("tensor", ring_guard[0]) if stage_ge("b") else None
        def f(e):
            e.matmul(pband[6][:, 0:1], onesrow[:], stats[0:1, 61:62],
                     start=True, stop=True, skip_group_check=True)\
                .then_inc(sem["te"], 1)
        E("tensor", f)
        t_fs = inc(sem["te"])
        W("vector", sem["te"], t_fs)
        E("vector", lambda e: e.tensor_copy(stats[:, 62:63], pband[6][:, 0:1])
          .then_inc(sem["ve"], 1))
        v_fs = inc(sem["ve"])
        scol = stats[:, 62:63]

        c1evac = {}
        gq_te = {}
        wq_te = {}
        for nblk in range(8):
            for kq in range(8):
                par = kq % 2
                key = nblk * 8 + kq
                ws = []
                if key >= 2:
                    ws.append((sem["te"], wq_te[key - 2]))
                elif stage_ge("b"):
                    ws.append(bstart["banks67_free"])
                d_w = dma("gpsimd", wq[par][:], woutt[nblk * 8 + kq], s="w",
                          waits=ws)
                gws = []
                if key >= 2:
                    gws.append((sem["te"], gq_te[key - 2]))
                for m in range(8):
                    d_g = dma("sync", gq[par][:, m * 1024:(m + 1) * 1024],
                              g_sp[m, :, kq * 8:kq * 8 + 8, :],
                              s="ld", waits=gws if m == 0 else ())
                W("tensor", sem["w"], d_w)
                W("tensor", sem["ld"], d_g)
                for m in range(8):
                    if kq == 0:
                        if nblk > 0:
                            W("tensor", sem["se"], c1evac[(nblk - 1, m)])
                        elif m == 0 and stage_ge("b"):
                            pass
                    def f(e, m=m, par=par, kq=kq):
                        last = None
                        for kt8 in range(8):
                            last = e.matmul(
                                pband[m][:, 0:512],
                                gq[par][:, m * 1024 + kt8 * 128:
                                        m * 1024 + kt8 * 128 + 128],
                                wq[par][:, kt8 * 512:kt8 * 512 + 512],
                                start=(kq == 0 and kt8 == 0),
                                stop=(kq == 7 and kt8 == 7),
                                skip_group_check=True)
                        last.then_inc(sem["te"], 1)
                    E("tensor", f)
                    t_m = inc(sem["te"])
                    if kq == 7:
                        W("scalar", sem["te"], t_m)
                        if nblk == 0 and stage_ge("b"):
                            WO("scalar", bstart["yfull_free"]) if m == 0 \
                                else None
                        def f(e, m=m, nblk=nblk):
                            e.copy(aslot[m][:, nblk * 512:nblk * 512 + 512],
                                   pband[m][:, 0:512]).then_inc(sem["se"], 1)
                        E("scalar", f)
                        c1evac[(nblk, m)] = inc(sem["se"])
                wq_te[key] = cnt[sem["te"]]
                gq_te[key] = cnt[sem["te"]]

        def emit_ln(src_ap, sc, prod_sem, prod_val):
            W("vector", prod_sem, prod_val)
            E("vector", lambda e, s=src_ap, sc=sc: e.reduce_sum(
                stats[:, sc:sc + 1], s, axis=AXX).then_inc(sem["ve"], 1))
            v1 = inc(sem["ve"])
            DR("vector")
            W("scalar", prod_sem, prod_val)
            E("scalar", lambda e, s=src_ap, sc=sc: e.activation(
                out=cscrap[:], in_=s, func=AF.Square,
                accum_out=stats[:, sc + 1:sc + 2]).then_inc(sem["se"], 1))
            s1 = inc(sem["se"])
            W("vector", sem["se"], s1)
            def f(e, sc=sc):
                e.tensor_scalar_mul(stats[:, sc + 2:sc + 3],
                                    stats[:, sc:sc + 1], 1.0 / HID)
                e.tensor_scalar_mul(stats[:, sc + 3:sc + 4],
                                    stats[:, sc + 1:sc + 2], 1.0 / HID)
                e.drain()
                e.tensor_mul(stats[:, sc + 4:sc + 5], stats[:, sc + 2:sc + 3],
                             stats[:, sc + 2:sc + 3])
                e.drain()
                e.tensor_sub(stats[:, sc + 5:sc + 6], stats[:, sc + 3:sc + 4],
                             stats[:, sc + 4:sc + 5])
                e.drain()
                e.tensor_scalar_add(stats[:, sc + 5:sc + 6],
                                    stats[:, sc + 5:sc + 6], EPS)\
                    .then_inc(sem["ve"], 1)
            E("vector", f)
            v2 = inc(sem["ve"])
            W("scalar", sem["ve"], v2)
            def f(e, sc=sc):
                e.activation(out=stats[:, sc + 6:sc + 7],
                             in_=stats[:, sc + 5:sc + 6], func=AF.Ln)
                e.drain()
                e.activation(out=stats[:, sc + 7:sc + 8],
                             in_=stats[:, sc + 6:sc + 7], func=AF.Exp,
                             scale=-0.5).then_inc(sem["se"], 1)
            E("scalar", f)
            s2 = inc(sem["se"])
            W("vector", sem["se"], s2)
            def f(e, sc=sc):
                e.tensor_mul(stats[:, sc + 6:sc + 7], stats[:, sc + 2:sc + 3],
                             stats[:, sc + 7:sc + 8])
                e.drain()
                e.tensor_scalar_mul(stats[:, sc + 6:sc + 7],
                                    stats[:, sc + 6:sc + 7], -1.0)\
                    .then_inc(sem["ve"], 1)
            E("vector", f)
            return inc(sem["ve"])

        c2_done = {}
        xrow_free = None
        for m in range(8):
            ln1 = emit_ln(aslot[m][:, :], 0, sem["se"], c1evac[(7, m)])
            dx = dma("sync", xrow[0][:], xs[3 + m * 128:3 + m * 128 + 128, :],
                     s="ld", waits=[x for x in [xrow_free] if x])
            W("vector", sem["ve"], ln1)
            W("vector", sem["ld"], dx)
            def f(e, m=m):
                e.scalar_tensor_tensor(
                    out=xrow[0][:], in0=aslot[m][:, :],
                    scalar=stats[:, 7:8], in1=xrow[0][:],
                    op0=ALU.mult, op1=ALU.add)
                e.drain()
                e.tensor_scalar(xrow[0][:], xrow[0][:], stats[:, 6:7], None,
                                ALU.add).then_inc(sem["ve"], 1)
            E("vector", f)
            v_x2 = inc(sem["ve"])
            d_sp = dma("sync", x2_sp[m], xrow[0][:], s="sp",
                       waits=[(sem["ve"], v_x2)])
            ln2 = emit_ln(xrow[0][:], 8, sem["ve"], v_x2)
            W("vector", sem["ve"], ln2)
            def f(e, m=m):
                e.tensor_scalar(aslot[m][:, :], xrow[0][:], stats[:, 15:16],
                                stats[:, 14:15], ALU.mult, ALU.add)\
                    .then_inc(sem["ve"], 1)
            E("vector", f)
            v_x2ln = inc(sem["ve"])
            xrow_free = (sem["ve"], v_x2ln)
            prev_ev = None
            for ks in range(16):
                W("tensor", sem["ve"], v_x2ln if ks == 0 else 0)
                if prev_ev is not None:
                    W("tensor", *prev_ev)
                def f(e, m=m, ks=ks):
                    e.transpose(pband[6][:, :].bitcast(BF16)[:, 0:128],
                                in_=aslot[m][:, ks * 256:ks * 256 + 256:2],
                                identity=ident_bf[:])
                    e.transpose(pband[7][:, :].bitcast(BF16)[:, 0:128],
                                in_=aslot[m][:, ks * 256 + 1:ks * 256 + 256:2],
                                identity=ident_bf[:]).then_inc(sem["te"], 1)
                E("tensor", f)
                t_tr = inc(sem["te"])
                W("vector", sem["te"], t_tr)
                W("vector", sem["ve"], v_fs)
                def f(e, m=m, ks=ks):
                    e.tensor_scalar(
                        aslot[m][:, ks * 256:ks * 256 + 128],
                        pband[6][:, :].bitcast(BF16)[:, 0:128],
                        scol, None, ALU.mult)
                    e.tensor_scalar(
                        aslot[m][:, ks * 256 + 128:ks * 256 + 256],
                        pband[7][:, :].bitcast(BF16)[:, 0:128],
                        scol, None, ALU.mult).then_inc(sem["ve"], 1)
                E("vector", f)
                prev_ev = (sem["ve"], inc(sem["ve"]))
            c2_done[m] = prev_ev[1]

        c3evac = {}
        unp_v = {}
        for nblk in range(8):
            for ks in range(16):
                par = ks % 2
                key = nblk * 16 + ks
                ws = []
                if key >= 2:
                    ws.append((sem["ve"], unp_v[key - 2]))
                d_pk = dma("sync", pkb[par],
                           pkT[ks * 128:ks * 128 + 128,
                               nblk * 512:nblk * 512 + 512],
                           s="ld", waits=ws)
                W("vector", sem["ld"], d_pk)
                if key >= 2:
                    W("vector", sem["te"], unp_te[key - 2])
                if key == 0:
                    for m in range(8):
                        W("vector", sem["ve"], c2_done[m])
                def f(e, par=par):
                    e.tensor_copy(p_bf[:], pkb[par])
                    e.tensor_single_scalar(lo4b[:].bitcast(I32),
                                           pkb[par].bitcast(I32),
                                           0x0F0F0F0F, ALU.bitwise_and)
                    e.drain()
                    e.tensor_copy(lo4_bf[:], lo4b[:])
                    e.drain()
                    e.tensor_sub(wfs[par][0][:], p_bf[:], lo4_bf[:])
                    e.tensor_single_scalar(p_bf[:], lo4_bf[:], 7.5, ALU.is_gt)
                    e.drain()
                    e.tensor_scalar_mul(wfs[par][0][:], wfs[par][0][:],
                                        1.0 / 16.0)
                    e.scalar_tensor_tensor(
                        out=wfs[par][1][:], in0=p_bf[:], scalar=-16.0,
                        in1=lo4_bf[:], op0=ALU.mult, op1=ALU.add)\
                        .then_inc(sem["ve"], 1)
                E("vector", f)
                unp_v[key] = inc(sem["ve"])
                W("tensor", sem["ve"], unp_v[key])
                for m in range(8):
                    if ks == 0:
                        if nblk > 0:
                            W("tensor", sem["se"], c3evac[(nblk - 1, m)])
                    def f(e, m=m, par=par, ks=ks, nblk=nblk):
                        e.matmul(pband[m][:, 0:512],
                                 aslot[m][:, (2 * ks) * 128:(2 * ks) * 128 + 128],
                                 wfs[par][0][:],
                                 start=(ks == 0), stop=False,
                                 skip_group_check=True)
                        i = e.matmul(pband[m][:, 0:512],
                                     aslot[m][:, (2 * ks + 1) * 128:
                                              (2 * ks + 1) * 128 + 128],
                                     wfs[par][1][:],
                                     start=False, stop=(ks == 15),
                                     skip_group_check=True)
                        if ks == 0:
                            i = e.matmul(pband[m][:, 0:512], onesbf[:],
                                         biasrow[0:1, nblk * 512:
                                                 nblk * 512 + 512],
                                         start=False, stop=False,
                                         skip_group_check=True)
                        i.then_inc(sem["te"], 1)
                    E("tensor", f)
                    t_m = inc(sem["te"])
                    if ks == 15:
                        W("scalar", sem["te"], t_m)
                        if nblk == 0 and m == 0:
                            pass
                        def f(e, m=m, nblk=nblk):
                            e.copy(ffn_bf[:, m * 4096 + nblk * 512:
                                          m * 4096 + nblk * 512 + 512],
                                   pband[m][:, 0:512]).then_inc(sem["se"], 1)
                        E("scalar", f)
                        c3evac[(nblk, m)] = inc(sem["se"])
                unp_te = unp_te if key > 0 else {}
                unp_te[key] = cnt[sem["te"]]

        x2back = None
        for m in range(8):
            fslice = ffn_bf[:, m * 4096:(m + 1) * 4096]
            ln3 = emit_ln(fslice, 16, sem["se"], c3evac[(7, m)])
            dx2 = dma("sync", xrow[0][:], x2_sp[m], s="ld",
                      waits=[x for x in [x2back] if x])
            W("vector", sem["ve"], ln3)
            W("vector", sem["ld"], dx2)
            def f(e, m=m):
                e.scalar_tensor_tensor(
                    out=xrow[0][:], in0=ffn_bf[:, m * 4096:(m + 1) * 4096],
                    scalar=stats[:, 23:24], in1=xrow[0][:],
                    op0=ALU.mult, op1=ALU.add)
                e.drain()
                e.tensor_scalar(xrow[0][:], xrow[0][:], stats[:, 22:23], None,
                                ALU.add).then_inc(sem["ve"], 1)
            E("vector", f)
            v_out = inc(sem["ve"])
            d_out = dma("sync", out[m * 128:(m + 1) * 128, :], xrow[0][:],
                        s="out", waits=[(sem["ve"], v_out)])
            x2back = (sem["out"], d_out)

    dump_waits = [(sem["ve"], cnt[sem["ve"]]), (sem["se"], cnt[sem["se"]])]
    for name, ap in dbgs.items():
        key = name[4:]
        src = {"srun": srun, "dtdrm": dtdrm, "ecirm": ecirm,
               "dtrm": dtrm, "cumrm": cumrm,
               "cdec": cdec, "brm": brm, "dtraw": dtraw, "stats": stats,
               "cols": cols, "xfm": xfm, "wbuf0": wbuf[0], "xln0": xln[0]}.get(key)
        if src is None:
            continue
        ncols = min(src.shape[1], INTER)
        dma("gpsimd", ap[:, 0:ncols], src[:, 0:ncols], s="dg",
            waits=dump_waits)
        dump_waits = []

    if not stage_ge("c"):
        W("vector", sem["ve"], cnt[sem["ve"]])
        W("vector", sem["dg"], cnt[sem["dg"]])
        E("vector", lambda e: e.memset(dtraw[:], 0.0).then_inc(sem["ve"], 1))
        vz = inc(sem["ve"])
        for mt in range(8):
            for q in range(4):
                dma("sync", out[mt * 128:(mt + 1) * 128,
                                q * 1024:(q + 1) * 1024],
                    dtraw[:], s="out", waits=[(sem["ve"], vz)] if q == 0 else ())
    W("sync", sem["out"], cnt[sem["out"]])
    W("sync", sem["sp"], cnt[sem["sp"]])
    W("sync", sem["dg"], cnt[sem["dg"]])
    W("sync", sem["cc"], cc_done)

    class _MockInstr:
        def __init__(self, rec):
            self.rec = rec
        def then_inc(self, s, n=1):
            self.rec[s] = self.rec.get(s, 0) + n
            return self

    class _MockEng:
        def __init__(self, rec):
            self._rec = rec
        def __getattr__(self, name):
            rec = self._rec
            def fn(*a, **k):
                return _MockInstr(rec)
            return fn

    mock_cnt = {}
    for engname in ENGS:
        me = _MockEng(mock_cnt)
        for fop in ops[engname]:
            fop(me)
    for s, v in cnt.items():
        mv = mock_cnt.get(s, 0)
        assert mv == v, f"sem accounting mismatch for {s.name}: planned {v} actual {mv}"
    for s, v in mock_cnt.items():
        assert cnt.get(s, 0) == v, f"unplanned incs on {s.name}: {v}"

    with nc.Block() as block:
        @block.sync
        def _(e):
            for f in ops["sync"]:
                f(e)

        @block.scalar
        def _(e):
            for f in ops["scalar"]:
                f(e)

        @block.vector
        def _(e):
            for f in ops["vector"]:
                f(e)

        @block.tensor
        def _(e):
            for f in ops["tensor"]:
                f(e)

        @block.gpsimd
        def _(e):
            for f in ops["gpsimd"]:
                f(e)

    es.close()
    return nc


def _prep(inputs):
    x = np.asarray(inputs["L0_attn_input_states"], dtype=np.float32)
    win = np.asarray(inputs["L0_attn_in_proj_w"], np.float32)
    winfm = np.ascontiguousarray(
        win[INTER:, :].reshape(81, 128, 32, 128).transpose(0, 3, 2, 1)
        .reshape(81, 128, 32 * 128))
    wingt = np.ascontiguousarray(
        win[:INTER, :].reshape(16, 512, 32, 128).transpose(0, 2, 3, 1)
        .reshape(16, 4, 8, 128, 512).transpose(0, 1, 3, 2, 4)
        .reshape(64, 128, 8 * 512))
    wout = np.asarray(inputs["L0_attn_out_w"], np.float32)
    woutt = np.ascontiguousarray(
        wout.reshape(8, 512, 64, 128).transpose(0, 2, 3, 1)
        .reshape(8, 8, 8, 128, 512).transpose(0, 1, 3, 2, 4)
        .reshape(64, 128, 8 * 512))
    pkT = np.ascontiguousarray(
        np.asarray(inputs["L0_ffn_weight_int4"], np.int8).T)
    convw = np.ascontiguousarray(
        np.asarray(inputs["L0_attn_conv_w"], np.float32).reshape(80, 128, 4)
        .transpose(1, 0, 2).reshape(128, 320))
    convb = np.ascontiguousarray(
        np.asarray(inputs["L0_attn_conv_b"], np.float32).reshape(80, 128).T)
    nw = np.ascontiguousarray(
        np.asarray(inputs["L0_attn_norm_w"], np.float32).reshape(64, 128))
    ffnb = np.asarray(inputs["L0_ffn_bias"], np.float16).reshape(1, HID)
    fscale = np.asarray(inputs["L0_ffn_weight_scale"], np.float16).reshape(1, 1)
    identbf = np.eye(128, dtype=ml_dtypes.bfloat16)
    identf = np.eye(128, dtype=np.float32)
    jj, ii = np.meshgrid(np.arange(128), np.arange(128), indexing="ij")
    maskd = (ii >= jj).astype(ml_dtypes.bfloat16)

    per_core = []
    for core in range(8):
        b, cl = core // 4, core % 4
        xb = x[b]
        r0 = cl * 1024
        halo = np.zeros((3, HID), np.float32) if cl == 0 else xb[r0 - 3:r0]
        xsh = np.ascontiguousarray(np.concatenate([halo, xb[r0:r0 + 1024]], 0))
        fm = np.zeros((128, 6), np.float32)
        for d in range(3):
            m = 1.0 if d < cl else 0.0
            fm[:, d] = m
            fm[:, 3 + d] = 1.0 - m
        per_core.append(dict(
            xs=xsh, winfm=winfm, wingt=wingt, woutt=woutt, pkT=pkT,
            convw=convw, convb=convb,
            dtb=np.asarray(inputs["L0_attn_dt_bias"], np.float32),
            alog=np.asarray(inputs["L0_attn_A_log"], np.float32),
            dvec=np.asarray(inputs["L0_attn_D"], np.float32).reshape(1, H),
            nw=nw, ffnb=ffnb, fscale=fscale, identbf=identbf, identf=identf,
            maskd=maskd, fmask=fm))
    return per_core


def kernel(**inputs):
    per_core = _prep(inputs)
    nc = build()
    res = run_bass_kernel_spmd(nc, per_core, core_ids=list(range(8)))
    outs = [res.results[i]["out"] for i in range(8)]
    b0 = np.concatenate(outs[0:4], axis=0)
    b1 = np.concatenate(outs[4:8], axis=0)
    return np.stack([b0, b1], axis=0)


# version 11
# speedup vs baseline: 1.0234x; 1.0069x over previous
import contextlib
from collections import defaultdict

import numpy as np
import ml_dtypes

import concourse.bass as bass
import concourse.mybir as mybir
from concourse.bass_utils import run_bass_kernel_spmd

F32 = mybir.dt.float32
BF16 = mybir.dt.bfloat16
F16 = mybir.dt.float16
I8 = mybir.dt.int8
I32 = mybir.dt.int32
AF = mybir.ActivationFunctionType
ALU = mybir.AluOpType
AXX = mybir.AxisListType.X

HID = 4096
PROJ = 18560
INTER = 8192
H = 128
PD = 64
G = 8
CS = 256
ROWS = 1024
LIN = 1027
KT = 32
NCH = 4
EPS = 1e-5
FM_SPLITS = ((0, 342), (342, 342), (684, 343))
ENGS = ("sync", "scalar", "vector", "tensor", "gpsimd")


def build(debug=(), upto="c", no_states=False, ag_late=True):
    nc = bass.Bass("TRN2", target_bir_lowering=False)

    xs = nc.dram_tensor("xs", [LIN, HID], F32, kind="ExternalInput").ap()
    winfm = nc.dram_tensor("winfm", [81, 128, KT * 128], F32, kind="ExternalInput").ap()
    wingt = nc.dram_tensor("wingt", [64, 128, 8 * 512], F32, kind="ExternalInput").ap()
    woutt = nc.dram_tensor("woutt", [64, 128, 8 * 512], F32, kind="ExternalInput").ap()
    pkT = nc.dram_tensor("pkT", [HID // 2, HID], I8, kind="ExternalInput").ap()
    convw_in = nc.dram_tensor("convw", [128, 320], F32, kind="ExternalInput").ap()
    convb_in = nc.dram_tensor("convb", [128, 80], F32, kind="ExternalInput").ap()
    dtb_in = nc.dram_tensor("dtb", [H], F32, kind="ExternalInput").ap()
    alog_in = nc.dram_tensor("alog", [H], F32, kind="ExternalInput").ap()
    dvec_in = nc.dram_tensor("dvec", [1, H], F32, kind="ExternalInput").ap()
    nw_in = nc.dram_tensor("nw", [64, 128], F32, kind="ExternalInput").ap()
    ffnb_in = nc.dram_tensor("ffnb", [1, HID], F16, kind="ExternalInput").ap()
    fscale_in = nc.dram_tensor("fscale", [1, 1], F16, kind="ExternalInput").ap()
    identbf_in = nc.dram_tensor("identbf", [128, 128], BF16, kind="ExternalInput").ap()
    identf_in = nc.dram_tensor("identf", [128, 128], F32, kind="ExternalInput").ap()
    maskd_in = nc.dram_tensor("maskd", [128, 128], BF16, kind="ExternalInput").ap()
    fmask_in = nc.dram_tensor("fmask", [128, 6], F32, kind="ExternalInput").ap()

    out = nc.dram_tensor("out", [ROWS, HID], F32, kind="ExternalOutput").ap()

    def scratch(name, shape, dtype):
        kind = "ExternalOutput" if name in debug else "Internal"
        return nc.dram_tensor(name, shape, dtype, kind=kind).ap()

    hs_sp = scratch("hs_sp", [8, 128, 64, 128], BF16)
    gate_sp = scratch("gate_sp", [8, 128, INTER], BF16)
    st_sp = scratch("st_sp", [NCH * 128, INTER], BF16)
    bc_sp = scratch("bc_sp", [NCH, 128, 4096], BF16)
    g_sp = scratch("g_sp", [8, 128, 64, 128], BF16)
    x2_sp = scratch("x2_sp", [8, 128, HID], F32)
    cum_dram = scratch("cum_dram", [NCH, 128, 256], F32)
    ag_in = scratch("ag_in", [128, INTER + 128], BF16)
    ag_out = scratch("ag_out", [4 * 128, INTER + 128], BF16)
    dbgs = {}
    for name in debug:
        if name.startswith("dbg_"):
            dbgs[name] = scratch(name, [128, INTER], F32)

    ops = {e: [] for e in ENGS}
    cnt = defaultdict(int)

    def E(eng, fn):
        ops[eng].append(fn)

    def inc(sem_h, n=1):
        cnt[sem_h] += n
        return cnt[sem_h]

    def DR(eng):
        E(eng, lambda e: e.drain())

    def W(eng, sem_h, val):
        assert val <= cnt[sem_h], f"wait on future value {val} > {cnt[sem_h]}"
        if val > 0:
            E(eng, lambda e, s=sem_h, v=val: e.wait_ge(s, v))

    es = contextlib.ExitStack()
    sem = {k: es.enter_context(nc.semaphore(f"s_{k}"))
           for k in ("ld", "w", "sp", "te", "ve", "se", "gp", "cc", "out", "dg")}
    MARK = {"tensor": "te", "vector": "ve", "scalar": "se", "gpsimd": "gp",
            "sync": "ld"}

    def dma(eng, out_ap, in_ap, s="ld", waits=()):
        v = inc(sem[s], 16)
        def f(e, o=out_ap, i=in_ap, s2=sem[s], ws=tuple(waits)):
            for (sh, vv) in ws:
                e.wait_ge(sh, vv)
            e.dma_start(out=o, in_=i).then_inc(s2, 16)
        E(eng, f)
        return v

    sb = {}

    def alloc(name, shape, dtype):
        t = es.enter_context(nc.sbuf_tensor("sb_" + name, shape, dtype))
        sb[name] = t
        return t

    ident_bf = alloc("ident_bf", [128, 128], BF16)
    ident_f = alloc("ident_f", [128, 128], F32)
    maskd = alloc("maskd", [128, 128], BF16)
    fmask = alloc("fmask", [128, 6], F32)
    convw_sb = alloc("convw_sb", [128, 320], F32)
    convb_sb = alloc("convb_sb", [128, 80], F32)
    cols = alloc("cols", [128, 8], F32)
    dbc = alloc("dbc", [128, 128], F32)
    nwcols = alloc("nwcols", [128, 64], F32)
    onesrow = alloc("onesrow", [1, 128], F32)
    ones256 = alloc("ones256", [128, 256], F32)
    drow = alloc("drow", [1, 128], F32)
    clrow = alloc("clrow", [1, 128], F32)
    nw64 = alloc("nw64", [64, 128], F32)
    dtdrm = alloc("dtdrm", [128, 1024], F32)
    ecirm = alloc("ecirm", [128, 1024], F32)
    dtrm = alloc("dtrm", [128, 1024], F32)
    cumrm = alloc("cumrm", [128, 1024], F32)
    cdec = alloc("cdec", [128, 512], F32)
    acc = alloc("acc", [128, INTER], BF16)
    pbc = alloc("pbc", [128, 128], F32)
    stats = alloc("stats", [128, 96], F32)

    xfm = alloc("xfm", [128, KT * LIN], BF16)
    agbuf = alloc("agbuf", [128, INTER], BF16)
    sqscrap = agbuf
    xrow0 = alloc("xrow0", [128, HID], F32)
    xrow = [xrow0, xrow0]
    xln = [alloc(f"xln{i}", [128, HID], BF16) for i in range(2)]
    wbuf = [alloc(f"wbuf{i}", [128, KT * 128], BF16) for i in range(2)]
    cbuf = alloc("cbuf", [128, LIN], BF16)
    cvacc = alloc("cvacc", [128, 1024], BF16)
    silout = alloc("silout", [128, 1024], BF16)
    brm = alloc("brm", [128, 8192], BF16)
    dtraw = alloc("dtraw", [128, 1024], F32)
    dtw = [alloc(f"dtw{i}", [128, 256], F32) for i in range(8)]
    hstr = alloc("hstr", [128, 1024], BF16)
    ststg = alloc("ststg", [128, 512], BF16)
    pstg = alloc("pstg", [128, 128], BF16)
    xwb = alloc("xwb", [128, 256], BF16)
    wgq = wbuf
    cbufs = [cbuf, xln[0][:, 0:1027]]
    cvaccs = [cvacc, xln[0][:, 1027:2051]]
    silouts = [silout, xln[0][:, 2051:3075]]
    gstage = [cbuf[:, 0:512], cbuf[:, 512:1024]]
    agp = alloc("agp", [128, 128], BF16)
    onesbf = alloc("onesbf", [1, 128], BF16)

    pband = [es.enter_context(nc.psum_tensor(f"pb{i}", [128, 512], F32))
             for i in range(8)]

    ring_guard = [None, None]
    ring_state = {"i": 0}

    def small_pe(pe_fn, ev_fn, evac_eng="vector", pre_waits=()):
        slot = ring_state["i"] % 2
        ring_state["i"] += 1
        for (sh, v) in pre_waits:
            W("tensor", sh, v)
        g = ring_guard[slot]
        if g is not None:
            W("tensor", g[0], g[1])
        def f(e, fn=pe_fn, s=slot):
            fn(e, s).then_inc(sem["te"], 1)
        E("tensor", f)
        tv = inc(sem["te"])
        W(evac_eng, sem["te"], tv)
        m = sem[MARK[evac_eng]]
        def f2(e, fn=ev_fn, s=slot, m=m):
            fn(e, s).then_inc(m, 1)
        E(evac_eng, f2)
        ev = inc(m)
        ring_guard[slot] = (m, ev)
        return ev

    dma("sync", ident_bf[:], identbf_in)
    dma("sync", ident_f[:], identf_in)
    dma("sync", maskd[:], maskd_in)
    dma("sync", fmask[:], fmask_in)
    dma("sync", convw_sb[:], convw_in)
    dma("sync", convb_sb[:], convb_in)
    dma("sync", cols[:, 0:1], dtb_in.rearrange("(p o) -> p o", o=1))
    dma("sync", cols[:, 2:3], alog_in.rearrange("(p o) -> p o", o=1))
    dma("sync", drow[:], dvec_in)
    v_consts = dma("sync", nw64[:], nw_in)

    W("vector", sem["ld"], v_consts)
    def f(e):
        e.memset(onesrow[:], 1.0)
        e.memset(onesbf[:], 1.0)
        e.memset(ones256[:], 1.0)
        e.memset(acc[:], 0.0)
        e.memset(pbc[:], 1.0).then_inc(sem["ve"], 1)
    E("vector", f)
    v_memset = inc(sem["ve"])

    W("scalar", sem["ld"], v_consts)
    E("scalar", lambda e: e.activation(out=cols[:, 1:2], in_=cols[:, 2:3],
                                       func=AF.Exp).then_inc(sem["se"], 1))
    s_A = inc(sem["se"])
    W("vector", sem["se"], s_A)
    E("vector", lambda e: e.tensor_scalar_mul(cols[:, 1:2], cols[:, 1:2], -1.0)
      .then_inc(sem["ve"], 1))
    v_A = inc(sem["ve"])

    W("tensor", sem["ld"], v_consts)
    W("tensor", sem["ve"], v_memset)
    def f(e):
        e.matmul(pband[6][:, 0:128], onesrow[:], drow[:], start=True, stop=True)
        e.transpose(pband[7][:, 0:64], in_=nw64[:], identity=ident_f[0:64, 0:64])\
            .then_inc(sem["te"], 1)
    E("tensor", f)
    t_const = inc(sem["te"])
    W("vector", sem["te"], t_const)
    def f(e):
        e.tensor_copy(dbc[:], pband[6][:, 0:128])
        e.tensor_copy(nwcols[:], pband[7][:, 0:64]).then_inc(sem["ve"], 1)
    E("vector", f)
    v_const2 = inc(sem["ve"])
    ring_guard[0] = (sem["ve"], v_const2)
    ring_guard[1] = (sem["ve"], v_const2)

    ln_done = {}
    tr_done = {}
    for rt in range(9):
        par = rt % 2
        sc = 8 * rt
        r0 = rt * 128
        nr = 128 if rt < 8 else 3
        dwaits = []
        if rt >= 1:
            dwaits.append((sem["ve"], ln_done[rt - 1]))
        if rt == 8:
            E("vector", lambda e, b=xrow[par]: e.memset(b[:], 1.0)
              .then_inc(sem["ve"], 1))
            dwaits = [(sem["ve"], inc(sem["ve"]))]
        vx = dma("sync", xrow[par][0:nr, :], xs[r0:r0 + nr, :], waits=dwaits)

        W("vector", sem["ld"], vx)
        E("vector", lambda e, b=xrow[par], sc=sc: e.reduce_sum(
            stats[:, sc + 0:sc + 1], b[:], axis=AXX).then_inc(sem["ve"], 1))
        v1 = inc(sem["ve"])
        DR("vector")
        W("scalar", sem["ld"], vx)
        E("scalar", lambda e, b=xrow[par], sc=sc: e.activation(
            out=sqscrap[:, 0:HID], in_=b[:], func=AF.Square,
            accum_out=stats[:, sc + 1:sc + 2]).then_inc(sem["se"], 1))
        s1 = inc(sem["se"])
        W("vector", sem["se"], s1)
        def f(e, sc=sc):
            e.tensor_scalar_mul(stats[:, sc + 2:sc + 3], stats[:, sc + 0:sc + 1],
                                1.0 / HID)
            e.tensor_scalar_mul(stats[:, sc + 3:sc + 4], stats[:, sc + 1:sc + 2],
                                1.0 / HID)
            e.drain()
            e.tensor_mul(stats[:, sc + 4:sc + 5], stats[:, sc + 2:sc + 3],
                         stats[:, sc + 2:sc + 3])
            e.drain()
            e.tensor_sub(stats[:, sc + 5:sc + 6], stats[:, sc + 3:sc + 4],
                         stats[:, sc + 4:sc + 5])
            e.drain()
            e.tensor_scalar_add(stats[:, sc + 5:sc + 6], stats[:, sc + 5:sc + 6],
                                EPS).then_inc(sem["ve"], 1)
        E("vector", f)
        v2 = inc(sem["ve"])
        W("scalar", sem["ve"], v2)
        def f(e, sc=sc):
            e.activation(out=stats[:, sc + 6:sc + 7], in_=stats[:, sc + 5:sc + 6],
                         func=AF.Ln)
            e.drain()
            e.activation(out=stats[:, sc + 7:sc + 8], in_=stats[:, sc + 6:sc + 7],
                         func=AF.Exp, scale=-0.5).then_inc(sem["se"], 1)
        E("scalar", f)
        s2 = inc(sem["se"])
        W("vector", sem["se"], s2)
        if rt >= 2:
            W("vector", sem["te"], tr_done[rt - 2])
        def f(e, sc=sc, par=par):
            e.tensor_mul(stats[:, sc + 6:sc + 7], stats[:, sc + 2:sc + 3],
                         stats[:, sc + 7:sc + 8])
            e.drain()
            e.tensor_scalar_mul(stats[:, sc + 6:sc + 7], stats[:, sc + 6:sc + 7],
                                -1.0)
            e.drain()
            e.tensor_scalar(xln[par][:], xrow[par][:], stats[:, sc + 7:sc + 8],
                            stats[:, sc + 6:sc + 7], ALU.mult, ALU.add)\
                .then_inc(sem["ve"], 1)
        E("vector", f)
        ln_done[rt] = inc(sem["ve"])

        for kt in range(KT):
            def mk_pe(kt=kt, par=par):
                def pe(e, slot):
                    return e.transpose(
                        pband[6 + slot][:, :].bitcast(BF16)[:, 0:128],
                        in_=xln[par][:, kt * 128:(kt + 1) * 128],
                        identity=ident_bf[:])
                return pe
            def mk_ev(kt=kt, rt=rt, nr=nr):
                def ev(e, slot):
                    return e.tensor_copy(
                        xfm[:, kt * LIN + rt * 128: kt * LIN + rt * 128 + nr],
                        pband[6 + slot][:, :].bitcast(BF16)[:, 0:nr])
                return ev
            small_pe(mk_pe(), mk_ev(),
                     pre_waits=[(sem["ve"], ln_done[rt])] if kt == 0 else ())
        tr_done[rt] = cnt[sem["te"]]
    v_xfm = cnt[sem["ve"]]

    STAGES = ["a1", "a2dt", "a2bc", "a2", "a3", "a4", "a5", "b", "c"]
    def stage_ge(s):
        return STAGES.index(upto) >= STAGES.index(s)
    nt_order = []
    if stage_ge("a2dt"):
        nt_order = [80]
    if stage_ge("a2bc"):
        nt_order += list(range(64, 80))
    if stage_ge("a2"):
        nt_order += list(range(64))
    te_after_nt = {}
    fm_evac_done = {}
    guard_cbuf = {}
    guard_sil = {0: [], 1: []}
    guard_hstr = []
    guard_xwb = []
    guard_ststg = []
    fold_v = 0

    def emit_post(nt, par, sd):
        nonlocal guard_hstr, guard_xwb, guard_ststg, fold_v
        silo = silouts[par]
        if 64 <= nt < 80:
            isB = nt < 72
            g2 = nt - 64 if isB else nt - 72
            for c in range(NCH):
                col = g2 * 256 if isB else 2048 + g2 * 256
                v = dma("sync", bc_sp[c, :, col:col + 256],
                        silo[:, c * 256:(c + 1) * 256], s="sp",
                        waits=[(sem["se"], sd)])
            guard_sil[par].append((sem["sp"], v))
            if isB:
                for c in range(NCH):
                    for jh in range(2):
                        def mk_pe(c=c, jh=jh, silo=silo):
                            def pe(e, slot):
                                return e.transpose(
                                    pband[6 + slot][:, :].bitcast(BF16)[:, 0:128],
                                    in_=silo[:, c * 256 + jh * 128:
                                             c * 256 + jh * 128 + 128],
                                    identity=ident_bf[:])
                            return pe
                        def mk_ev(c=c, jh=jh, g2=g2):
                            def ev(e, slot):
                                col = ((g2 * 4 + c) * 2 + jh) * 128
                                return e.tensor_copy(
                                    brm[:, col:col + 128],
                                    pband[6 + slot][:, :].bitcast(BF16)[:, 0:128])
                            return ev
                        small_pe(mk_pe(), mk_ev(), pre_waits=[(sem["se"], sd)])
                guard_sil[par].append((sem["te"], cnt[sem["te"]]))
            return
        h0 = 2 * nt
        g2 = h0 >> 4
        ev_blocks = {}
        first = True
        for c in range(NCH):
            for mh in range(2):
                def mk_pe(c=c, mh=mh, silo=silo):
                    def pe(e, slot):
                        return e.transpose(
                            pband[6 + slot][:, :].bitcast(BF16)[:, 0:128],
                            in_=silo[:, c * 256 + mh * 128:
                                     c * 256 + mh * 128 + 128],
                            identity=ident_bf[:])
                    return pe
                def mk_ev(c=c, mh=mh):
                    def ev(e, slot):
                        col = (2 * c + mh) * 128
                        return e.tensor_copy(
                            hstr[:, col:col + 128],
                            pband[6 + slot][:, :].bitcast(BF16)[:, 0:128])
                    return ev
                pw = [(sem["se"], sd)]
                if first:
                    for gg in guard_hstr:
                        W("vector", *gg)
                    guard_hstr = []
                    first = False
                ev_blocks[(c, mh)] = small_pe(mk_pe(), mk_ev(), pre_waits=pw)
        guard_sil[par].append((sem["te"], cnt[sem["te"]]))
        for c in range(NCH):
            for mh in range(2):
                rb = 2 * c + mh
                v = dma("sync", hs_sp[rb, :, nt, :],
                        hstr[:, rb * 128:rb * 128 + 128], s="sp",
                        waits=[(sem["ve"], ev_blocks[(c, mh)])])
        guard_hstr.append((sem["sp"], v))
        for c in range(NCH if not no_states else 0):
            W("vector", sem["ve"], ev_blocks[(c, 1)])
            for gg in guard_xwb:
                W("vector", *gg)
            guard_xwb = []
            def f(e, c=c, h0=h0):
                last = None
                for jh in range(2):
                    for hh in range(2):
                        h = h0 + hh
                        last = e.tensor_scalar(
                            xwb[:, (jh * 2 + hh) * 64:(jh * 2 + hh) * 64 + 64],
                            hstr[:, (2 * c + jh) * 128 + hh * 64:
                                 (2 * c + jh) * 128 + hh * 64 + 64],
                            dtdrm[:, (2 * c + jh) * 128 + h:
                                  (2 * c + jh) * 128 + h + 1],
                            None, ALU.mult)
                last.then_inc(sem["ve"], 1)
            E("vector", f)
            vxw = inc(sem["ve"])
            for hh in range(2):
                h = h0 + hh
                def mk_pe(c=c, hh=hh, g2=g2):
                    def pe(e, slot):
                        last = None
                        for jh in range(2):
                            col = ((g2 * 4 + c) * 2 + jh) * 128
                            last = e.matmul(
                                pband[6 + slot][:, 384:448],
                                brm[:, col:col + 128],
                                xwb[:, (jh * 2 + hh) * 64:
                                    (jh * 2 + hh) * 64 + 64],
                                start=(jh == 0), stop=(jh == 1),
                                skip_group_check=True)
                        return last
                    return pe
                def mk_ev(c=c, hh=hh):
                    def ev(e, slot):
                        return e.tensor_copy(
                            ststg[:, c * 128 + hh * 64:c * 128 + hh * 64 + 64],
                            pband[6 + slot][:, 384:448])
                    return ev
                pw = [(sem["ve"], vxw)]
                if guard_ststg:
                    for gg in guard_ststg:
                        W("vector", *gg)
                    guard_ststg = []
                evs = small_pe(mk_pe(), mk_ev(), pre_waits=pw)
                W("vector", sem["ve"], evs)
                DR("vector")
                def f(e, c=c, hh=hh, h0=h0):
                    h = h0 + hh
                    e.scalar_tensor_tensor(
                        out=acc[:, h * 64:h * 64 + 64],
                        in0=acc[:, h * 64:h * 64 + 64],
                        scalar=cdec[:, c * 128 + h:c * 128 + h + 1],
                        in1=ststg[:, c * 128 + hh * 64:
                                  c * 128 + hh * 64 + 64],
                        op0=ALU.mult, op1=ALU.add).then_inc(sem["ve"], 1)
                E("vector", f)
                fold_v = inc(sem["ve"])
            guard_xwb.append((sem["te"], cnt[sem["te"]]))
        for c in range(NCH if not no_states else 0):
            v = dma("sync",
                    st_sp[c * 128:(c + 1) * 128, 128 * nt:128 * nt + 128],
                    ststg[:, c * 128:(c + 1) * 128], s="sp",
                    waits=[(sem["ve"], fold_v)])
            guard_ststg.append((sem["sp"], v))

    pending_post = None
    for it, nt in enumerate(nt_order):
        par = it % 2
        col0 = INTER + 128 * nt
        wws = []
        if it >= 2:
            wws.append((sem["te"], te_after_nt[nt_order[it - 2]]))
        wd = dma("gpsimd", wbuf[par][:], winfm[nt], s="w", waits=wws)
        W("tensor", sem["w"], wd)
        if it == 0:
            W("tensor", sem["ve"], v_xfm)
        if par in fm_evac_done:
            W("tensor", *fm_evac_done[par])
        pbase = 0 if par == 0 else 3
        def f(e, par=par, pbase=pbase):
            for kt in range(KT):
                lhsT = wbuf[par][:, kt * 128:(kt + 1) * 128]
                last = None
                for si, (off, n) in enumerate(FM_SPLITS):
                    last = e.matmul(pband[pbase + si][:, 0:n], lhsT,
                                    xfm[:, kt * LIN + off: kt * LIN + off + n],
                                    start=(kt == 0), stop=(kt == KT - 1),
                                    skip_group_check=True)
            last.then_inc(sem["te"], 1)
        E("tensor", f)
        te_after_nt[nt] = inc(sem["te"])

        W("scalar", sem["te"], te_after_nt[nt])
        if nt == 80:
            def f(e, pbase=pbase):
                e.copy(dtraw[:, 0:339], pband[pbase + 0][:, 3:342])
                e.copy(dtraw[:, 339:681], pband[pbase + 1][:, 0:342])
                e.copy(dtraw[:, 681:1024], pband[pbase + 2][:, 0:343])\
                    .then_inc(sem["se"], 1)
        else:
            if par in guard_cbuf:
                W("scalar", *guard_cbuf[par])
            def f(e, pbase=pbase, par=par):
                e.copy(cbufs[par][:, 0:342], pband[pbase + 0][:, 0:342])
                e.copy(cbufs[par][:, 342:684], pband[pbase + 1][:, 0:342])
                e.copy(cbufs[par][:, 684:1027], pband[pbase + 2][:, 0:343])\
                    .then_inc(sem["se"], 1)
        E("scalar", f)
        ed = inc(sem["se"])
        fm_evac_done[par] = (sem["se"], ed)

        if nt == 80:
            for c in range(NCH):
                t0, t1, t2, t3 = dtw[4 * (c % 2):4 * (c % 2) + 4]
                sc = 16 * (c % 2)
                W("scalar", sem["se"], ed)
                W("scalar", sem["ve"], v_A)
                if c >= 2:
                    W("scalar", sem["te"], dt_tr[c - 2])
                def f(e, c=c, t0=t0, t1=t1, t2=t2, t3=t3):
                    dcol = dtraw[:, c * 256:(c + 1) * 256]
                    e.activation(out=t0[:], in_=dcol, func=AF.Abs,
                                 bias=cols[:, 0:1])
                    e.drain()
                    e.activation(out=t1[:], in_=t0[:], func=AF.Exp, scale=-1.0)
                    e.drain()
                    e.activation(out=t2[:], in_=t1[:], func=AF.Ln, bias=1.0)
                    e.activation(out=t3[:], in_=dcol, func=AF.Relu,
                                 bias=cols[:, 0:1]).then_inc(sem["se"], 1)
                E("scalar", f)
                sd = inc(sem["se"])
                W("vector", sem["se"], sd)
                def f(e, t1=t1, t2=t2, t3=t3, sc=sc):
                    e.tensor_add(t3[:], t3[:], t2[:])
                    e.drain()
                    e.tensor_scalar(t1[:], t3[:], cols[:, 1:2], None,
                                    ALU.mult)
                    e.drain()
                    e.tensor_tensor_scan(out=t2[:], data0=ones256[:],
                                         data1=t1[:], initial=0.0,
                                         op0=ALU.mult, op1=ALU.add)
                    e.drain()
                    e.tensor_scalar_mul(stats[:, sc + 12:sc + 13],
                                        t2[:, 255:256], -1.0)\
                        .then_inc(sem["ve"], 1)
                E("vector", f)
                vd = inc(sem["ve"])
                dma("sync", cum_dram[c], t2[:], s="sp", waits=[(sem["ve"], vd)])
                W("scalar", sem["ve"], vd)
                def f(e, t0=t0, t1=t1, t2=t2):
                    e.activation(out=t1[:], in_=t2[:], func=AF.Exp, scale=-1.0,
                                 bias=t2[:, 255:256])
                    e.activation(out=t0[:], in_=t2[:], func=AF.Exp, scale=1.0)\
                        .then_inc(sem["se"], 1)
                E("scalar", f)
                sd2 = inc(sem["se"])
                W("vector", sem["se"], sd2)
                def f(e, t1=t1, t3=t3, c=c):
                    e.tensor_mul(t1[:], t3[:], t1[:]).then_inc(sem["ve"], 1)
                E("vector", f)
                DR("vector")
                vd2 = inc(sem["ve"])
                for (tsrc, dstap) in ((t1, dtdrm), (t0, ecirm), (t3, dtrm),
                                      (t2, cumrm)):
                    for jh in range(2):
                        def mk_pe(tsrc=tsrc, jh=jh):
                            def pe(e, slot):
                                return e.transpose(
                                    pband[6 + slot][:, 128:256],
                                    in_=tsrc[:, jh * 128:(jh + 1) * 128],
                                    identity=ident_f[:])
                            return pe
                        def mk_ev(dstap=dstap, c=c, jh=jh):
                            def ev(e, slot):
                                col = (2 * c + jh) * 128
                                return e.tensor_copy(
                                    dstap[:, col:col + 128],
                                    pband[6 + slot][:, 128:256])
                            return ev
                        small_pe(mk_pe(), mk_ev(),
                                 pre_waits=[(sem["ve"], vd2)])
                def pe2(e, slot, t2=t2):
                    return e.transpose(pband[6 + slot][0:1, 128:256],
                                       in_=t2[:, 255:256], identity=ident_f[:])
                def ev2(e, slot):
                    return e.tensor_copy(clrow[:], pband[6 + slot][0:1, 128:256])
                evn = small_pe(pe2, ev2, pre_waits=[(sem["ve"], vd)])
                def pe3(e, slot):
                    return e.matmul(pband[6 + slot][:, 256:384], onesrow[:],
                                    clrow[:], start=True, stop=True)
                def ev3(e, slot, c=c):
                    return e.activation(out=cdec[:, c * 128:(c + 1) * 128],
                                        in_=pband[6 + slot][:, 256:384],
                                        func=AF.Exp)
                small_pe(pe3, ev3, evac_eng="scalar",
                         pre_waits=[(sem["ve"], evn)])
                dt_tr = dt_tr if c > 0 else {}
                dt_tr[c] = cnt[sem["te"]]
        else:
            W("vector", sem["se"], ed)
            def f(e, t=nt, par=par):
                e.tensor_scalar(cvaccs[par][:], cbufs[par][:, 0:1024],
                                convw_sb[:, 4 * t:4 * t + 1],
                                convb_sb[:, t:t + 1], ALU.mult, ALU.add)
                for k in range(1, 4):
                    e.scalar_tensor_tensor(
                        out=cvaccs[par][:], in0=cbufs[par][:, k:k + 1024],
                        scalar=convw_sb[:, 4 * t + k:4 * t + k + 1],
                        in1=cvaccs[par][:], op0=ALU.mult, op1=ALU.add)
                e.tensor_copy(cvaccs[par][:, 0:1], cvaccs[par][:, 0:1])\
                    .then_inc(sem["ve"], 1)
            E("vector", f)
            cv = inc(sem["ve"])
            guard_cbuf[par] = (sem["ve"], cv)
            W("scalar", sem["ve"], cv)
            for gg in guard_sil[par]:
                W("scalar", *gg)
            guard_sil[par] = []
            E("scalar", lambda e, par=par: e.activation(
                out=silouts[par][:], in_=cvaccs[par][:], func=AF.Silu)
              .then_inc(sem["se"], 1))
            sd = inc(sem["se"])
            if pending_post is not None:
                emit_post(*pending_post)
            pending_post = (nt, par, sd)
    if pending_post is not None:
        emit_post(*pending_post)
        pending_post = None

    cc_done = 0
    d2 = 0
    if stage_ge("a3"):
        W("vector", sem["ve"], cnt[sem["ve"]])
        def f(e):
            last = None
            for c in range(NCH):
                e.drain()
                last = e.tensor_mul(pbc[:], pbc[:],
                                    cdec[:, c * 128:(c + 1) * 128])
            last.then_inc(sem["ve"], 1)
        E("vector", f)
        v_p = inc(sem["ve"])
        d1 = dma("sync", ag_in[:, 0:INTER], acc[:], s="sp",
                 waits=[(sem["ve"], cnt[sem["ve"]])])
        W("vector", sem["ve"], v_p)
        E("vector", lambda e: e.tensor_copy(pstg[:], pbc[:])
          .then_inc(sem["ve"], 1))
        v_pbf = inc(sem["ve"])
        d2 = dma("sync", ag_in[:, INTER:INTER + 128], pstg[:], s="sp",
                 waits=[(sem["ve"], v_pbf)])
        if "dbg_acc" in dbgs:
            dma("gpsimd", dbgs["dbg_acc"][:, 0:INTER], acc[:], s="dg",
                waits=[(sem["ve"], cnt[sem["ve"]])])
        def emit_collective():
            W("gpsimd", sem["sp"], cnt[sem["sp"]])
            W("gpsimd", sem["w"], cnt[sem["w"]])
            W("gpsimd", sem["dg"], cnt[sem["dg"]])
            W("gpsimd", sem["ld"], cnt[sem["ld"]])
            def f(e):
                e.collective_compute(
                    "AllGather", ALU.bypass,
                    replica_groups=[[0, 1, 2, 3], [4, 5, 6, 7]],
                    ins=[ag_in], outs=[ag_out]).then_inc(sem["cc"], 1)
            E("gpsimd", f)
            return inc(sem["cc"])
        if not ag_late:
            cc_done = emit_collective()

    W("tensor", sem["se"], cnt[sem["se"]])
    W("tensor", sem["ve"], cnt[sem["ve"]])
    a4_ve = cnt[sem["ve"]]
    a4_te = cnt[sem["te"]]
    a4_first_scalar = [True]
    gate_te = {}
    for gnb in range(16 if stage_ge("a4") else 0):
        for kq in range(4):
            gi = gnb * 4 + kq
            par = gi % 2
            wws = []
            if gi >= 2:
                wws.append((sem["te"], gate_te[gi - 2]))
            else:
                wws.append((sem["te"], a4_te))
            wd = dma("gpsimd", wgq[par][:], wingt[gnb * 4 + kq], s="w",
                     waits=wws)
            W("tensor", sem["w"], wd)
            if kq == 0 and gnb > 0:
                W("tensor", sem["sp"], gate_spill_done)
            def f(e, par=par, kq=kq):
                last = None
                for kt8 in range(8):
                    kt = kq * 8 + kt8
                    for mt in range(8):
                        last = e.matmul(
                            pband[mt][:, 0:512],
                            xfm[:, kt * LIN + 3 + 128 * mt:
                                kt * LIN + 3 + 128 * mt + 128],
                            wgq[par][:, kt8 * 512:kt8 * 512 + 512],
                            start=(kt == 0), stop=(kt == KT - 1),
                            skip_group_check=True)
                last.then_inc(sem["te"], 1)
            E("tensor", f)
            gate_te[gi] = inc(sem["te"])
        ge = gate_te[gnb * 4 + 3]
        for mt in range(8):
            gpar = mt % 2
            W("scalar", sem["te"], ge)
            if a4_first_scalar[0]:
                W("scalar", sem["ve"], a4_ve)
                a4_first_scalar[0] = False
            if gnb > 0 or mt >= 2:
                W("scalar", sem["sp"], gstage_guard[gpar])
            E("scalar", lambda e, mt=mt, gpar=gpar: e.activation(
                out=gstage[gpar], in_=pband[mt][:, 0:512], func=AF.Silu)
              .then_inc(sem["se"], 1))
            sv = inc(sem["se"])
            v = dma("sync", gate_sp[mt, :, 512 * gnb:512 * gnb + 512],
                    gstage[gpar], s="sp", waits=[(sem["se"], sv)])
            if gnb == 0 and mt == 0:
                gstage_guard = {}
            gstage_guard[gpar] = v
        gate_spill_done = cnt[sem["sp"]]

    if stage_ge("a3") and ag_late:
        cc_done = emit_collective()
    srun = acc
    if stage_ge("a5"):
        W("vector", sem["sp"], d2)
        W("vector", sem["dg"], cnt[sem["dg"]])
        E("vector", lambda e: e.memset(srun[:], 0.0).then_inc(sem["ve"], 1))
    v_sz = inc(sem["ve"]) if stage_ge("a5") else 0
    for d in range(3 if stage_ge("a5") else 0):
        da = dma("sync", agbuf[:], ag_out[d * 128:(d + 1) * 128, 0:INTER],
                 s="ld", waits=[(sem["cc"], cc_done)] if d == 0 else
                 [(sem["ve"], v_sz)])
        dp = dma("sync", agp[:], ag_out[d * 128:(d + 1) * 128, INTER:INTER + 128],
                 s="ld")
        W("vector", sem["ld"], dp)
        W("vector", sem["ve"], v_sz)
        def f(e, d=d):
            e.tensor_scalar(pbc[:], agp[:], fmask[:, d:d + 1],
                            fmask[:, 3 + d:3 + d + 1], ALU.mult, ALU.add)
            e.tensor_scalar(agbuf[:], agbuf[:], fmask[:, d:d + 1], None,
                            ALU.mult)
            e.drain()
            last = None
            for h in range(H):
                last = e.scalar_tensor_tensor(
                    out=srun[:, h * 64:h * 64 + 64],
                    in0=srun[:, h * 64:h * 64 + 64],
                    scalar=pbc[:, h:h + 1],
                    in1=agbuf[:, h * 64:h * 64 + 64],
                    op0=ALU.mult, op1=ALU.add)
            last.then_inc(sem["ve"], 1)
        E("vector", f)
        v_sz = inc(sem["ve"])


    if stage_ge("b"):
        hsc = [xfm[:, 0:8192], xfm[:, 8192:16384]]
        yfull = [xfm[:, 16384:24576], xfm[:, 24576:32768]]
        stc = brm[:, 0:8192]
        gate_mh1 = brm[:, 0:8192]
        gate_mh0 = agbuf[:, 0:8192]
        bcc = wbuf[0][:, 0:4096]
        smb = wbuf[1][:, 0:3072]
        cumb = [dtraw[:, i * 256:(i + 1) * 256] for i in range(4)]
        argb = [cbuf[:, 0:384], cbuf[:, 384:768],
                cvacc[:, 0:384], cvacc[:, 384:768]]
        Lb = [silout[:, 0:384], silout[:, 384:768],
              hstr[:, 0:384], hstr[:, 384:768]]
        swb = [dtw[1][:].bitcast(BF16)[:, 0:384],
               dtw[3][:].bitcast(BF16)[:, 0:384]]
        diagb = [ststg[:, i * 128:(i + 1) * 128] for i in range(4)]
        ibb = dtdrm[:, :].bitcast(BF16)
        ib = [[ibb[:, 0:512], ibb[:, 512:1024]],
              [ibb[:, 1024:1536], ibb[:, 1536:2048]]]
        gstag = [dtw[0][:].bitcast(BF16), dtw[2][:].bitcast(BF16)]
        ecix = dtw[5][:].bitcast(BF16)

        def tcol(ap, col):
            return ap[:, col:col + 1]

        def tsl(h):
            return xln[h // 64][:, (h % 64) * 64:(h % 64) * 64 + 64]

        vnow = lambda: (sem["ve"], cnt[sem["ve"]])
        gB = {
            "hs_free": None, "yfull_free": None, "stc_free": None,
            "bcc_free": None, "smb_free": None,
            "tst_free": (sem["te"], cnt[sem["te"]]),
            "gate0_free": None, "gate1_free": None,
            "banks01_free": None, "banks67_free": vnow(),
            "gstag_free": [None, None],
        }
        first_b_dma = [True]

        def bdma(out_ap, in_ap, waits=()):
            w = [x for x in waits if x is not None]
            if first_b_dma[0]:
                w.append((sem["cc"], cc_done))
                first_b_dma[0] = False
            return dma("sync", out_ap, in_ap, s="ld", waits=w)

        def WO(eng, g):
            if g is not None:
                W(eng, *g)

        sw_te = {}
        cumb_done = {}
        arg_done = {}
        L_done = {}
        yevac = {}
        yass = {}
        for c in range(NCH):
            d_hs0 = bdma(hsc[0], hs_sp[2 * c], waits=[gB["hs_free"]])
            d_hs1 = bdma(hsc[1], hs_sp[2 * c + 1])
            d_st = bdma(stc, st_sp[c * 128:(c + 1) * 128, :],
                        waits=[gB["stc_free"]])
            d_bc = bdma(bcc, bc_sp[c], waits=[gB["bcc_free"]])
            d_g0 = bdma(gate_mh0, gate_sp[2 * c], waits=[gB["gate0_free"]])

            sevac = {}
            for g in range(G):
                if g == 0:
                    WO("tensor", gB["banks01_free"])
                    W("tensor", sem["ld"], d_bc)
                else:
                    W("tensor", sem["se"], sevac[g - 1])
                def f(e, g=g):
                    bcol = g * 256
                    e.matmul(pband[0][:, 0:256], bcc[:, bcol:bcol + 128],
                             bcc[:, 2048 + bcol:2048 + bcol + 256],
                             start=True, stop=True, skip_group_check=True)
                    e.matmul(pband[1][:, 0:128], bcc[:, bcol + 128:bcol + 256],
                             bcc[:, 2048 + bcol + 128:2048 + bcol + 256],
                             start=True, stop=True, skip_group_check=True)\
                        .then_inc(sem["te"], 1)
                E("tensor", f)
                t_g = inc(sem["te"])
                W("scalar", sem["te"], t_g)
                if g == 0:
                    WO("scalar", gB["smb_free"])
                def f(e, g=g):
                    e.copy(smb[:, g * 384:g * 384 + 256], pband[0][:, 0:256])
                    e.copy(smb[:, g * 384 + 256:g * 384 + 384],
                           pband[1][:, 0:128]).then_inc(sem["se"], 1)
                E("scalar", f)
                sevac[g] = inc(sem["se"])
                W("vector", sem["se"], sevac[g])
                def f(e, g=g):
                    e.tensor_tensor(smb[:, g * 384:g * 384 + 128],
                                    smb[:, g * 384:g * 384 + 128],
                                    maskd[:], ALU.mult)
                    e.tensor_tensor(smb[:, g * 384 + 256:g * 384 + 384],
                                    smb[:, g * 384 + 256:g * 384 + 384],
                                    maskd[:], ALU.mult).then_inc(sem["ve"], 1)
                E("vector", f)
                inc(sem["ve"])
            gp_masks = cnt[sem["ve"]]
            ldtc = dtw[4]
            W("scalar", sem["ve"], gp_masks)
            E("scalar", lambda e, c=c: e.activation(
                out=ldtc[:], in_=dtrm[:, 2 * c * 128:(2 * c + 2) * 128],
                func=AF.Ln).then_inc(sem["se"], 1))
            s_ldt = inc(sem["se"])

            WO("vector", gB["tst_free"])
            def f(e, c=c):
                last = None
                for h in range(H):
                    last = e.tensor_scalar(tsl(h), srun[:, h * 64:h * 64 + 64],
                                           cdec[:, c * 128 + h:c * 128 + h + 1],
                                           None, ALU.mult)
                last.then_inc(sem["ve"], 1)
            E("vector", f)
            v_t = inc(sem["ve"])

            for h in range(0, H, 2):
                idx = c * H + h
                par = h % 4
                g = h >> 4
                hg = h >> 3
                hgidx = c * 16 + hg
                ypar = hg % 2
                ws = []
                if idx >= 3:
                    ws.append((sem["ve"], arg_done[idx - 3]))
                cum_src = cum_dram[c, h:h + 2, :].unsqueeze(0)\
                    .broadcast_to((128, 2, 256))
                cd = dma("sync",
                         dtraw[:, par * 256:par * 256 + 512].rearrange(
                             "p (two i) -> p two i", two=2),
                         cum_src, s="ld", waits=ws)
                cumb_done[h] = cd
                W("vector", sem["ld"], cumb_done[h])
                if idx == 0:
                    W("vector", sem["ve"], gp_masks)
                if idx >= 4:
                    W("vector", sem["se"], L_done[idx - 4])
                def f(e, c=c, h=h, par=par):
                    for dh in range(2):
                        hh, pp = h + dh, par + dh
                        e.tensor_scalar(argb[pp][:, 0:256], cumb[pp][:, :],
                                        tcol(cumrm, (2 * c) * 128 + hh), 0.0,
                                        ALU.subtract, ALU.min)
                        last = e.tensor_scalar(argb[pp][:, 256:384],
                                               cumb[pp][:, 128:256],
                                               tcol(cumrm,
                                                    (2 * c + 1) * 128 + hh),
                                               0.0, ALU.subtract, ALU.min)
                    last.then_inc(sem["ve"], 1)
                E("vector", f)
                arg_done[idx] = arg_done[idx + 1] = inc(sem["ve"])
                W("scalar", sem["ve"], arg_done[idx])
                if idx >= 4:
                    W("scalar", sem["te"], sw_te[idx - 4])
                def f(e, c=c, h=h, par=par):
                    for dh in range(2):
                        hh, pp = h + dh, par + dh
                        e.activation(out=Lb[pp][:, 0:256],
                                     in_=argb[pp][:, 0:256], func=AF.Exp,
                                     bias=tcol(ldtc, hh))
                        last = e.activation(out=Lb[pp][:, 256:384],
                                            in_=argb[pp][:, 256:384],
                                            func=AF.Exp,
                                            bias=tcol(ldtc, 128 + hh))
                    last.then_inc(sem["se"], 1)
                E("scalar", f)
                L_done[idx] = L_done[idx + 1] = inc(sem["se"])
                W("vector", sem["se"], L_done[idx])
                if idx >= 2:
                    W("vector", sem["te"], sw_te[idx - 2])
                def f(e, h=h, par=par, g=g):
                    e.tensor_tensor(swb[0][:, 0:384], Lb[par][:, 0:384],
                                    smb[:, g * 384:g * 384 + 384], ALU.mult)
                    e.tensor_tensor(swb[1][:, 0:384], Lb[par + 1][:, 0:384],
                                    smb[:, g * 384:g * 384 + 384], ALU.mult)
                    e.tensor_scalar(diagb[par], ident_bf[:], tcol(dbc, h),
                                    None, ALU.mult)
                    e.tensor_scalar(diagb[par + 1], ident_bf[:],
                                    tcol(dbc, h + 1), None, ALU.mult)\
                        .then_inc(sem["ve"], 1)
                E("vector", f)
                sw_v = inc(sem["ve"])
                ybank = [pband[2 + 2 * ypar], pband[3 + 2 * ypar]]
                ibank = [pband[6], pband[7]] if ypar == 0 else \
                        [pband[0], pband[1]]
                W("tensor", sem["ve"], sw_v)
                if h == 0:
                    W("tensor", sem["ve"], v_t)
                    W("tensor", sem["ld"], d_hs1)
                if h % 8 == 0:
                    if hgidx >= 2:
                        W("tensor", sem["se"], yevac[hgidx - 2])
                    if ypar == 0:
                        WO("tensor", gB["banks67_free"])
                def f(e, c=c, h=h, par=par, g=g, ybank=ybank, ibank=ibank):
                    for dh in range(2):
                        hh, pp = h + dh, par + dh
                        sl = slice(64 * (hh % 8), 64 * (hh % 8) + 64)
                        e.matmul(ybank[0][:, sl], swb[dh][:, 0:128],
                                 hsc[0][:, hh * 64:hh * 64 + 64],
                                 start=True, stop=False, skip_group_check=True)
                        e.matmul(ybank[0][:, sl], diagb[pp],
                                 hsc[0][:, hh * 64:hh * 64 + 64],
                                 start=False, stop=True, skip_group_check=True)
                        e.matmul(ybank[1][:, sl], swb[dh][:, 128:256],
                                 hsc[0][:, hh * 64:hh * 64 + 64],
                                 start=True, stop=False, skip_group_check=True)
                        e.matmul(ybank[1][:, sl], swb[dh][:, 256:384],
                                 hsc[1][:, hh * 64:hh * 64 + 64],
                                 start=False, stop=False,
                                 skip_group_check=True)
                        e.matmul(ybank[1][:, sl], diagb[pp],
                                 hsc[1][:, hh * 64:hh * 64 + 64],
                                 start=False, stop=True, skip_group_check=True)
                        e.matmul(ibank[0][:, sl],
                                 bcc[:, 2048 + g * 256:2048 + g * 256 + 128],
                                 tsl(hh), start=True, stop=True,
                                 skip_group_check=True)
                        last = e.matmul(
                            ibank[1][:, sl],
                            bcc[:, 2048 + g * 256 + 128:2048 + g * 256 + 256],
                            tsl(hh), start=True, stop=True,
                            skip_group_check=True)
                    last.then_inc(sem["te"], 1)
                E("tensor", f)
                sw_te[idx] = sw_te[idx + 1] = inc(sem["te"])
                if h % 8 == 6:
                    W("scalar", sem["te"], sw_te[idx])
                    if hg == 0:
                        WO("scalar", gB["yfull_free"])
                    if hgidx >= 2:
                        W("scalar", sem["ve"], yass[hgidx - 2])
                    def f(e, hg=hg, ybank=ybank, ibank=ibank, ypar=ypar):
                        e.copy(yfull[0][:, hg * 512:hg * 512 + 512],
                               ybank[0][:, :])
                        e.copy(yfull[1][:, hg * 512:hg * 512 + 512],
                               ybank[1][:, :])
                        e.copy(ib[0][ypar], ibank[0][:, :])
                        e.copy(ib[1][ypar], ibank[1][:, :])\
                            .then_inc(sem["se"], 1)
                    E("scalar", f)
                    yevac[hgidx] = inc(sem["se"])
                    W("vector", sem["se"], yevac[hgidx])
                    def f(e, c=c, hg=hg, ypar=ypar):
                        sl2 = slice(hg * 512, hg * 512 + 512)
                        for ih in range(2):
                            eci_src = ecirm[:, (2 * c + ih) * 128 + hg * 8:
                                            (2 * c + ih) * 128 + hg * 8 + 8]
                            e.tensor_copy(
                                ecix[:].rearrange("p (h q) -> p h q", h=8),
                                eci_src.unsqueeze(2).broadcast_to((128, 8, 64)))
                            e.drain()
                            e.tensor_mul(ib[ih][ypar], ib[ih][ypar], ecix[:])
                            e.drain()
                            e.tensor_add(yfull[ih][:, sl2], yfull[ih][:, sl2],
                                         ib[ih][ypar])
                        e.drain()
                        e.tensor_copy(ecix[:, 0:1], ecix[:, 0:1])\
                            .then_inc(sem["ve"], 1)
                    E("vector", f)
                    yass[hgidx] = inc(sem["ve"])

            W("vector", sem["te"], cnt[sem["te"]])
            def f(e):
                e.tensor_add(srun[:, 0:4096], xln[0][:], stc[:, 0:4096])
                e.tensor_add(srun[:, 4096:8192], xln[1][:],
                             stc[:, 4096:8192]).then_inc(sem["ve"], 1)
            E("vector", f)
            v_srun = inc(sem["ve"])
            d_g1 = dma("sync", gate_mh1, gate_sp[2 * c + 1], s="ld",
                       waits=[x for x in [(sem["ve"], v_srun),
                                          gB["gate1_free"]] if x])

            sgs = stats[:, 80:88]
            rst8 = stats[:, 88:96]
            gate_dma = {0: d_g0, 1: d_g1}
            gatebuf = {0: gate_mh0, 1: gate_mh1}
            gsp_guard = list(gB["gstag_free"])
            for mh in range(2):
                W("vector", sem["ve"], yass[c * 16 + 15])
                W("vector", sem["ld"], gate_dma[mh])
                def f(e, mh=mh):
                    e.tensor_mul(yfull[mh][:, :], yfull[mh][:, :],
                                 gatebuf[mh][:, :]).then_inc(sem["ve"], 1)
                E("vector", f)
                v_g1 = inc(sem["ve"])
                W("scalar", sem["ve"], v_g1)
                def f(e, mh=mh):
                    last = None
                    for g in range(G):
                        last = e.activation(
                            out=sqscrap[:, 0:1024],
                            in_=yfull[mh][:, g * 1024:(g + 1) * 1024],
                            func=AF.Square, accum_out=sgs[:, g:g + 1])
                        e.drain()
                    last.then_inc(sem["se"], 1)
                E("scalar", f)
                s_sq = inc(sem["se"])
                W("vector", sem["se"], s_sq)
                E("vector", lambda e: e.tensor_scalar(
                    sgs[:], sgs[:], 1.0 / 1024.0, EPS, ALU.mult, ALU.add)
                  .then_inc(sem["ve"], 1))
                v_v8 = inc(sem["ve"])
                W("scalar", sem["ve"], v_v8)
                def f(e):
                    e.activation(out=rst8[:], in_=sgs[:], func=AF.Ln)
                    e.drain()
                    e.activation(out=rst8[:], in_=rst8[:], func=AF.Exp,
                                 scale=-0.5).then_inc(sem["se"], 1)
                E("scalar", f)
                s_r8 = inc(sem["se"])
                W("vector", sem["se"], s_r8)
                def f(e, mh=mh):
                    last = None
                    for g in range(G):
                        last = e.tensor_scalar(
                            yfull[mh][:, g * 1024:(g + 1) * 1024],
                            yfull[mh][:, g * 1024:(g + 1) * 1024],
                            rst8[:, g:g + 1], None, ALU.mult)
                    last.then_inc(sem["ve"], 1)
                E("vector", f)
                v_gn = inc(sem["ve"])
                mb = 2 * c + mh
                for kb in range(16):
                    spar = kb % 2
                    if gsp_guard[spar] is not None:
                        WO("vector", gsp_guard[spar])
                    for k4 in range(4):
                        kt = kb * 4 + k4
                        def mk_pe(kt=kt, mh=mh):
                            def pe(e, slot):
                                return e.transpose(
                                    pband[6 + slot][:, :]
                                    .bitcast(BF16)[:, 0:128],
                                    in_=yfull[mh][:, kt * 128:kt * 128 + 128],
                                    identity=ident_bf[:])
                            return pe
                        def mk_ev(kt=kt, k4=k4, spar=spar):
                            def ev(e, slot):
                                return e.tensor_scalar(
                                    gstag[spar][:, k4 * 128:k4 * 128 + 128],
                                    pband[6 + slot][:, :]
                                    .bitcast(BF16)[:, 0:128],
                                    nwcols[:, kt:kt + 1], None, ALU.mult)
                            return ev
                        pw = [(sem["ve"], v_gn)] if (kt == 0 and mh == 0) \
                            else ()
                        if kt == 0:
                            pw = [(sem["ve"], v_gn)]
                        small_pe(mk_pe(), mk_ev(), pre_waits=pw)
                    v_last = cnt[sem["ve"]]
                    sp_v = dma("sync", g_sp[mb, :, kb * 4:kb * 4 + 4, :],
                               gstag[spar][:, :].rearrange(
                                   "p (k m) -> p k m", k=4),
                               s="sp", waits=[(sem["ve"], v_last)])
                    gsp_guard[spar] = (sem["sp"], sp_v)

            gB["hs_free"] = (sem["te"], cnt[sem["te"]])
            gB["yfull_free"] = (sem["te"], cnt[sem["te"]])
            gB["stc_free"] = (sem["ve"], cnt[sem["ve"]])
            gB["bcc_free"] = (sem["te"], cnt[sem["te"]])
            gB["smb_free"] = (sem["ve"], cnt[sem["ve"]])
            gB["tst_free"] = (sem["ve"], v_srun)
            gB["gate0_free"] = (sem["ve"], cnt[sem["ve"]])
            gB["gate1_free"] = (sem["ve"], cnt[sem["ve"]])
            gB["banks01_free"] = (sem["se"], cnt[sem["se"]])
            gB["banks67_free"] = (sem["ve"], cnt[sem["ve"]])
            gB["gstag_free"] = gsp_guard
        b_end_guards = dict(gB)
        ring_guard[0] = (sem["ve"], cnt[sem["ve"]])
        ring_guard[1] = (sem["ve"], cnt[sem["ve"]])

    if stage_ge("c"):
        gq = [xfm[:, 0:8192], xfm[:, 8192:16384]]
        wq = [xfm[:, 16384:20480], xfm[:, 20480:24576]]
        ffn_bf = xfm[:, 0:32768]
        aslot = [agbuf[:, 0:4096], agbuf[:, 4096:8192],
                 brm[:, 0:4096], brm[:, 4096:8192],
                 wbuf[0][:, 0:4096], wbuf[1][:, 0:4096],
                 xln[0][:, 0:4096], xln[1][:, 0:4096]]
        biasrow = acc[0:1, 0:4096]
        cscrap = acc[:, 4096:8192]
        pk2 = dtraw[:, :].bitcast(I8)
        pkb = [pk2[:, 0:512], pk2[:, 512:1024]]
        lo4b = pk2[:, 1024:1536]
        p_bf = cbuf[:, 0:512]
        lo4_bf = cbuf[:, 512:1024]
        wfs = [[cvacc[:, 0:512], cvacc[:, 512:1024]],
               [silout[:, 0:512], silout[:, 512:1024]]]
        x2ln_tmp = hstr[:, 0:512]

        bstart = b_end_guards if stage_ge("b") else {}

        d_bias = dma("gpsimd", biasrow[:], ffnb_in, s="dg")
        d_fs = dma("gpsimd", stats[0:1, 61:62], fscale_in, s="dg")
        W("tensor", sem["dg"], d_fs)
        WO("tensor", ring_guard[0]) if stage_ge("b") else None
        def f(e):
            e.matmul(pband[6][:, 0:1], onesrow[:], stats[0:1, 61:62],
                     start=True, stop=True, skip_group_check=True)\
                .then_inc(sem["te"], 1)
        E("tensor", f)
        t_fs = inc(sem["te"])
        W("vector", sem["te"], t_fs)
        E("vector", lambda e: e.tensor_copy(stats[:, 62:63], pband[6][:, 0:1])
          .then_inc(sem["ve"], 1))
        v_fs = inc(sem["ve"])
        scol = stats[:, 62:63]

        c1evac = {}
        gq_te = {}
        wq_te = {}
        for nblk in range(8):
            for kq in range(8):
                par = kq % 2
                key = nblk * 8 + kq
                ws = []
                if key >= 2:
                    ws.append((sem["te"], wq_te[key - 2]))
                elif stage_ge("b"):
                    ws.append(bstart["banks67_free"])
                d_w = dma("gpsimd", wq[par][:], woutt[nblk * 8 + kq], s="w",
                          waits=ws)
                gws = []
                if key >= 2:
                    gws.append((sem["te"], gq_te[key - 2]))
                for m in range(8):
                    d_g = dma("sync", gq[par][:, m * 1024:(m + 1) * 1024],
                              g_sp[m, :, kq * 8:kq * 8 + 8, :],
                              s="ld", waits=gws if m == 0 else ())
                W("tensor", sem["w"], d_w)
                W("tensor", sem["ld"], d_g)
                for m in range(8):
                    if kq == 0:
                        if nblk > 0:
                            W("tensor", sem["se"], c1evac[(nblk - 1, m)])
                        elif m == 0 and stage_ge("b"):
                            pass
                    def f(e, m=m, par=par, kq=kq):
                        last = None
                        for kt8 in range(8):
                            last = e.matmul(
                                pband[m][:, 0:512],
                                gq[par][:, m * 1024 + kt8 * 128:
                                        m * 1024 + kt8 * 128 + 128],
                                wq[par][:, kt8 * 512:kt8 * 512 + 512],
                                start=(kq == 0 and kt8 == 0),
                                stop=(kq == 7 and kt8 == 7),
                                skip_group_check=True)
                        last.then_inc(sem["te"], 1)
                    E("tensor", f)
                    t_m = inc(sem["te"])
                    if kq == 7:
                        W("scalar", sem["te"], t_m)
                        if nblk == 0 and stage_ge("b"):
                            WO("scalar", bstart["yfull_free"]) if m == 0 \
                                else None
                        def f(e, m=m, nblk=nblk):
                            e.copy(aslot[m][:, nblk * 512:nblk * 512 + 512],
                                   pband[m][:, 0:512]).then_inc(sem["se"], 1)
                        E("scalar", f)
                        c1evac[(nblk, m)] = inc(sem["se"])
                wq_te[key] = cnt[sem["te"]]
                gq_te[key] = cnt[sem["te"]]

        def emit_ln(src_ap, sc, prod_sem, prod_val):
            W("vector", prod_sem, prod_val)
            E("vector", lambda e, s=src_ap, sc=sc: e.reduce_sum(
                stats[:, sc:sc + 1], s, axis=AXX).then_inc(sem["ve"], 1))
            v1 = inc(sem["ve"])
            DR("vector")
            W("scalar", prod_sem, prod_val)
            E("scalar", lambda e, s=src_ap, sc=sc: e.activation(
                out=cscrap[:], in_=s, func=AF.Square,
                accum_out=stats[:, sc + 1:sc + 2]).then_inc(sem["se"], 1))
            s1 = inc(sem["se"])
            W("vector", sem["se"], s1)
            def f(e, sc=sc):
                e.tensor_scalar_mul(stats[:, sc + 2:sc + 3],
                                    stats[:, sc:sc + 1], 1.0 / HID)
                e.tensor_scalar_mul(stats[:, sc + 3:sc + 4],
                                    stats[:, sc + 1:sc + 2], 1.0 / HID)
                e.drain()
                e.tensor_mul(stats[:, sc + 4:sc + 5], stats[:, sc + 2:sc + 3],
                             stats[:, sc + 2:sc + 3])
                e.drain()
                e.tensor_sub(stats[:, sc + 5:sc + 6], stats[:, sc + 3:sc + 4],
                             stats[:, sc + 4:sc + 5])
                e.drain()
                e.tensor_scalar_add(stats[:, sc + 5:sc + 6],
                                    stats[:, sc + 5:sc + 6], EPS)\
                    .then_inc(sem["ve"], 1)
            E("vector", f)
            v2 = inc(sem["ve"])
            W("scalar", sem["ve"], v2)
            def f(e, sc=sc):
                e.activation(out=stats[:, sc + 6:sc + 7],
                             in_=stats[:, sc + 5:sc + 6], func=AF.Ln)
                e.drain()
                e.activation(out=stats[:, sc + 7:sc + 8],
                             in_=stats[:, sc + 6:sc + 7], func=AF.Exp,
                             scale=-0.5).then_inc(sem["se"], 1)
            E("scalar", f)
            s2 = inc(sem["se"])
            W("vector", sem["se"], s2)
            def f(e, sc=sc):
                e.tensor_mul(stats[:, sc + 6:sc + 7], stats[:, sc + 2:sc + 3],
                             stats[:, sc + 7:sc + 8])
                e.drain()
                e.tensor_scalar_mul(stats[:, sc + 6:sc + 7],
                                    stats[:, sc + 6:sc + 7], -1.0)\
                    .then_inc(sem["ve"], 1)
            E("vector", f)
            return inc(sem["ve"])

        c2_done = {}
        xrow_free = None
        for m in range(8):
            ln1 = emit_ln(aslot[m][:, :], 0, sem["se"], c1evac[(7, m)])
            dx = dma("sync", xrow[0][:], xs[3 + m * 128:3 + m * 128 + 128, :],
                     s="ld", waits=[x for x in [xrow_free] if x])
            W("vector", sem["ve"], ln1)
            W("vector", sem["ld"], dx)
            def f(e, m=m):
                e.scalar_tensor_tensor(
                    out=xrow[0][:], in0=aslot[m][:, :],
                    scalar=stats[:, 7:8], in1=xrow[0][:],
                    op0=ALU.mult, op1=ALU.add)
                e.drain()
                e.tensor_scalar(xrow[0][:], xrow[0][:], stats[:, 6:7], None,
                                ALU.add).then_inc(sem["ve"], 1)
            E("vector", f)
            v_x2 = inc(sem["ve"])
            d_sp = dma("sync", x2_sp[m], xrow[0][:], s="sp",
                       waits=[(sem["ve"], v_x2)])
            ln2 = emit_ln(xrow[0][:], 8, sem["ve"], v_x2)
            W("vector", sem["ve"], ln2)
            def f(e, m=m):
                e.tensor_scalar(aslot[m][:, :], xrow[0][:], stats[:, 15:16],
                                stats[:, 14:15], ALU.mult, ALU.add)\
                    .then_inc(sem["ve"], 1)
            E("vector", f)
            v_x2ln = inc(sem["ve"])
            xrow_free = (sem["ve"], v_x2ln)
            prev_ev = None
            for ks in range(16):
                W("tensor", sem["ve"], v_x2ln if ks == 0 else 0)
                if prev_ev is not None:
                    W("tensor", *prev_ev)
                def f(e, m=m, ks=ks):
                    e.transpose(pband[6][:, :].bitcast(BF16)[:, 0:128],
                                in_=aslot[m][:, ks * 256:ks * 256 + 256:2],
                                identity=ident_bf[:])
                    e.transpose(pband[7][:, :].bitcast(BF16)[:, 0:128],
                                in_=aslot[m][:, ks * 256 + 1:ks * 256 + 256:2],
                                identity=ident_bf[:]).then_inc(sem["te"], 1)
                E("tensor", f)
                t_tr = inc(sem["te"])
                W("vector", sem["te"], t_tr)
                W("vector", sem["ve"], v_fs)
                def f(e, m=m, ks=ks):
                    e.tensor_scalar(
                        aslot[m][:, ks * 256:ks * 256 + 128],
                        pband[6][:, :].bitcast(BF16)[:, 0:128],
                        scol, None, ALU.mult)
                    e.tensor_scalar(
                        aslot[m][:, ks * 256 + 128:ks * 256 + 256],
                        pband[7][:, :].bitcast(BF16)[:, 0:128],
                        scol, None, ALU.mult).then_inc(sem["ve"], 1)
                E("vector", f)
                prev_ev = (sem["ve"], inc(sem["ve"]))
            c2_done[m] = prev_ev[1]

        c3evac = {}
        unp_v = {}
        for nblk in range(8):
            for ks in range(16):
                par = ks % 2
                key = nblk * 16 + ks
                ws = []
                if key >= 2:
                    ws.append((sem["ve"], unp_v[key - 2]))
                d_pk = dma("sync", pkb[par],
                           pkT[ks * 128:ks * 128 + 128,
                               nblk * 512:nblk * 512 + 512],
                           s="ld", waits=ws)
                W("vector", sem["ld"], d_pk)
                if key >= 2:
                    W("vector", sem["te"], unp_te[key - 2])
                if key == 0:
                    for m in range(8):
                        W("vector", sem["ve"], c2_done[m])
                def f(e, par=par):
                    e.tensor_copy(p_bf[:], pkb[par])
                    e.tensor_single_scalar(lo4b[:].bitcast(I32),
                                           pkb[par].bitcast(I32),
                                           0x0F0F0F0F, ALU.bitwise_and)
                    e.drain()
                    e.tensor_copy(lo4_bf[:], lo4b[:])
                    e.drain()
                    e.tensor_sub(wfs[par][0][:], p_bf[:], lo4_bf[:])
                    e.tensor_single_scalar(p_bf[:], lo4_bf[:], 7.5, ALU.is_gt)
                    e.drain()
                    e.tensor_scalar_mul(wfs[par][0][:], wfs[par][0][:],
                                        1.0 / 16.0)
                    e.scalar_tensor_tensor(
                        out=wfs[par][1][:], in0=p_bf[:], scalar=-16.0,
                        in1=lo4_bf[:], op0=ALU.mult, op1=ALU.add)\
                        .then_inc(sem["ve"], 1)
                E("vector", f)
                unp_v[key] = inc(sem["ve"])
                W("tensor", sem["ve"], unp_v[key])
                for m in range(8):
                    if ks == 0:
                        if nblk > 0:
                            W("tensor", sem["se"], c3evac[(nblk - 1, m)])
                    def f(e, m=m, par=par, ks=ks, nblk=nblk):
                        e.matmul(pband[m][:, 0:512],
                                 aslot[m][:, (2 * ks) * 128:(2 * ks) * 128 + 128],
                                 wfs[par][0][:],
                                 start=(ks == 0), stop=False,
                                 skip_group_check=True)
                        i = e.matmul(pband[m][:, 0:512],
                                     aslot[m][:, (2 * ks + 1) * 128:
                                              (2 * ks + 1) * 128 + 128],
                                     wfs[par][1][:],
                                     start=False, stop=(ks == 15),
                                     skip_group_check=True)
                        if ks == 0:
                            i = e.matmul(pband[m][:, 0:512], onesbf[:],
                                         biasrow[0:1, nblk * 512:
                                                 nblk * 512 + 512],
                                         start=False, stop=False,
                                         skip_group_check=True)
                        i.then_inc(sem["te"], 1)
                    E("tensor", f)
                    t_m = inc(sem["te"])
                    if ks == 15:
                        W("scalar", sem["te"], t_m)
                        if nblk == 0 and m == 0:
                            pass
                        def f(e, m=m, nblk=nblk):
                            e.copy(ffn_bf[:, m * 4096 + nblk * 512:
                                          m * 4096 + nblk * 512 + 512],
                                   pband[m][:, 0:512]).then_inc(sem["se"], 1)
                        E("scalar", f)
                        c3evac[(nblk, m)] = inc(sem["se"])
                unp_te = unp_te if key > 0 else {}
                unp_te[key] = cnt[sem["te"]]

        x2back = None
        for m in range(8):
            fslice = ffn_bf[:, m * 4096:(m + 1) * 4096]
            ln3 = emit_ln(fslice, 16, sem["se"], c3evac[(7, m)])
            dx2 = dma("sync", xrow[0][:], x2_sp[m], s="ld",
                      waits=[x for x in [x2back] if x])
            W("vector", sem["ve"], ln3)
            W("vector", sem["ld"], dx2)
            def f(e, m=m):
                e.scalar_tensor_tensor(
                    out=xrow[0][:], in0=ffn_bf[:, m * 4096:(m + 1) * 4096],
                    scalar=stats[:, 23:24], in1=xrow[0][:],
                    op0=ALU.mult, op1=ALU.add)
                e.drain()
                e.tensor_scalar(xrow[0][:], xrow[0][:], stats[:, 22:23], None,
                                ALU.add).then_inc(sem["ve"], 1)
            E("vector", f)
            v_out = inc(sem["ve"])
            d_out = dma("sync", out[m * 128:(m + 1) * 128, :], xrow[0][:],
                        s="out", waits=[(sem["ve"], v_out)])
            x2back = (sem["out"], d_out)

    dump_waits = [(sem["ve"], cnt[sem["ve"]]), (sem["se"], cnt[sem["se"]])]
    for name, ap in dbgs.items():
        key = name[4:]
        src = {"srun": srun, "dtdrm": dtdrm, "ecirm": ecirm,
               "dtrm": dtrm, "cumrm": cumrm,
               "cdec": cdec, "brm": brm, "dtraw": dtraw, "stats": stats,
               "cols": cols, "xfm": xfm, "wbuf0": wbuf[0], "xln0": xln[0]}.get(key)
        if src is None:
            continue
        ncols = min(src.shape[1], INTER)
        dma("gpsimd", ap[:, 0:ncols], src[:, 0:ncols], s="dg",
            waits=dump_waits)
        dump_waits = []

    if not stage_ge("c"):
        W("vector", sem["ve"], cnt[sem["ve"]])
        W("vector", sem["dg"], cnt[sem["dg"]])
        E("vector", lambda e: e.memset(dtraw[:], 0.0).then_inc(sem["ve"], 1))
        vz = inc(sem["ve"])
        for mt in range(8):
            for q in range(4):
                dma("sync", out[mt * 128:(mt + 1) * 128,
                                q * 1024:(q + 1) * 1024],
                    dtraw[:], s="out", waits=[(sem["ve"], vz)] if q == 0 else ())
    W("sync", sem["out"], cnt[sem["out"]])
    W("sync", sem["sp"], cnt[sem["sp"]])
    W("sync", sem["dg"], cnt[sem["dg"]])
    W("sync", sem["cc"], cc_done)

    class _MockInstr:
        def __init__(self, rec):
            self.rec = rec
        def then_inc(self, s, n=1):
            self.rec[s] = self.rec.get(s, 0) + n
            return self

    class _MockEng:
        def __init__(self, rec):
            self._rec = rec
        def __getattr__(self, name):
            rec = self._rec
            def fn(*a, **k):
                return _MockInstr(rec)
            return fn

    mock_cnt = {}
    for engname in ENGS:
        me = _MockEng(mock_cnt)
        for fop in ops[engname]:
            fop(me)
    for s, v in cnt.items():
        mv = mock_cnt.get(s, 0)
        assert mv == v, f"sem accounting mismatch for {s.name}: planned {v} actual {mv}"
    for s, v in mock_cnt.items():
        assert cnt.get(s, 0) == v, f"unplanned incs on {s.name}: {v}"

    with nc.Block() as block:
        @block.sync
        def _(e):
            for f in ops["sync"]:
                f(e)

        @block.scalar
        def _(e):
            for f in ops["scalar"]:
                f(e)

        @block.vector
        def _(e):
            for f in ops["vector"]:
                f(e)

        @block.tensor
        def _(e):
            for f in ops["tensor"]:
                f(e)

        @block.gpsimd
        def _(e):
            for f in ops["gpsimd"]:
                f(e)

    es.close()
    return nc


def _prep(inputs):
    x = np.asarray(inputs["L0_attn_input_states"], dtype=np.float32)
    win = np.asarray(inputs["L0_attn_in_proj_w"], np.float32)
    winfm = np.ascontiguousarray(
        win[INTER:, :].reshape(81, 128, 32, 128).transpose(0, 3, 2, 1)
        .reshape(81, 128, 32 * 128))
    wingt = np.ascontiguousarray(
        win[:INTER, :].reshape(16, 512, 32, 128).transpose(0, 2, 3, 1)
        .reshape(16, 4, 8, 128, 512).transpose(0, 1, 3, 2, 4)
        .reshape(64, 128, 8 * 512))
    wout = np.asarray(inputs["L0_attn_out_w"], np.float32)
    woutt = np.ascontiguousarray(
        wout.reshape(8, 512, 64, 128).transpose(0, 2, 3, 1)
        .reshape(8, 8, 8, 128, 512).transpose(0, 1, 3, 2, 4)
        .reshape(64, 128, 8 * 512))
    pkT = np.ascontiguousarray(
        np.asarray(inputs["L0_ffn_weight_int4"], np.int8).T)
    convw = np.ascontiguousarray(
        np.asarray(inputs["L0_attn_conv_w"], np.float32).reshape(80, 128, 4)
        .transpose(1, 0, 2).reshape(128, 320))
    convb = np.ascontiguousarray(
        np.asarray(inputs["L0_attn_conv_b"], np.float32).reshape(80, 128).T)
    nw = np.ascontiguousarray(
        np.asarray(inputs["L0_attn_norm_w"], np.float32).reshape(64, 128))
    ffnb = np.asarray(inputs["L0_ffn_bias"], np.float16).reshape(1, HID)
    fscale = np.asarray(inputs["L0_ffn_weight_scale"], np.float16).reshape(1, 1)
    identbf = np.eye(128, dtype=ml_dtypes.bfloat16)
    identf = np.eye(128, dtype=np.float32)
    jj, ii = np.meshgrid(np.arange(128), np.arange(128), indexing="ij")
    maskd = (ii >= jj).astype(ml_dtypes.bfloat16)

    per_core = []
    for core in range(8):
        b, cl = core // 4, core % 4
        xb = x[b]
        r0 = cl * 1024
        halo = np.zeros((3, HID), np.float32) if cl == 0 else xb[r0 - 3:r0]
        xsh = np.ascontiguousarray(np.concatenate([halo, xb[r0:r0 + 1024]], 0))
        fm = np.zeros((128, 6), np.float32)
        for d in range(3):
            m = 1.0 if d < cl else 0.0
            fm[:, d] = m
            fm[:, 3 + d] = 1.0 - m
        per_core.append(dict(
            xs=xsh, winfm=winfm, wingt=wingt, woutt=woutt, pkT=pkT,
            convw=convw, convb=convb,
            dtb=np.asarray(inputs["L0_attn_dt_bias"], np.float32),
            alog=np.asarray(inputs["L0_attn_A_log"], np.float32),
            dvec=np.asarray(inputs["L0_attn_D"], np.float32).reshape(1, H),
            nw=nw, ffnb=ffnb, fscale=fscale, identbf=identbf, identf=identf,
            maskd=maskd, fmask=fm))
    return per_core


def kernel(**inputs):
    per_core = _prep(inputs)
    nc = build()
    res = run_bass_kernel_spmd(nc, per_core, core_ids=list(range(8)))
    outs = [res.results[i]["out"] for i in range(8)]
    b0 = np.concatenate(outs[0:4], axis=0)
    b1 = np.concatenate(outs[4:8], axis=0)
    return np.stack([b0, b1], axis=0)
